# version 01
# baseline (speedup 1.0000x reference)
import contextlib
import numpy as np
import ml_dtypes
import concourse.bass as bass
import concourse.mybir as mybir
from concourse.bass_utils import run_bass_kernel_spmd
from concourse.alu_op_type import AluOpType as Op

F32 = mybir.dt.float32
BF16 = mybir.dt.bfloat16
AF = mybir.ActivationFunctionType
AX = mybir.AxisListType

T = 4096
D = 1024
H = 16
HD = 64
FH = 4096
FR = 256
NCORES = 8
TC = 512
NTC = T // TC
ENGINES = ("sync", "vector", "scalar", "tensor", "gpsimd")


class Ctx:
    def __init__(self, nc, es, kinds=None):
        self.nc = nc
        self.es = es
        self.kinds = kinds or {}
        self.prog = {e: [] for e in ENGINES}
        self.hbm = {}
        self.tiles = {}
        self.psums = {}
        self.sems = {}
        self.cnt = {}

    def dram(self, name, shape, dtype, kind="Internal"):
        kind = self.kinds.get(name, kind)
        if name not in self.hbm:
            self.hbm[name] = self.nc.dram_tensor(name, list(shape), dtype, kind=kind).ap()
        return self.hbm[name]

    def sbuf(self, name, shape, dtype):
        if name not in self.tiles:
            self.tiles[name] = self.es.enter_context(self.nc.sbuf_tensor(name, list(shape), dtype))
        return self.tiles[name]

    def psum(self, name, shape, dtype=F32):
        if name not in self.psums:
            self.psums[name] = self.es.enter_context(self.nc.psum_tensor(name, list(shape), dtype))
        return self.psums[name]

    def sem(self, name):
        if name not in self.sems:
            self.sems[name] = self.es.enter_context(self.nc.semaphore(name))
            self.cnt[name] = 0
        return self.sems[name]

    def inc(self, name, n=1):
        self.sem(name)
        self.cnt[name] += n
        return self.cnt[name]

    def val(self, name):
        self.sem(name)
        return self.cnt[name]

    def emit(self, engine, f):
        self.prog[engine].append(f)

    def wait(self, engine, semname, value):
        s = self.sem(semname)
        self.emit(engine, lambda eng, _s=s, _v=value: eng.wait_ge(_s, _v))

    def op(self, engine, fname, *args, inc=None, **kw):
        if inc is not None:
            tot = self.inc(inc[0], inc[1])
            s = self.sems[inc[0]]

            def f(eng, _f=fname, _a=args, _k=kw, _s=s, _n=inc[1]):
                getattr(eng, _f)(*_a, **_k).then_inc(_s, _n)
            self.emit(engine, f)
            return tot

        def f(eng, _f=fname, _a=args, _k=kw):
            getattr(eng, _f)(*_a, **_k)
        self.emit(engine, f)
        return None

    def dma(self, engine, dst, src, sem="ld", **kw):
        return self.op(engine, "dma_start", inc=(sem, 16), out=dst, in_=src, **kw)

    def drain(self, engine):
        self.emit(engine, lambda eng: eng.drain())

    def sem_inc(self, engine, semname, n=1):
        tot = self.inc(semname, n)
        s = self.sems[semname]
        self.emit(engine, lambda eng, _s=s, _n=n: eng.sem_inc(_s, _n))
        return tot


def finalize(ctx, block):
    @block.sync
    def _(eng):
        for f in ctx.prog["sync"]:
            f(eng)

    @block.vector
    def _(eng):
        for f in ctx.prog["vector"]:
            f(eng)

    @block.scalar
    def _(eng):
        for f in ctx.prog["scalar"]:
            f(eng)

    @block.tensor
    def _(eng):
        for f in ctx.prog["tensor"]:
            f(eng)

    @block.gpsimd
    def _(eng):
        for f in ctx.prog["gpsimd"]:
            f(eng)


def setup_common(c):
    iot = c.sbuf("c_iot", [128, 128], F32)
    idn = c.sbuf("c_idnf", [128, 128], F32)
    idnb = c.sbuf("c_idnb", [128, 128], BF16)
    onesr = c.sbuf("c_onesr", [1, 128], F32)
    ones128 = c.sbuf("c_ones128", [128, 1], F32)
    sclr = c.sbuf("c_sclr", [1, 128], F32)
    c.op("gpsimd", "iota", iot[:], pattern=[[1, 128]], base=0,
         channel_multiplier=-1, allow_small_or_imprecise_dtypes=True, inc=("g_idn", 1))
    c.wait("vector", "g_idn", 1)
    c.op("vector", "tensor_scalar", idn[:], iot[:], 0.0, None, Op.is_equal)
    c.op("vector", "tensor_scalar", idnb[:], iot[:], 0.0, None, Op.is_equal)
    c.op("vector", "memset", onesr[:], 1.0)
    c.op("vector", "memset", ones128[:], 1.0)
    c.op("vector", "memset", sclr[:], -0.125)
    eps = c.sbuf("c_eps", [128, 1], F32)
    c.op("vector", "memset", eps[:], 1e-5)
    c.drain("vector")
    c.sem_inc("vector", "v_init", 1)



def ph_qkv(c, scratch_kind="ExternalOutput"):
    x = c.dram("x_in", [T, D], F32, kind="ExternalInput")
    wq = c.dram("L0_attn_w_qkv", [D, 3 * D], F32, kind="ExternalInput")
    bqt_h = c.dram("b_qkv_t", [128, 24], F32, kind="ExternalInput")
    bqr_h = c.dram("b_qkv_r", [1, 3 * D], F32, kind="ExternalInput")
    Qf = c.dram("Qf", [D, T], BF16, kind=scratch_kind)
    Kf = c.dram("Kf", [D, T], BF16, kind=scratch_kind)
    Vt = c.dram("Vt", [T, D], BF16, kind=scratch_kind)

    idn = c.tiles["c_idnf"]
    onesr = c.tiles["c_onesr"]
    wsb = c.sbuf("q_w", [128, 8 * 3072], BF16)
    bqt = c.sbuf("q_bt", [128, 24], F32)
    bvrow = c.sbuf("q_bvrow", [1, D], F32)
    vbias = c.sbuf("q_vbias", [128, D], F32)
    xtok = [c.sbuf(f"q_xtok{i}", [128, 4 * D], F32) for i in range(2)]
    xfm = c.sbuf("q_xfm", [128, 8 * TC], BF16)
    stg = [c.sbuf(f"q_stg{i}", [128, TC], BF16) for i in range(2)]
    vstg = [c.sbuf(f"q_vstg{i}", [128, D], BF16) for i in range(2)]

    ps_tr = [c.psum(f"q_pstr{i}", [128, 128], F32) for i in range(2)]
    ps_mm = [c.psum(f"q_psmm{i}", [128, TC], F32) for i in range(2)]
    ps_v = [c.psum(f"q_psv{i}", [128, D], F32) for i in range(2)]

    for kc in range(8):
        c.dma("gpsimd", wsb[:, kc * 3072:(kc + 1) * 3072],
              wq[kc * 128:(kc + 1) * 128, :], sem="q_w")
    c.dma("sync", bqt[:], bqt_h, sem="q_b")
    c.dma("sync", bvrow[:], bqr_h[:, 2 * D:3 * D], sem="q_br")

    c.wait("tensor", "v_init", 1)
    c.wait("tensor", "q_br", 16)
    for half in range(2):
        c.op("tensor", "matmul", ps_v[0][:, half * 512:(half + 1) * 512],
             onesr[:], bvrow[:, half * 512:(half + 1) * 512],
             start=True, stop=True, inc=("q_pev", 1) if half == 1 else None)
    c.wait("vector", "q_pev", 1)
    c.op("vector", "tensor_copy", vbias[:], ps_v[0][:])
    c.drain("vector")
    c.sem_inc("vector", "q_vv", 1)

    c.wait("scalar", "q_b", 16)
    c.wait("tensor", "q_w", 16 * 8)

    jtr = 0
    gm = 0
    gv = 0
    for tc in range(NTC):
        if tc >= 2:
            c.wait("sync", "q_pet", 32 * (tc - 1))
        src = x[tc * TC:(tc + 1) * TC, :].rearrange("(i p) f -> p i f", p=128)
        c.dma("sync", xtok[tc % 2][:].rearrange("p (i f) -> p i f", i=4), src, sem="q_x")

        c.wait("tensor", "q_x", 16 * (tc + 1))
        for i in range(4):
            for fc in range(8):
                if jtr >= 2:
                    c.wait("tensor", "q_vx", jtr - 1)
                c.op("tensor", "transpose", ps_tr[jtr % 2][:],
                     xtok[tc % 2][:, i * D + fc * 128:i * D + (fc + 1) * 128],
                     idn[:], inc=("q_pet", 1))
                c.wait("vector", "q_pet", jtr + 1)
                c.op("vector", "tensor_copy",
                     xfm[:, fc * TC + i * 128:fc * TC + (i + 1) * 128],
                     ps_tr[jtr % 2][:])
                c.drain("vector")
                c.sem_inc("vector", "q_vx", 1)
                jtr += 1

        for m in range(16):
            if gm >= 2:
                c.wait("tensor", "q_a", gm - 1)
            for kc in range(8):
                c.op("tensor", "matmul", ps_mm[gm % 2][:],
                     wsb[:, kc * 3072 + m * 128:kc * 3072 + (m + 1) * 128],
                     xfm[:, kc * TC:(kc + 1) * TC],
                     start=(kc == 0), stop=(kc == 7),
                     inc=("q_pem", 1) if kc == 7 else None)
            c.wait("scalar", "q_pem", gm + 1)
            if gm >= 2:
                c.wait("scalar", "q_st", 16 * (gm - 1))
            c.op("scalar", "add", stg[gm % 2][:], ps_mm[gm % 2][:],
                 bqt[:, m:m + 1])
            c.drain("scalar")
            c.sem_inc("scalar", "q_a", 1)
            c.wait("sync", "q_a", gm + 1)
            dst = (Qf if m < 8 else Kf)
            mm = m if m < 8 else m - 8
            c.dma("sync", dst[mm * 128:(mm + 1) * 128, tc * TC:(tc + 1) * TC],
                  stg[gm % 2][:], sem="q_st")
            gm += 1

        for i in range(4):
            if gv >= 2:
                c.wait("tensor", "q_vv", 1 + (gv - 1))
            for kc in range(8):
                for half in range(2):
                    c.op("tensor", "matmul",
                         ps_v[gv % 2][:, half * 512:(half + 1) * 512],
                         xfm[:, kc * TC + i * 128:kc * TC + (i + 1) * 128],
                         wsb[:, kc * 3072 + 2048 + half * 512:kc * 3072 + 2048 + (half + 1) * 512],
                         start=(kc == 0), stop=(kc == 7),
                         inc=("q_pev", 1) if (kc == 7 and half == 1) else None)
            c.wait("vector", "q_pev", 1 + gv + 1)
            if gv >= 2:
                c.wait("vector", "q_stv", 16 * (gv - 1))
            c.op("vector", "tensor_tensor", vstg[gv % 2][:], ps_v[gv % 2][:],
                 vbias[:], Op.add)
            c.drain("vector")
            c.sem_inc("vector", "q_vv", 1)
            c.wait("sync", "q_vv", 1 + gv + 1)
            c.dma("sync", Vt[tc * TC + i * 128:tc * TC + (i + 1) * 128, :],
                  vstg[gv % 2][:], sem="q_stv")
            gv += 1

    c.wait("sync", "q_st", 16 * 128)
    c.wait("sync", "q_stv", 16 * 32)



def ph_attn(c):
    Qf = c.dram("Qf", [D, T], BF16)
    Kf = c.dram("Kf", [D, T], BF16)
    Vt = c.dram("Vt", [T, D], BF16)
    attnf = c.dram("attnf", [D, T], BF16)

    onesr = c.tiles["c_onesr"]
    sclr = c.tiles["c_sclr"]
    qsb = [c.sbuf(f"a_q{i}", [64, T], BF16) for i in range(2)]
    ksb = [c.sbuf(f"a_k{i}", [64, T], BF16) for i in range(2)]
    vaug = [c.sbuf(f"a_v{i}", [128, 32 * 65], BF16) for i in range(2)]
    ssb = [c.sbuf(f"a_s{i}", [128, 16384], BF16) for i in range(2)]
    t_mx = c.sbuf("a_mx", [128, 1], F32)
    t_mxr = c.sbuf("a_mxr", [1, 128], F32)
    t_negc = c.sbuf("a_negc", [1, 1], F32)
    t_bias = [c.sbuf(f"a_bias{i}", [128, 1], F32) for i in range(2)]
    t_r = c.sbuf("a_r", [1, TC], F32)
    t_rb = c.sbuf("a_rb", [64, TC], F32)
    astg = [c.sbuf(f"a_stg{i}", [64, TC], BF16) for i in range(2)]
    pst = [c.psum(f"a_ps{i}", [128, 1024], F32) for i in range(4)]

    for i in range(2):
        c.op("vector", "memset", vaug[i][:, 64:32 * 65:65], 1.0)
    c.drain("vector")
    c.sem_inc("vector", "a_vone", 1)
    c.wait("gpsimd", "a_vone", 1)

    NIT = H * NTC
    for it in range(NIT + 1):
        if it < NIT:
            h, qc = divmod(it, NTC)
            pair = it % 2
            tA0, tA1 = pst[pair * 2], pst[pair * 2 + 1]
            if qc == 0:
                hb = h % 2
                if h >= 2:
                    c.wait("sync", "a_av", (h - 1) * NTC)
                    c.wait("gpsimd", "a_av", (h - 1) * NTC)
                    c.wait("sync", "a_qk", (h - 1) * NTC * 16)
                c.dma("sync", qsb[hb][:], Qf[64 * h:64 * (h + 1), :], sem="a_ld")
                c.dma("sync", ksb[hb][:], Kf[64 * h:64 * (h + 1), :], sem="a_ld")
                vdst = vaug[hb][:].rearrange("p (kc f) -> p kc f", kc=32)[:, :, 0:64]
                vsrc = Vt[:, 64 * h:64 * (h + 1)].rearrange("(kc p) f -> p kc f", p=128)
                c.dma("gpsimd", vdst, vsrc, sem="a_ld")
            hb = h % 2
            c.wait("tensor", "a_ld", 48 * (h + 1))
            for g in range(16):
                if g < 2:
                    if it >= 2:
                        c.wait("tensor", "a_ev", (it - 1) * 16)
                        c.wait("tensor", "a_fin", it - 1)
                else:
                    c.wait("tensor", "a_ev", it * 16 + g - 1)
                for half in range(2):
                    kc = 2 * g + half
                    c.op("tensor", "matmul", (tA0 if g % 2 == 0 else tA1)[:, half * 512:(half + 1) * 512],
                         ksb[hb][:, kc * 128:(kc + 1) * 128],
                         qsb[hb][:, qc * 512:(qc + 1) * 512],
                         start=True, stop=True,
                         inc=("a_qk", 1) if half == 1 else None)
            for g in range(16):
                c.wait("vector", "a_qk", it * 16 + g + 1)
                if g == 0 and it >= 2:
                    c.wait("vector", "a_av", it - 1)
                c.op("vector", "tensor_copy", ssb[pair][:, g * 1024:(g + 1) * 1024],
                     (tA0 if g % 2 == 0 else tA1)[:], inc=("a_ev", 1))
            c.drain("vector")
            c.sem_inc("vector", "a_evd", 1)
            c.wait("gpsimd", "a_evd", it + 1)
            c.op("gpsimd", "tensor_reduce", t_negc[:], ssb[pair][:], AX.XYZWC, Op.max)
            c.drain("gpsimd")
            c.sem_inc("gpsimd", "a_g", 1)
            c.wait("tensor", "a_g", it + 1)
            c.op("tensor", "matmul", tA0[:, 0:1], sclr[:], t_negc[:],
                 start=True, stop=True, inc=("a_pb1", 1))
            c.wait("vector", "a_pb1", it + 1)
            c.op("vector", "tensor_copy", t_bias[pair][:], tA0[:, 0:1])
            c.drain("vector")
            c.sem_inc("vector", "a_vb", 1)
            c.wait("scalar", "a_vb", it + 1)
            c.op("scalar", "activation", ssb[pair][:], ssb[pair][:], AF.Exp,
                 bias=t_bias[pair][:, 0:1], scale=0.125)
            c.drain("scalar")
            c.sem_inc("scalar", "a_exp", 1)

        if it >= 1:
            hq = it - 1
            hB, qcB = divmod(hq, NTC)
            pb = hq % 2
            tB0, tB1 = pst[pb * 2], pst[pb * 2 + 1]
            hbB = hB % 2
            c.wait("tensor", "a_exp", hq + 1)
            for kc in range(32):
                c.op("tensor", "matmul", tB0[0:65, 0:512],
                     vaug[hbB][:, kc * 65:kc * 65 + 65],
                     ssb[pb][:, kc * 512:(kc + 1) * 512],
                     start=(kc == 0), stop=(kc == 31),
                     inc=("a_av", 1) if kc == 31 else None)
            c.wait("vector", "a_av", hq + 1)
            c.op("vector", "reciprocal", t_r[:], tB0[64:65, 0:512])
            c.drain("vector")
            c.sem_inc("vector", "a_vr", 1)
            c.wait("tensor", "a_vr", hq + 1)
            c.op("tensor", "matmul", tB1[0:64, 0:512], onesr[:, 0:64], t_r[:],
                 start=True, stop=True, inc=("a_pb2", 1))
            c.wait("vector", "a_pb2", hq + 1)
            c.op("vector", "tensor_copy", t_rb[:], tB1[0:64, 0:512])
            c.drain("vector")
            if hq >= 2:
                c.wait("vector", "a_st", 16 * (hq - 1))
            c.op("vector", "tensor_tensor", astg[hq % 2][:], tB0[0:64, 0:512],
                 t_rb[:], Op.mult)
            c.drain("vector")
            c.sem_inc("vector", "a_fin", 1)
            c.wait("sync", "a_fin", hq + 1)
            c.dma("sync", attnf[64 * hB:64 * (hB + 1), qcB * 512:(qcB + 1) * 512],
                  astg[hq % 2][:], sem="a_st")

    c.wait("sync", "a_st", 16 * NIT)



def emit_ln_tail(c, P, qi, rsb, tiles, out_dst, gslice=None, bslice=None,
                 rsb_ready=None, bfstg=None, bf_dst=None, eps=1e-5, nfo=8):
    (sq1, mrow, sq2, vrow, lnrow, rstdrow, xstg, ps_s0, ps_s1, ps_b0, ps_b1) = tiles
    ones128 = c.tiles["c_ones128"]
    onesr = c.tiles["c_onesr"]
    for fo in range(nfo):
        if rsb_ready is not None:
            c.wait("scalar", rsb_ready[0], rsb_ready[1](fo))
        if qi * nfo + fo >= 1:
            c.wait("scalar", f"{P}_psq", qi * nfo + fo)
        c.op("scalar", "activation", sq1[:], rsb[:, fo * 512:(fo + 1) * 512], AF.Square)
        c.drain("scalar")
        c.sem_inc("scalar", f"{P}_sq", 1)
        if rsb_ready is not None:
            c.wait("tensor", rsb_ready[0], rsb_ready[1](fo))
        if qi > 0 or fo > 0:
            c.wait("tensor", f"{P}_row", qi)
        c.op("tensor", "matmul", ps_s0[0:1, :], ones128[:], rsb[:, fo * 512:(fo + 1) * 512],
             start=(fo == 0), stop=(fo == nfo - 1),
             inc=(f"{P}_ps", 1) if fo == nfo - 1 else None)
        c.wait("tensor", f"{P}_sq", qi * nfo + fo + 1)
        c.op("tensor", "matmul", ps_s1[0:1, :], ones128[:], sq1[:],
             start=(fo == 0), stop=(fo == nfo - 1),
             inc=(f"{P}_psq", 1))
        if fo == nfo - 1:
            c.sem_inc("tensor", f"{P}_ps", 1)
    c.wait("scalar", f"{P}_ps", 2 * (qi + 1))
    c.op("scalar", "mul", mrow[0:1, :], ps_s0[0:1, :], 1.0 / D)
    c.drain("scalar")
    c.sem_inc("scalar", f"{P}_m", 1)
    c.wait("vector", f"{P}_m", qi + 1)
    c.op("vector", "tensor_mul", sq2[0:1, :], mrow[0:1, :], mrow[0:1, :])
    c.drain("vector")
    c.op("vector", "scalar_tensor_tensor", vrow[0:1, :], ps_s1[0:1, :], 1.0 / D,
         sq2[0:1, :], Op.mult, Op.subtract)
    c.drain("vector")
    c.sem_inc("vector", f"{P}_v", 1)
    c.wait("scalar", f"{P}_v", qi + 1)
    c.op("scalar", "activation", lnrow[0:1, :], vrow[0:1, :], AF.Ln,
         bias=c.tiles["c_eps"][0:1, 0:1])
    c.drain("scalar")
    c.op("scalar", "activation", rstdrow[0:1, :], lnrow[0:1, :], AF.Exp, scale=-0.5)
    c.drain("scalar")
    c.sem_inc("scalar", f"{P}_row", 1)
    c.wait("tensor", f"{P}_row", qi + 1)
    c.op("tensor", "matmul", ps_b0[:, 0:512], onesr[:], mrow[0:1, :],
         start=True, stop=True)
    c.op("tensor", "matmul", ps_b1[:, 0:512], onesr[:], rstdrow[0:1, :],
         start=True, stop=True, inc=(f"{P}_bc", 1))
    for fo in range(nfo):
        c.wait("vector", f"{P}_bc", qi + 1)
        if out_dst is not None and qi * nfo + fo >= 2:
            c.wait("vector", f"{P}_st", 16 * (qi * nfo + fo - 1))
        st = xstg[(qi * nfo + fo) % 2]
        if bf_dst is not None:
            final = bf_dst(fo)
        elif bfstg is not None:
            final = bfstg[(qi * nfo + fo) % 2]
        else:
            final = st
        c.op("vector", "tensor_tensor", st[:], rsb[:, fo * 512:(fo + 1) * 512],
             ps_b0[:, 0:512], Op.subtract)
        c.drain("vector")
        if gslice is not None:
            c.op("vector", "tensor_tensor", st[:], st[:], ps_b1[:, 0:512], Op.mult)
            c.drain("vector")
            c.op("vector", "scalar_tensor_tensor", final[:], st[:], gslice(fo),
                 bslice(fo).broadcast_to([128, 512]), Op.mult, Op.add)
            c.drain("vector")
        else:
            c.op("vector", "tensor_tensor", final[:], st[:], ps_b1[:, 0:512], Op.mult)
            c.drain("vector")
        c.sem_inc("vector", f"{P}_va", 1)
        if out_dst is not None:
            c.wait("sync", f"{P}_va", qi * nfo + fo + 1)
            c.dma("sync", out_dst(fo), final[:], sem=f"{P}_st")
    c.wait("tensor", f"{P}_va", (qi + 1) * nfo)


def ln_tiles(c, P):
    return (
        c.sbuf(f"{P}_sq1", [128, 512], F32),
        c.sbuf(f"{P}_mrow", [1, 512], F32),
        c.sbuf(f"{P}_sq2", [1, 512], F32),
        c.sbuf(f"{P}_vrow", [1, 512], F32),
        c.sbuf(f"{P}_lnrow", [1, 512], F32),
        c.sbuf(f"{P}_rstd", [1, 512], F32),
        [c.sbuf(f"{P}_xstg{i}", [128, 512], F32) for i in range(2)],
        c.psum(f"{P}_pss0", [128, 512], F32),
        c.psum(f"{P}_pss1", [128, 512], F32),
        c.psum(f"{P}_psb0", [128, 512], F32),
        c.psum(f"{P}_psb1", [128, 512], F32),
    )



def ph_proj(c):
    x = c.dram("x_in", [T, D], F32, kind="ExternalInput")
    wp = c.dram("L0_attn_w_proj", [D, D], F32, kind="ExternalInput")
    bp_h = c.dram("b_proj_t", [128, 8], F32, kind="ExternalInput")
    attnf = c.dram("attnf", [D, T], BF16)
    x1 = c.dram("x1", [D, T], F32)

    idn = c.tiles["c_idnf"]
    wsb = c.sbuf("p3_w", [128, 8 * D], BF16)
    bpt = c.sbuf("p3_b", [128, 8], F32)
    xtok = [c.sbuf(f"p3_xtok{i}", [128, 4 * D], F32) for i in range(2)]
    att = [c.sbuf(f"p3_att{i}", [128, 8 * 512], BF16) for i in range(2)]
    xf32 = [c.sbuf(f"p3_xf{i}", [128, 8 * 512], F32) for i in range(2)]
    rsb = c.sbuf("p3_rsb", [128, 8 * 512], F32)
    lt = ln_tiles(c, "p3")
    ps_b0, ps_b1 = lt[-2], lt[-1]
    ps_p = [c.psum(f"p3_psp{i}", [128, 512], F32) for i in range(2)]

    for kc in range(8):
        c.dma("gpsimd", wsb[:, kc * D:(kc + 1) * D], wp[kc * 128:(kc + 1) * 128, :],
              sem="p3_w")
    c.dma("sync", bpt[:], bp_h, sem="p3_b")
    c.wait("tensor", "p3_w", 16 * 8)
    c.wait("vector", "p3_b", 16)
    c.wait("tensor", "v_init", 1)

    jtr = 0
    for qc in range(NTC):
        if qc >= 2:
            c.wait("sync", "p3_tr", 32 * (qc - 1))
            c.wait("sync", "p3_pm", 8 * (qc - 1))
        xs = x[qc * TC:(qc + 1) * TC, :].rearrange("(i p) f -> p i f", p=128)
        c.dma("sync", xtok[qc % 2][:].rearrange("p (i f) -> p i f", i=4), xs, sem="p3_x")
        asrc = attnf[:, qc * TC:(qc + 1) * TC].rearrange("(kc p) t -> p kc t", p=128)
        c.dma("sync", att[qc % 2][:].rearrange("p (kc t) -> p kc t", kc=8), asrc,
              sem="p3_a")

        c.wait("tensor", "p3_x", 16 * (qc + 1))
        if qc >= 1:
            c.wait("tensor", "p3_va", qc * 8)
        for i in range(4):
            for fc in range(8):
                tt = (ps_b0 if jtr % 2 == 0 else ps_b1)
                if jtr >= 2:
                    c.wait("tensor", "p3_vx", jtr - 1)
                c.op("tensor", "transpose", tt[:, 0:128],
                     xtok[qc % 2][:, i * D + fc * 128:i * D + (fc + 1) * 128],
                     idn[:], inc=("p3_tr", 1))
                c.wait("vector", "p3_tr", jtr + 1)
                c.op("vector", "tensor_copy",
                     xf32[qc % 2][:, fc * 512 + i * 128:fc * 512 + (i + 1) * 128],
                     tt[:, 0:128])
                c.drain("vector")
                c.sem_inc("vector", "p3_vx", 1)
                jtr += 1

        c.wait("tensor", "p3_a", 16 * (qc + 1))
        for fo in range(8):
            g = qc * 8 + fo
            if g >= 2:
                c.wait("tensor", "p3_vr", g - 1)
            for kc in range(8):
                c.op("tensor", "matmul", ps_p[fo % 2][:],
                     wsb[:, kc * D + fo * 128:kc * D + (fo + 1) * 128],
                     att[qc % 2][:, kc * 512:(kc + 1) * 512],
                     start=(kc == 0), stop=(kc == 7),
                     inc=("p3_pm", 1) if kc == 7 else None)
            c.wait("vector", "p3_pm", g + 1)
            c.op("vector", "scalar_tensor_tensor", rsb[:, fo * 512:(fo + 1) * 512],
                 ps_p[fo % 2][:], bpt[:, fo:fo + 1],
                 xf32[qc % 2][:, fo * 512:(fo + 1) * 512], Op.add, Op.add)
            c.drain("vector")
            c.sem_inc("vector", "p3_vr", 1)

        emit_ln_tail(
            c, "p3", qc, rsb, lt,
            out_dst=lambda fo, _qc=qc: x1[:, _qc * TC:(_qc + 1) * TC]
                .rearrange("(fc p) t -> fc p t", p=128)[fo],
            rsb_ready=("p3_vr", lambda fo, _qc=qc: _qc * 8 + fo + 1),
        )

    c.wait("sync", "p3_st", 16 * 64)



def ph_ffnA(c):
    x1 = c.dram("x1", [D, T], F32)
    w1h = c.dram("w1T", [D, FH], F32, kind="ExternalInput")
    w2h = c.dram("w2T", [D, FH], F32, kind="ExternalInput")
    s1h = c.dram("w1s_t", [128, 32], F32, kind="ExternalInput")
    s2h = c.dram("w2s_t", [128, 32], F32, kind="ExternalInput")
    hA = c.dram("hA", [FH, T], BF16)

    w1 = c.sbuf("f4_w1", [128, 8 * FH], BF16)
    w2 = c.sbuf("f4_w2", [128, 8 * FH], BF16)
    s1 = c.sbuf("f4_s1", [128, 32], F32)
    s2 = c.sbuf("f4_s2", [128, 32], F32)
    x1b = [c.sbuf(f"f4_x{i}", [128, 8 * 512], BF16) for i in range(2)]
    tA = c.sbuf("f4_tA", [128, 512], F32)
    tB = c.sbuf("f4_tB", [128, 512], F32)
    hstg = [c.sbuf(f"f4_h{i}", [128, 512], BF16) for i in range(2)]
    psA = [c.psum(f"f4_psA{i}", [128, 512], F32) for i in range(2)]
    psB = [c.psum(f"f4_psB{i}", [128, 512], F32) for i in range(2)]

    for kc in range(8):
        c.dma("gpsimd", w1[:, kc * FH:(kc + 1) * FH], w1h[kc * 128:(kc + 1) * 128, :], sem="f4_w")
        c.dma("gpsimd", w2[:, kc * FH:(kc + 1) * FH], w2h[kc * 128:(kc + 1) * 128, :], sem="f4_w")
    c.dma("sync", s1[:], s1h, sem="f4_s")
    c.dma("sync", s2[:], s2h, sem="f4_s")
    c.wait("tensor", "f4_w", 16 * 16)
    c.wait("scalar", "f4_s", 32)

    g = 0
    for qc in range(NTC):
        if qc >= 2:
            c.wait("gpsimd", "f4_pm", 32 * (qc - 1))
        xs = x1[:, qc * TC:(qc + 1) * TC].rearrange("(kc p) t -> p kc t", p=128)
        c.dma("gpsimd", x1b[qc % 2][:].rearrange("p (kc t) -> p kc t", kc=8), xs, sem="f4_x")
        c.wait("tensor", "f4_x", 16 * (qc + 1))
        for m in range(32):
            if g >= 2:
                c.wait("tensor", "f4_v", g - 1)
            for kc in range(8):
                c.op("tensor", "matmul", psA[g % 2][:],
                     w1[:, kc * FH + m * 128:kc * FH + (m + 1) * 128],
                     x1b[qc % 2][:, kc * 512:(kc + 1) * 512],
                     start=(kc == 0), stop=(kc == 7))
                c.op("tensor", "matmul", psB[g % 2][:],
                     w2[:, kc * FH + m * 128:kc * FH + (m + 1) * 128],
                     x1b[qc % 2][:, kc * 512:(kc + 1) * 512],
                     start=(kc == 0), stop=(kc == 7),
                     inc=("f4_pm", 1) if kc == 7 else None)
            c.wait("scalar", "f4_pm", g + 1)
            c.wait("scalar", "f4_v", g)
            c.op("scalar", "activation", tA[:], psA[g % 2][:], AF.Silu,
                 scale=s1[:, m:m + 1])
            c.drain("scalar")
            c.op("scalar", "mul", tB[:], psB[g % 2][:], s2[:, m:m + 1])
            c.drain("scalar")
            c.sem_inc("scalar", "f4_a", 1)
            c.wait("vector", "f4_a", g + 1)
            if g >= 2:
                c.wait("vector", "f4_st", 16 * (g - 1))
            c.op("vector", "tensor_tensor", hstg[g % 2][:], tA[:], tB[:], Op.mult)
            c.drain("vector")
            c.sem_inc("vector", "f4_v", 1)
            c.wait("sync", "f4_v", g + 1)
            c.dma("sync", hA[m * 128:(m + 1) * 128, qc * TC:(qc + 1) * TC],
                  hstg[g % 2][:], sem="f4_st")
            g += 1
    c.wait("sync", "f4_st", 16 * 256)



def ph_hmid(c, P, win_name, s_name, hin_name, hout_name):
    wb_h = c.dram(win_name, [32, 32, 128, 128], F32, kind="ExternalInput")
    sh = c.dram(s_name, [128, 32], F32, kind="ExternalInput")
    hIn = c.dram(hin_name, [FH, T], BF16)
    hOut = c.dram(hout_name, [FH, T], BF16)

    hAt = c.sbuf(f"{P}_hin", [128, 32 * 2048], BF16)
    wblk = [c.sbuf(f"{P}_w{i}", [128, 32 * 128], BF16) for i in range(2)]
    ssb = c.sbuf(f"{P}_s", [128, 32], F32)
    hstg = [c.sbuf(f"{P}_h{i}", [128, 512], BF16) for i in range(2)]
    ps = [c.psum(f"{P}_ps{i}", [128, 512], F32) for i in range(2)]

    c.dma("sync", ssb[:], sh, sem=f"{P}_sld")
    c.wait("scalar", f"{P}_sld", 16)

    g = 0
    for half in range(2):
        if half >= 1:
            c.wait("sync", f"{P}_pm", 128 * half)
        hs = hIn[:, half * 2048:(half + 1) * 2048].rearrange("(kc p) t -> p kc t", p=128)
        c.dma("sync", hAt[:].rearrange("p (kc t) -> p kc t", kc=32), hs, sem=f"{P}_hx")
        c.wait("gpsimd", f"{P}_hx", 16 * (half + 1))
        for m in range(32):
            gm = half * 32 + m
            if gm >= 2:
                c.wait("gpsimd", f"{P}_pm", 4 * (gm - 1))
            c.dma("gpsimd", wblk[gm % 2][:].rearrange("p (kc mo) -> p kc mo", kc=32),
                  wb_h[m].rearrange("kc p mo -> p kc mo"), sem=f"{P}_w")
            c.wait("tensor", f"{P}_w", 16 * (gm + 1))
            for n in range(4):
                if g >= 2:
                    c.wait("tensor", f"{P}_a", g - 1)
                for kc in range(32):
                    c.op("tensor", "matmul", ps[g % 2][:],
                         wblk[gm % 2][:, kc * 128:(kc + 1) * 128],
                         hAt[:, kc * 2048 + n * 512:kc * 2048 + (n + 1) * 512],
                         start=(kc == 0), stop=(kc == 31),
                         inc=(f"{P}_pm", 1) if kc == 31 else None)
                c.wait("scalar", f"{P}_pm", g + 1)
                if g >= 2:
                    c.wait("scalar", f"{P}_st", 16 * (g - 1))
                c.op("scalar", "activation", hstg[g % 2][:], ps[g % 2][:], AF.Gelu,
                     scale=ssb[:, m:m + 1])
                c.drain("scalar")
                c.sem_inc("scalar", f"{P}_a", 1)
                c.wait("sync", f"{P}_a", g + 1)
                c.dma("sync", hOut[m * 128:(m + 1) * 128,
                                   half * 2048 + n * 512:half * 2048 + (n + 1) * 512],
                      hstg[g % 2][:], sem=f"{P}_st")
                g += 1
    c.wait("sync", f"{P}_st", 16 * 256)


def ph_h1(c):
    ph_hmid(c, "p5", "h1b", "h1s_t", "hA", "hB")


def ph_h2(c):
    ph_hmid(c, "p6", "h2b", "h2s_t", "hB", "hC")



def ph_w3ln(c):
    w3h = c.dram("w3T", [FH, D], F32, kind="ExternalInput")
    s3h = c.dram("w3s_t", [128, 8], F32, kind="ExternalInput")
    hC = c.dram("hC", [FH, T], BF16)
    x1 = c.dram("x1", [D, T], F32)
    x2 = c.dram("x2", [D, T], F32)

    w3 = c.sbuf("p7_w3", [128, 32 * D], BF16)
    s3 = c.sbuf("p7_s3", [128, 8], F32)
    hCt = [c.sbuf(f"p7_hc{i}", [128, 32 * 512], BF16) for i in range(2)]
    x1f = [c.sbuf(f"p7_x1{i}", [128, 8 * 512], F32) for i in range(2)]
    rsb = c.sbuf("p7_rsb", [128, 8 * 512], F32)
    lt = ln_tiles(c, "p7")
    ps_p = [c.psum(f"p7_psp{i}", [128, 512], F32) for i in range(2)]

    for kc in range(32):
        c.dma("gpsimd", w3[:, kc * D:(kc + 1) * D], w3h[kc * 128:(kc + 1) * 128, :],
              sem="p7_w")
    c.dma("sync", s3[:], s3h, sem="p7_s")
    c.wait("tensor", "p7_w", 16 * 32)
    c.wait("vector", "p7_s", 16)
    c.wait("tensor", "v_init", 1)

    for qc in range(NTC):
        if qc >= 2:
            c.wait("sync", "p7_pm", 8 * (qc - 1))
            c.wait("sync", "p7_vr", 8 * (qc - 1))
        hs = hC[:, qc * TC:(qc + 1) * TC].rearrange("(kc p) t -> p kc t", p=128)
        c.dma("sync", hCt[qc % 2][:].rearrange("p (kc t) -> p kc t", kc=32), hs,
              sem="p7_hx")
        xs = x1[:, qc * TC:(qc + 1) * TC].rearrange("(kc p) t -> p kc t", p=128)
        c.dma("sync", x1f[qc % 2][:].rearrange("p (kc t) -> p kc t", kc=8), xs,
              sem="p7_x1")
        c.wait("tensor", "p7_hx", 16 * (qc + 1))
        c.wait("vector", "p7_x1", 16 * (qc + 1))
        for fo in range(8):
            g = qc * 8 + fo
            if g >= 2:
                c.wait("tensor", "p7_vr", g - 1)
            for kc in range(32):
                c.op("tensor", "matmul", ps_p[fo % 2][:],
                     w3[:, kc * D + fo * 128:kc * D + (fo + 1) * 128],
                     hCt[qc % 2][:, kc * 512:(kc + 1) * 512],
                     start=(kc == 0), stop=(kc == 31),
                     inc=("p7_pm", 1) if kc == 31 else None)
            c.wait("vector", "p7_pm", g + 1)
            c.op("vector", "scalar_tensor_tensor", rsb[:, fo * 512:(fo + 1) * 512],
                 ps_p[fo % 2][:], s3[:, fo:fo + 1],
                 x1f[qc % 2][:, fo * 512:(fo + 1) * 512], Op.mult, Op.add)
            c.drain("vector")
            c.sem_inc("vector", "p7_vr", 1)
        emit_ln_tail(
            c, "p7", qc, rsb, lt,
            out_dst=lambda fo, _qc=qc: x2[:, _qc * TC:(_qc + 1) * TC]
                .rearrange("(fc p) t -> fc p t", p=128)[fo],
            rsb_ready=("p7_vr", lambda fo, _qc=qc: _qc * 8 + fo + 1),
        )
    c.wait("sync", "p7_st", 16 * 64)



def ph_lnwto(c):
    x2 = c.dram("x2", [D, T], F32)
    wtoh = c.dram("W_toT", [D, 2 * FR], F32, kind="ExternalInput")
    g1h = c.dram("ln1g_t", [128, 8], F32, kind="ExternalInput")
    b1h = c.dram("ln1b_t", [128, 8], F32, kind="ExternalInput")
    frh = c.dram("freq2", [128, 2], F32, kind="ExternalInput")
    ubuf = c.dram("ubuf", [128, 4 * T], F32)

    wto = c.sbuf("p8_wto", [128, 8 * 512], BF16)
    g1 = c.sbuf("p8_g1", [128, 8], F32)
    b1 = c.sbuf("p8_b1", [128, 8], F32)
    fr2 = c.sbuf("p8_fr", [128, 2], F32)
    om = c.sbuf("p8_om", [128, 2], F32)
    c01 = c.sbuf("p8_c01", [128, 2], F32)
    s01 = c.sbuf("p8_s01", [128, 2], F32)
    ns01 = c.sbuf("p8_ns01", [128, 2], F32)
    halfpi = c.sbuf("p8_hpi", [128, 1], F32)
    x2f = [c.sbuf(f"p8_x2{i}", [128, 8 * 512], F32) for i in range(2)]
    lnx = c.sbuf("p8_lnx", [128, 8 * T], BF16)
    tra = c.sbuf("p8_tra", [128, 512], F32)
    trb = c.sbuf("p8_trb", [128, 512], F32)
    ustg = [c.sbuf(f"p8_u{i}", [128, 4 * 512], F32) for i in range(2)]
    lt = ln_tiles(c, "p8")
    ps_w = [c.psum(f"p8_psw{i}", [128, 512], F32) for i in range(4)]

    for kc in range(8):
        c.dma("gpsimd", wto[:, kc * 512:(kc + 1) * 512],
              wtoh[kc * 128:(kc + 1) * 128, :], sem="p8_w")
    c.dma("sync", g1[:], g1h, sem="p8_g")
    c.dma("sync", b1[:], b1h, sem="p8_g")
    c.dma("sync", fr2[:], frh, sem="p8_g")
    c.wait("scalar", "p8_g", 48)
    c.wait("scalar", "v_init", 1)
    c.op("scalar", "activation", om[:], fr2[:], AF.Tanh)
    c.drain("scalar")
    c.op("scalar", "mul", om[:], om[:], 0.1)
    c.drain("scalar")
    c.op("vector", "memset", halfpi[:], 1.5707963267948966)
    c.drain("vector")
    c.sem_inc("vector", "p8_hp", 1)
    c.wait("scalar", "p8_hp", 1)
    c.op("scalar", "activation", c01[:], om[:], AF.Sin, bias=halfpi[:, 0:1])
    c.drain("scalar")
    c.op("scalar", "activation", s01[:], om[:], AF.Sin)
    c.drain("scalar")
    c.op("scalar", "mul", c01[:], c01[:], 0.1)
    c.drain("scalar")
    c.op("scalar", "mul", s01[:], s01[:], 0.1)
    c.drain("scalar")
    c.op("scalar", "mul", ns01[:], s01[:], -1.0)
    c.drain("scalar")
    c.sem_inc("scalar", "p8_cs", 1)
    c.wait("vector", "p8_cs", 1)
    c.wait("vector", "p8_g", 48)
    c.wait("tensor", "p8_w", 16 * 8)
    c.wait("tensor", "v_init", 1)

    for qc in range(NTC):
        xs = x2[:, qc * TC:(qc + 1) * TC].rearrange("(kc p) t -> p kc t", p=128)
        c.dma("sync", x2f[qc % 2][:].rearrange("p (kc t) -> p kc t", kc=8), xs,
              sem="p8_x")
        c.wait("vector", "p8_x", 16 * (qc + 1))
        c.wait("scalar", "p8_x", 16 * (qc + 1))
        c.wait("tensor", "p8_x", 16 * (qc + 1))
        emit_ln_tail(
            c, "p8l", qc, x2f[qc % 2], lt, out_dst=None,
            gslice=lambda fo: g1[:, fo:fo + 1],
            bslice=lambda fo: b1[:, fo:fo + 1],
            bf_dst=lambda fo, _qc=qc: lnx[:, fo * T + _qc * 512:fo * T + (_qc + 1) * 512],
        )
        c.drain("vector")
        c.sem_inc("vector", "p8_lnq", 1)
        if qc < NTC - 1:
            c.wait("sync", "p8l_va", (qc + 1) * 8)

    for qc in range(NTC):
        c.wait("tensor", "p8_lnq", NTC)
        for m in range(4):
            if qc >= 1:
                c.wait("tensor", "p8_rot", 4 * qc)
            for kc in range(8):
                c.op("tensor", "matmul", ps_w[m][:],
                     wto[:, kc * 512 + m * 128:kc * 512 + (m + 1) * 128],
                     lnx[:, kc * T + qc * 512:kc * T + (qc + 1) * 512],
                     start=(kc == 0), stop=(kc == 7),
                     inc=("p8_pm", 1) if kc == 7 else None)
        for h in range(2):
            c.wait("scalar", "p8_pm", 4 * (qc + 1))
            c.wait("scalar", "p8_rot", 4 * qc + 2 * h)
            if qc >= 2:
                c.wait("scalar", "p8_ust", 16 * (qc - 1))
            c.op("scalar", "mul", tra[:], ps_w[h][:], c01[:, h:h + 1])
            c.drain("scalar")
            c.op("scalar", "mul", trb[:], ps_w[h][:], s01[:, h:h + 1])
            c.drain("scalar")
            c.sem_inc("scalar", "p8_ra", 1)
            c.wait("vector", "p8_ra", 2 * qc + h + 1)
            us = ustg[qc % 2]
            c.op("vector", "scalar_tensor_tensor", us[:, h:4 * 512:4],
                 ps_w[2 + h][:], ns01[:, h:h + 1], tra[:], Op.mult, Op.add)
            c.drain("vector")
            c.op("vector", "scalar_tensor_tensor", us[:, 2 + h:4 * 512:4],
                 ps_w[2 + h][:], c01[:, h:h + 1], trb[:], Op.mult, Op.add)
            c.drain("vector")
            c.sem_inc("vector", "p8_rot", 2)
        c.wait("sync", "p8_rot", 4 * (qc + 1))
        c.dma("sync", ubuf[:, qc * 2048:(qc + 1) * 2048], ustg[qc % 2][:],
              sem="p8_ust")
    c.wait("sync", "p8_ust", 16 * NTC)



def ph_scan(c):
    ubuf = c.dram("ubuf", [128, 4 * T], F32)
    ldh = c.dram("ldec2", [128, 2], F32, kind="ExternalInput")
    frh = c.dram("freq2", [128, 2], F32, kind="ExternalInput")
    spec = c.dram("spec", [4 * 128, T], BF16)

    ld2 = c.sbuf("p9_ld", [128, 2], F32)
    fr2 = c.sbuf("p9_fr", [128, 2], F32)
    dec = c.sbuf("p9_dec", [128, 2], F32)
    om = c.sbuf("p9_om", [128, 2], F32)
    cs = c.sbuf("p9_cs", [128, 2], F32)
    sn = c.sbuf("p9_sn", [128, 2], F32)
    cd2 = c.sbuf("p9_cd2", [128, 2], F32)
    sd2 = c.sbuf("p9_sd2", [128, 2], F32)
    CD4 = c.sbuf("p9_cd4", [128, 4], F32)
    SD4 = c.sbuf("p9_sd4", [128, 4], F32)
    halfpi = c.sbuf("p9_hpi", [128, 1], F32)
    ut = [c.sbuf(f"p9_u{i}", [128, 4 * 512], F32) for i in range(2)]
    hist = c.sbuf("p9_hist", [128, 4 * (T + 1)], F32)
    tmpa = c.sbuf("p9_ta", [128, 4], F32)
    tmpb = c.sbuf("p9_tb", [128, 4], F32)
    sstg = [c.sbuf(f"p9_ss{i}", [128, 4 * 512], BF16) for i in range(2)]

    c.dma("sync", ld2[:], ldh, sem="p9_g")
    c.dma("sync", fr2[:], frh, sem="p9_g")
    c.wait("scalar", "p9_g", 32)
    c.wait("scalar", "v_init", 1)
    c.op("scalar", "activation", dec[:], ld2[:], AF.Sigmoid)
    c.drain("scalar")
    c.op("scalar", "activation", om[:], fr2[:], AF.Tanh)
    c.drain("scalar")
    c.op("scalar", "mul", om[:], om[:], 0.1)
    c.drain("scalar")
    c.sem_inc("scalar", "p9_a1", 1)
    c.wait("vector", "p9_a1", 1)
    c.op("vector", "memset", halfpi[:], 1.5707963267948966)
    c.drain("vector")
    c.sem_inc("vector", "p9_v1", 1)
    c.wait("scalar", "p9_v1", 1)
    c.op("scalar", "activation", cs[:], om[:], AF.Sin, bias=halfpi[:, 0:1])
    c.drain("scalar")
    c.op("scalar", "activation", sn[:], om[:], AF.Sin)
    c.drain("scalar")
    c.sem_inc("scalar", "p9_a2", 1)
    c.wait("vector", "p9_a2", 1)
    c.op("vector", "tensor_mul", cd2[:], cs[:], dec[:])
    c.drain("vector")
    c.op("vector", "tensor_mul", sd2[:], sn[:], dec[:])
    c.drain("vector")
    c.op("vector", "tensor_copy", CD4[:, 0:2], cd2[:])
    c.op("vector", "tensor_copy", CD4[:, 2:4], cd2[:])
    c.op("vector", "tensor_copy", SD4[:, 2:4], sd2[:])
    c.drain("vector")
    c.op("vector", "tensor_scalar_mul", SD4[:, 0:2], sd2[:], -1.0)
    c.drain("vector")
    c.op("vector", "memset", hist[:, 0:4], 0.0)
    c.drain("vector")
    c.sem_inc("vector", "p9_k", 1)
    c.wait("gpsimd", "p9_k", 1)

    for qc in range(NTC):
        if qc >= 2:
            c.wait("sync", "p9_gc", qc - 1)
        c.dma("sync", ut[qc % 2][:], ubuf[:, qc * 2048:(qc + 1) * 2048], sem="p9_u")
        c.wait("gpsimd", "p9_u", 16 * (qc + 1))
        for tl in range(512):
            t = qc * 512 + tl
            prev = hist[:, 4 * t:4 * t + 4]
            prev_sw = prev.rearrange("p (b a) -> p b a", b=2)[:, ::-1, :]
            cur = hist[:, 4 * (t + 1):4 * (t + 1) + 4]
            c.op("gpsimd", "tensor_tensor",
                 tmpb[:].rearrange("p (b a) -> p b a", b=2), prev_sw,
                 SD4[:].rearrange("p (b a) -> p b a", b=2), Op.mult)
            c.op("gpsimd", "tensor_tensor", tmpa[:], prev, CD4[:], Op.mult)
            c.op("gpsimd", "tensor_add", tmpa[:], tmpa[:], tmpb[:])
            c.op("gpsimd", "tensor_add", tmpa[:], tmpa[:], ut[qc % 2][:, 4 * tl:4 * tl + 4])
            c.op("gpsimd", "tensor_scalar", cur, tmpa[:], 10.0, -10.0, Op.min, Op.max)
        c.drain("gpsimd")
        c.sem_inc("gpsimd", "p9_gc", 1)
        c.wait("vector", "p9_gc", qc + 1)
        if qc >= 2:
            c.wait("vector", "p9_st", 16 * (qc - 1))
        for j in range(4):
            hs = hist[:, 4 * (qc * 512 + 1) + j:4 * (qc * 512 + 512 + 1):4]
            c.op("vector", "tensor_copy", sstg[qc % 2][:, j * 512:(j + 1) * 512], hs)
        c.drain("vector")
        c.sem_inc("vector", "p9_cv", 1)
        c.wait("sync", "p9_cv", qc + 1)
        c.dma("sync",
              spec[:, qc * 512:(qc + 1) * 512].rearrange("(m p) t -> p m t", p=128),
              sstg[qc % 2][:].rearrange("p (m t) -> p m t", m=4), sem="p9_st")
    c.wait("sync", "p9_st", 16 * NTC)



def ph_wfrom_ffn1(c):
    spec = c.dram("spec", [4 * 128, T], BF16)
    x2 = c.dram("x2", [D, T], F32)
    wfh = c.dram("W_fromT", [2 * FR, D], F32, kind="ExternalInput")
    g3h = c.dram("ln3g_t", [128, 8], F32, kind="ExternalInput")
    b3h = c.dram("ln3b_t", [128, 8], F32, kind="ExternalInput")
    f1h = c.dram("ffn1T", [D, FH], F32, kind="ExternalInput")
    bf1h = c.dram("bffn1_t", [128, 32], F32, kind="ExternalInput")
    r3d = c.dram("r3", [D, T], F32)
    gh = c.dram("gh", [FH, T], BF16)

    wfrom = c.sbuf("pa_wf", [128, 4 * D], BF16)
    f1 = c.sbuf("pa_f1", [128, 8 * FH], BF16)
    g3 = c.sbuf("pa_g3", [128, 8], F32)
    b3 = c.sbuf("pa_b3", [128, 8], F32)
    bf1 = c.sbuf("pa_bf1", [128, 32], F32)
    spc = [c.sbuf(f"pa_sp{i}", [128, 4 * 512], BF16) for i in range(2)]
    x2f = [c.sbuf(f"pa_x2{i}", [128, 8 * 512], F32) for i in range(2)]
    rsb = c.sbuf("pa_rsb", [128, 8 * 512], F32)
    x3b = c.sbuf("pa_x3b", [128, 8 * 512], BF16)
    gstg = [c.sbuf(f"pa_g{i}", [128, 512], BF16) for i in range(2)]
    lt = ln_tiles(c, "pa")
    ps_p = [c.psum(f"pa_psp{i}", [128, 512], F32) for i in range(2)]
    ps_f = [c.psum(f"pa_psf{i}", [128, 512], F32) for i in range(2)]

    for kc in range(4):
        c.dma("gpsimd", wfrom[:, kc * D:(kc + 1) * D], wfh[kc * 128:(kc + 1) * 128, :],
              sem="pa_w")
    for kc in range(8):
        c.dma("gpsimd", f1[:, kc * FH:(kc + 1) * FH], f1h[kc * 128:(kc + 1) * 128, :],
              sem="pa_w")
    c.dma("sync", g3[:], g3h, sem="pa_g")
    c.dma("sync", b3[:], b3h, sem="pa_g")
    c.dma("sync", bf1[:], bf1h, sem="pa_g")
    c.wait("tensor", "pa_w", 16 * 12)
    c.wait("vector", "pa_g", 48)
    c.wait("scalar", "pa_g", 48)
    c.wait("tensor", "v_init", 1)

    for qc in range(NTC):
        if qc >= 2:
            c.wait("sync", "pa_pm", 8 * (qc - 1))
            c.wait("sync", "pa_vr", 8 * (qc - 1))
        ss = spec[:, qc * TC:(qc + 1) * TC].rearrange("(kc p) t -> p kc t", p=128)
        c.dma("sync", spc[qc % 2][:].rearrange("p (kc t) -> p kc t", kc=4), ss,
              sem="pa_sx")
        xs = x2[:, qc * TC:(qc + 1) * TC].rearrange("(kc p) t -> p kc t", p=128)
        c.dma("sync", x2f[qc % 2][:].rearrange("p (kc t) -> p kc t", kc=8), xs,
              sem="pa_xx")
        c.wait("tensor", "pa_sx", 16 * (qc + 1))
        c.wait("vector", "pa_xx", 16 * (qc + 1))
        for fo in range(8):
            g = qc * 8 + fo
            if g >= 2:
                c.wait("tensor", "pa_vr", g - 1)
            for kc in range(4):
                c.op("tensor", "matmul", ps_p[fo % 2][:],
                     wfrom[:, kc * D + fo * 128:kc * D + (fo + 1) * 128],
                     spc[qc % 2][:, kc * 512:(kc + 1) * 512],
                     start=(kc == 0), stop=(kc == 3),
                     inc=("pa_pm", 1) if kc == 3 else None)
            c.wait("vector", "pa_pm", g + 1)
            if g >= 2:
                c.wait("vector", "pa_r3st", 16 * (g - 1))
            c.op("vector", "scalar_tensor_tensor", rsb[:, fo * 512:(fo + 1) * 512],
                 ps_p[fo % 2][:], 0.1,
                 x2f[qc % 2][:, fo * 512:(fo + 1) * 512], Op.mult, Op.add)
            c.drain("vector")
            c.sem_inc("vector", "pa_vr", 1)
            c.wait("sync", "pa_vr", g + 1)
            c.dma("sync", r3d[:, qc * TC:(qc + 1) * TC]
                  .rearrange("(fc p) t -> fc p t", p=128)[fo],
                  rsb[:, fo * 512:(fo + 1) * 512], sem="pa_r3st")
        emit_ln_tail(
            c, "pa", qc, rsb, lt, out_dst=None,
            gslice=lambda fo: g3[:, fo:fo + 1],
            bslice=lambda fo: b3[:, fo:fo + 1],
            bf_dst=lambda fo: x3b[:, fo * 512:(fo + 1) * 512],
            rsb_ready=("pa_vr", lambda fo, _qc=qc: _qc * 8 + fo + 1),
        )
        for m in range(32):
            gg = qc * 32 + m
            if gg >= 2:
                c.wait("tensor", "pa_a", gg - 1)
            c.wait("tensor", "pa_va", (qc + 1) * 8)
            for kc in range(8):
                c.op("tensor", "matmul", ps_f[m % 2][:],
                     f1[:, kc * FH + m * 128:kc * FH + (m + 1) * 128],
                     x3b[:, kc * 512:(kc + 1) * 512],
                     start=(kc == 0), stop=(kc == 7),
                     inc=("pa_fm", 1) if kc == 7 else None)
            c.wait("scalar", "pa_fm", gg + 1)
            if gg >= 2:
                c.wait("scalar", "pa_gst", 16 * (gg - 1))
            c.op("scalar", "activation", gstg[gg % 2][:], ps_f[m % 2][:], AF.Gelu,
                 bias=bf1[:, m:m + 1], scale=1.0)
            c.drain("scalar")
            c.sem_inc("scalar", "pa_a", 1)
            c.wait("sync", "pa_a", gg + 1)
            c.dma("sync", gh[m * 128:(m + 1) * 128, qc * TC:(qc + 1) * TC],
                  gstg[gg % 2][:], sem="pa_gst")
        c.wait("vector", "pa_fm", (qc + 1) * 32)
    c.wait("sync", "pa_gst", 16 * 256)
    c.wait("sync", "pa_r3st", 16 * 64)



def ph_ffn2ln(c):
    gh = c.dram("gh", [FH, T], BF16)
    x2 = c.dram("x2", [D, T], F32)
    r3d = c.dram("r3", [D, T], F32)
    f2h = c.dram("ffn2T", [FH, D], F32, kind="ExternalInput")
    bf2h = c.dram("bffn2_t", [128, 8], F32, kind="ExternalInput")
    x4 = c.dram("x4", [D, T], F32)

    f2 = c.sbuf("pb_f2", [128, 32 * D], BF16)
    bf2 = c.sbuf("pb_bf2", [128, 8], F32)
    b2s = c.sbuf("pb_b2s", [128, 8], F32)
    ght = [c.sbuf("pb_gh0", [128, 32 * 512], BF16)] * 2
    x2f = [c.sbuf(f"pb_x2{i}", [128, 8 * 512], F32) for i in range(2)]
    r3f = [c.sbuf(f"pb_r3{i}", [128, 8 * 512], F32) for i in range(2)]
    rsb = c.sbuf("pb_rsb", [128, 8 * 512], F32)
    tps = c.sbuf("pb_tps", [128, 512], F32)
    lt = ln_tiles(c, "pb")
    ps_p = [c.psum(f"pb_psp{i}", [128, 512], F32) for i in range(2)]

    for kc in range(32):
        c.dma("gpsimd", f2[:, kc * D:(kc + 1) * D], f2h[kc * 128:(kc + 1) * 128, :],
              sem="pb_w")
    c.dma("sync", bf2[:], bf2h, sem="pb_g")
    c.wait("scalar", "pb_g", 16)
    c.wait("scalar", "v_init", 1)
    c.op("scalar", "mul", b2s[:], bf2[:], 0.1)
    c.drain("scalar")
    c.sem_inc("scalar", "pb_b", 1)
    c.wait("tensor", "pb_w", 16 * 32)
    c.wait("vector", "pb_b", 1)
    c.wait("tensor", "v_init", 1)

    for qc in range(NTC):
        if qc >= 1:
            c.wait("sync", "pb_pm", 8 * qc)
        if qc >= 2:
            c.wait("sync", "pb_vr", 8 * (qc - 1))
        gs = gh[:, qc * TC:(qc + 1) * TC].rearrange("(kc p) t -> p kc t", p=128)
        c.dma("sync", ght[qc % 2][:].rearrange("p (kc t) -> p kc t", kc=32), gs,
              sem="pb_gx")
        xs = x2[:, qc * TC:(qc + 1) * TC].rearrange("(kc p) t -> p kc t", p=128)
        c.dma("sync", x2f[qc % 2][:].rearrange("p (kc t) -> p kc t", kc=8), xs,
              sem="pb_xx")
        rs = r3d[:, qc * TC:(qc + 1) * TC].rearrange("(kc p) t -> p kc t", p=128)
        c.dma("sync", r3f[qc % 2][:].rearrange("p (kc t) -> p kc t", kc=8), rs,
              sem="pb_rx")
        c.wait("tensor", "pb_gx", 16 * (qc + 1))
        c.wait("vector", "pb_xx", 16 * (qc + 1))
        c.wait("vector", "pb_rx", 16 * (qc + 1))
        for fo in range(8):
            g = qc * 8 + fo
            if g >= 2:
                c.wait("tensor", "pb_vr", g - 1)
            for kc in range(32):
                c.op("tensor", "matmul", ps_p[fo % 2][:],
                     f2[:, kc * D + fo * 128:kc * D + (fo + 1) * 128],
                     ght[qc % 2][:, kc * 512:(kc + 1) * 512],
                     start=(kc == 0), stop=(kc == 31),
                     inc=("pb_pm", 1) if kc == 31 else None)
            c.wait("vector", "pb_pm", g + 1)
            c.op("vector", "scalar_tensor_tensor", tps[:],
                 ps_p[fo % 2][:], 0.1,
                 r3f[qc % 2][:, fo * 512:(fo + 1) * 512], Op.mult, Op.add)
            c.drain("vector")
            c.op("vector", "scalar_tensor_tensor", rsb[:, fo * 512:(fo + 1) * 512],
                 tps[:], b2s[:, fo:fo + 1],
                 x2f[qc % 2][:, fo * 512:(fo + 1) * 512], Op.add, Op.add)
            c.drain("vector")
            c.sem_inc("vector", "pb_vr", 1)
        emit_ln_tail(
            c, "pb", qc, rsb, lt,
            out_dst=lambda fo, _qc=qc: x4[:, _qc * TC:(_qc + 1) * TC]
                .rearrange("(fc p) t -> fc p t", p=128)[fo],
            rsb_ready=("pb_vr", lambda fo, _qc=qc: _qc * 8 + fo + 1),
        )
    c.wait("sync", "pb_st", 16 * 64)



def ph_l1ffn(c, upto="full"):
    x4 = c.dram("x4", [D, T], F32)
    kh = c.dram("L1_ffn_kernel", [D, D], F32, kind="ExternalInput")
    ah = c.dram("L1_ffn_a", [D, 16], F32, kind="ExternalInput")
    bh = c.dram("L1_ffn_b", [16, D], F32, kind="ExternalInput")
    bih = c.dram("bias_t", [128, 8], F32, kind="ExternalInput")
    x5 = c.dram("x5", [D, T], F32)

    kf = c.sbuf("pc_kf", [128, 8 * D], F32)
    wq = c.sbuf("pc_wq", [128, 8 * D], BF16)
    asb = c.sbuf("pc_a", [128, 8 * 16], BF16)
    bsb = c.sbuf("pc_b", [16, D], BF16)
    bi = c.sbuf("pc_bi", [128, 8], F32)
    amax = c.sbuf("pc_amax", [1, 1], F32)
    srow = c.sbuf("pc_srow", [1, 1], F32)
    rrow = c.sbuf("pc_rrow", [1, 1], F32)
    sc128 = c.sbuf("pc_sc", [128, 1], F32)
    rc128 = c.sbuf("pc_rc", [128, 1], F32)
    qt = c.sbuf("pc_qt", [128, D], F32)
    x4b = [c.sbuf(f"pc_x4b{i}", [128, 8 * 512], BF16) for i in range(2)]
    x4f = [c.sbuf(f"pc_x4f{i}", [128, 8 * 512], F32) for i in range(2)]
    t1 = c.sbuf("pc_t1", [16, 512], BF16)
    rsb = c.sbuf("pc_rsb", [128, 8 * 512], F32)
    lt = ln_tiles(c, "pc")
    ps_p = [c.psum(f"pc_psp{i}", [128, 512], F32) for i in range(2)]
    ps_l = c.psum("pc_psl", [128, 512], F32)
    onesr = None

    for kc in range(8):
        c.dma("sync", kf[:, kc * D:(kc + 1) * D], kh[kc * 128:(kc + 1) * 128, :],
              sem="pc_k")
    c.dma("gpsimd", asb[:].rearrange("p (kc r) -> p kc r", kc=8),
          ah.rearrange("(kc p) r -> p kc r", p=128), sem="pc_k")
    c.dma("gpsimd", bsb[:], bh, sem="pc_k")
    c.dma("sync", bi[:], bih, sem="pc_k")
    c.wait("gpsimd", "pc_k", 16 * 11)
    c.op("gpsimd", "tensor_reduce", amax[:], kf[:], AX.XYZWC, Op.abs_max)
    c.drain("gpsimd")
    c.sem_inc("gpsimd", "pc_am", 1)
    c.wait("scalar", "pc_am", 1)
    c.op("scalar", "mul", srow[:], amax[:], 1.0 / 7.0)
    c.drain("scalar")
    c.sem_inc("scalar", "pc_s", 1)
    c.wait("vector", "pc_s", 1)
    c.op("vector", "reciprocal", rrow[:], srow[:])
    c.drain("vector")
    c.sem_inc("vector", "pc_r", 1)
    c.wait("tensor", "v_init", 1)
    c.wait("tensor", "pc_r", 1)
    onesr = c.tiles["c_onesr"]
    c.op("tensor", "matmul", ps_l[:, 0:1], onesr[:], srow[:], start=True, stop=True)
    c.op("tensor", "matmul", ps_l[:, 1:2], onesr[:], rrow[:], start=True, stop=True,
         inc=("pc_pb", 1))
    c.wait("vector", "pc_pb", 1)
    c.op("vector", "tensor_copy", sc128[:], ps_l[:, 0:1])
    c.op("vector", "tensor_copy", rc128[:], ps_l[:, 1:2])
    c.drain("vector")
    for kc in range(8):
        c.op("vector", "tensor_scalar", qt[:], kf[:, kc * D:(kc + 1) * D],
             rc128[:, 0:1], None, Op.mult)
        c.drain("vector")
        c.op("vector", "tensor_scalar", qt[:], qt[:], 12582912.0, None, Op.add)
        c.drain("vector")
        c.op("vector", "tensor_scalar", qt[:], qt[:], 12582912.0, None, Op.subtract)
        c.drain("vector")
        c.op("vector", "tensor_scalar", qt[:], qt[:], 7.0, -8.0, Op.min, Op.max)
        c.drain("vector")
        c.op("vector", "tensor_scalar", wq[:, kc * D:(kc + 1) * D], qt[:],
             sc128[:, 0:1], None, Op.mult)
        c.drain("vector")
    c.sem_inc("vector", "pc_q", 1)
    c.wait("tensor", "pc_q", 1)
    if upto == "quant":
        wqd = c.dram("wq_dbg", [D, D], BF16, kind="ExternalOutput")
        dbg = c.dram("dbg", [128, 8], F32, kind="ExternalOutput")
        c.wait("sync", "pc_q", 1)
        for kc in range(8):
            c.dma("sync", wqd[kc * 128:(kc + 1) * 128, :], wq[:, kc * D:(kc + 1) * D],
                  sem="pc_dbg")
        dbt = c.sbuf("pc_dbt", [128, 8], F32)
        c.op("vector", "memset", dbt[:], 0.0)
        c.drain("vector")
        c.op("vector", "tensor_copy", dbt[0:1, 0:1], amax[:])
        c.op("vector", "tensor_copy", dbt[0:1, 1:2], srow[:])
        c.op("vector", "tensor_copy", dbt[0:1, 2:3], rrow[:])
        c.op("vector", "tensor_copy", dbt[:, 3:4], sc128[:])
        c.op("vector", "tensor_copy", dbt[:, 4:5], rc128[:])
        c.drain("vector")
        c.sem_inc("vector", "pc_dv", 1)
        c.wait("sync", "pc_dv", 1)
        c.dma("sync", dbg, dbt[:], sem="pc_dbg")
        c.wait("sync", "pc_dbg", 16 * 9)
        return

    for qc in range(NTC):
        if qc >= 2:
            c.wait("gpsimd", "pc_pm", 8 * (qc - 1))
            c.wait("sync", "pc_pm", 8 * (qc - 1))
        xs = x4[:, qc * TC:(qc + 1) * TC].rearrange("(kc p) t -> p kc t", p=128)
        c.dma("gpsimd", x4b[qc % 2][:].rearrange("p (kc t) -> p kc t", kc=8), xs,
              sem="pc_xb")
        c.dma("sync", x4f[qc % 2][:].rearrange("p (kc t) -> p kc t", kc=8), xs,
              sem="pc_xf")
        c.wait("tensor", "pc_xb", 16 * (qc + 1))
        c.wait("vector", "pc_xf", 16 * (qc + 1))
        if qc >= 1:
            c.wait("tensor", "pc_l1v", qc)
        for kc in range(8):
            c.op("tensor", "matmul", ps_l[0:16, 0:512],
                 asb[:, kc * 16:(kc + 1) * 16],
                 x4b[qc % 2][:, kc * 512:(kc + 1) * 512],
                 start=(kc == 0), stop=(kc == 7),
                 inc=("pc_l1", 1) if kc == 7 else None)
        c.wait("vector", "pc_l1", qc + 1)
        if qc >= 1:
            c.wait("vector", "pc_pm", 8 * qc)
        c.op("vector", "tensor_copy", t1[:], ps_l[0:16, 0:512])
        c.drain("vector")
        c.sem_inc("vector", "pc_l1v", 1)
        c.wait("tensor", "pc_l1v", qc + 1)
        if upto == "lora1":
            continue
        for fo in range(8):
            g = qc * 8 + fo
            if g >= 2:
                c.wait("tensor", "pc_vr", g - 1)
            for kc in range(8):
                c.op("tensor", "matmul", ps_p[fo % 2][:],
                     wq[:, kc * D + fo * 128:kc * D + (fo + 1) * 128],
                     x4b[qc % 2][:, kc * 512:(kc + 1) * 512],
                     start=(kc == 0), stop=False)
            c.op("tensor", "matmul", ps_p[fo % 2][:],
                 bsb[:, fo * 128:(fo + 1) * 128], t1[:],
                 start=False, stop=True, inc=("pc_pm", 1))
            c.wait("vector", "pc_pm", g + 1)
            c.op("vector", "scalar_tensor_tensor", rsb[:, fo * 512:(fo + 1) * 512],
                 ps_p[fo % 2][:], bi[:, fo:fo + 1],
                 x4f[qc % 2][:, fo * 512:(fo + 1) * 512], Op.add, Op.add)
            c.drain("vector")
            c.sem_inc("vector", "pc_vr", 1)
        emit_ln_tail(
            c, "pc", qc, rsb, lt,
            out_dst=lambda fo, _qc=qc: x5[:, _qc * TC:(_qc + 1) * TC]
                .rearrange("(fc p) t -> fc p t", p=128)[fo],
            rsb_ready=("pc_vr", lambda fo, _qc=qc: _qc * 8 + fo + 1),
        )
    if upto == "full":
        c.wait("sync", "pc_st", 16 * 64)



def ph_outT(c):
    x5 = c.dram("x5", [D, T], F32)
    out = c.dram("y_out", [T, D], F32, kind="ExternalOutput")

    idn = c.tiles["c_idnf"]
    x5f = [c.sbuf(f"pd_x5{i}", [128, 8 * 512], F32) for i in range(2)]
    ostg = [c.sbuf(f"pd_o{i}", [128, D], F32) for i in range(2)]
    ps_t = [c.psum(f"pd_ps{i}", [128, 128], F32) for i in range(2)]

    c.wait("tensor", "v_init", 1)
    j = 0
    gi = 0
    for qc in range(NTC):
        if qc >= 2:
            c.wait("sync", "pd_tr", 32 * (qc - 1))
        xs = x5[:, qc * TC:(qc + 1) * TC].rearrange("(kc p) t -> p kc t", p=128)
        c.dma("sync", x5f[qc % 2][:].rearrange("p (kc t) -> p kc t", kc=8), xs,
              sem="pd_x")
        c.wait("tensor", "pd_x", 16 * (qc + 1))
        for i in range(4):
            for fc in range(8):
                if j >= 2:
                    c.wait("tensor", "pd_v", j - 1)
                c.op("tensor", "transpose", ps_t[j % 2][:],
                     x5f[qc % 2][:, fc * 512 + i * 128:fc * 512 + (i + 1) * 128],
                     idn[:], inc=("pd_tr", 1))
                c.wait("vector", "pd_tr", j + 1)
                if gi >= 2 and fc == 0:
                    c.wait("vector", "pd_st", 16 * (gi - 1))
                c.op("vector", "tensor_copy",
                     ostg[gi % 2][:, fc * 128:(fc + 1) * 128], ps_t[j % 2][:])
                c.drain("vector")
                c.sem_inc("vector", "pd_v", 1)
                j += 1
            c.wait("sync", "pd_v", j)
            c.dma("sync", out[qc * TC + i * 128:qc * TC + (i + 1) * 128, :],
                  ostg[gi % 2][:], sem="pd_st")
            gi += 1
    c.wait("sync", "pd_st", 16 * 32)



def build_program(phase_fns, kinds=None):
    nc = bass.Bass("TRN2", target_bir_lowering=False)
    es = contextlib.ExitStack()
    c = Ctx(nc, es, kinds)
    setup_common(c)
    for fn in phase_fns:
        fn(c)
    block = es.enter_context(nc.Block())
    finalize(c, block)
    es.close()
    return nc


def host_layouts(inputs):
    f32 = np.float32
    g = {k: np.asarray(v, f32) for k, v in inputs.items() if k != "L0_attn_x"}
    out = {}
    out["L0_attn_w_qkv"] = np.ascontiguousarray(g["L0_attn_w_qkv"])
    out["b_qkv_t"] = np.ascontiguousarray(g["L0_attn_b_qkv"].reshape(24, 128).T)
    out["b_qkv_r"] = np.ascontiguousarray(g["L0_attn_b_qkv"].reshape(1, 3 * D))
    out["L0_attn_w_proj"] = np.ascontiguousarray(g["L0_attn_w_proj"])
    out["b_proj_t"] = np.ascontiguousarray(g["L0_attn_b_proj"].reshape(8, 128).T)
    out["w1T"] = np.ascontiguousarray(g["L0_ffn_w1_w"].T)
    out["w2T"] = np.ascontiguousarray(g["L0_ffn_w2_w"].T)
    out["w1s_t"] = np.ascontiguousarray(g["L0_ffn_w1_s"].reshape(32, 128).T)
    out["w2s_t"] = np.ascontiguousarray(g["L0_ffn_w2_s"].reshape(32, 128).T)
    out["h1b"] = np.ascontiguousarray(
        g["L0_ffn_h1_w"].T.reshape(32, 128, 32, 128).transpose(2, 0, 1, 3))
    out["h2b"] = np.ascontiguousarray(
        g["L0_ffn_h2_w"].T.reshape(32, 128, 32, 128).transpose(2, 0, 1, 3))
    out["h1s_t"] = np.ascontiguousarray(g["L0_ffn_h1_s"].reshape(32, 128).T)
    out["h2s_t"] = np.ascontiguousarray(g["L0_ffn_h2_s"].reshape(32, 128).T)
    out["w3T"] = np.ascontiguousarray(g["L0_ffn_w3_w"].T)
    out["w3s_t"] = np.ascontiguousarray(g["L0_ffn_w3_s"].reshape(8, 128).T)
    out["W_toT"] = np.ascontiguousarray(g["L1_attn_W_to"].T)
    out["W_fromT"] = np.ascontiguousarray(g["L1_attn_W_from"].T)
    out["freq2"] = np.ascontiguousarray(g["L1_attn_frequencies"].reshape(2, 128).T)
    out["ldec2"] = np.ascontiguousarray(g["L1_attn_log_decay"].reshape(2, 128).T)
    out["ln1g_t"] = np.ascontiguousarray(g["L1_attn_ln1_g"].reshape(8, 128).T)
    out["ln1b_t"] = np.ascontiguousarray(g["L1_attn_ln1_b"].reshape(8, 128).T)
    out["ln3g_t"] = np.ascontiguousarray(g["L1_attn_ln3_g"].reshape(8, 128).T)
    out["ln3b_t"] = np.ascontiguousarray(g["L1_attn_ln3_b"].reshape(8, 128).T)
    out["ffn1T"] = np.ascontiguousarray(g["L1_attn_W_ffn1"].T)
    out["bffn1_t"] = np.ascontiguousarray(g["L1_attn_b_ffn1"].reshape(32, 128).T)
    out["ffn2T"] = np.ascontiguousarray(g["L1_attn_W_ffn2"].T)
    out["bffn2_t"] = np.ascontiguousarray(g["L1_attn_b_ffn2"].reshape(8, 128).T)
    out["L1_ffn_kernel"] = np.ascontiguousarray(g["L1_ffn_kernel"])
    out["L1_ffn_a"] = np.ascontiguousarray(g["L1_ffn_a"])
    out["L1_ffn_b"] = np.ascontiguousarray(g["L1_ffn_b"])
    out["bias_t"] = np.ascontiguousarray(g["L1_ffn_bias"].reshape(8, 128).T)
    return out


PHASES = [
    ("qkv", ph_qkv, [], ["Qf", "Kf", "Vt"]),
    ("attn", ph_attn, ["Qf", "Kf", "Vt"], ["attnf"]),
    ("proj", ph_proj, ["attnf"], ["x1"]),
    ("ffnA", ph_ffnA, ["x1"], ["hA"]),
    ("h1", ph_h1, ["hA"], ["hB"]),
    ("h2", ph_h2, ["hB"], ["hC"]),
    ("w3ln", ph_w3ln, ["hC", "x1"], ["x2"]),
    ("lnwto", ph_lnwto, ["x2"], ["ubuf"]),
    ("scan", ph_scan, ["ubuf"], ["spec"]),
    ("wfrom", ph_wfrom_ffn1, ["spec", "x2"], ["r3", "gh"]),
    ("ffn2", ph_ffn2ln, ["gh", "x2", "r3"], ["x4"]),
    ("l1ffn", ph_l1ffn, ["x4"], ["x5"]),
    ("outT", ph_outT, ["x5"], ["y_out"]),
]

PHASE_EXT = {
    "qkv": ["x_in", "L0_attn_w_qkv", "b_qkv_t", "b_qkv_r"],
    "attn": [],
    "proj": ["x_in", "L0_attn_w_proj", "b_proj_t"],
    "ffnA": ["w1T", "w2T", "w1s_t", "w2s_t"],
    "h1": ["h1b", "h1s_t"],
    "h2": ["h2b", "h2s_t"],
    "w3ln": ["w3T", "w3s_t"],
    "lnwto": ["W_toT", "ln1g_t", "ln1b_t", "freq2"],
    "scan": ["ldec2", "freq2"],
    "wfrom": ["W_fromT", "ln3g_t", "ln3b_t", "ffn1T", "bffn1_t"],
    "ffn2": ["ffn2T", "bffn2_t"],
    "l1ffn": ["L1_ffn_kernel", "L1_ffn_a", "L1_ffn_b", "bias_t"],
    "outT": [],
}


def kernel(**inputs):
    xfull = np.ascontiguousarray(np.asarray(inputs["L0_attn_x"], np.float32))
    hl = host_layouts(inputs)
    xs = [np.ascontiguousarray(xfull[b]) for b in range(NCORES)]

    state = [dict() for _ in range(NCORES)]
    for b in range(NCORES):
        state[b]["x_in"] = xs[b]

    for name, fn, ins, outs in PHASES:
        kinds = {k: "ExternalInput" for k in ins}
        kinds.update({k: "ExternalOutput" for k in outs})
        nc = build_program([fn], kinds)
        in_maps = []
        for b in range(NCORES):
            m = {}
            for k in PHASE_EXT[name]:
                m[k] = state[b][k] if k in state[b] else hl[k]
            for k in ins:
                m[k] = state[b][k]
            in_maps.append(m)
        res = run_bass_kernel_spmd(nc, in_maps, core_ids=list(range(NCORES)))
        for b in range(NCORES):
            for k in outs:
                state[b][k] = res.results[b][k]

    out = np.stack([state[b]["y_out"] for b in range(NCORES)], axis=0)
    return out.astype(np.float32)


# version 03
# speedup vs baseline: 1.0032x; 1.0032x over previous
import contextlib
import numpy as np
import ml_dtypes
import concourse.bass as bass
import concourse.mybir as mybir
from concourse.bass_utils import run_bass_kernel_spmd
from concourse.alu_op_type import AluOpType as Op

F32 = mybir.dt.float32
BF16 = mybir.dt.bfloat16
AF = mybir.ActivationFunctionType
AX = mybir.AxisListType

T = 4096
D = 1024
H = 16
HD = 64
FH = 4096
FR = 256
NCORES = 8
TC = 512
NTC = T // TC
ENGINES = ("sync", "vector", "scalar", "tensor", "gpsimd")


ARENA_F32 = 42000
GLOBAL_SEMS = ("g_idn", "v_init", "bar", "bar2")


class Ctx:
    def __init__(self, nc, es, kinds=None, fused=False):
        self.nc = nc
        self.es = es
        self.kinds = kinds or {}
        self.fused = fused
        self.prog = {e: [] for e in ENGINES}
        self.hbm = {}
        self.tiles = {}
        self.psums = {}
        self.sems = {}
        self.cnt = {}
        if fused:
            self.arena = es.enter_context(nc.sbuf_tensor("ARENA", [128, ARENA_F32], F32))
            self.parena = es.enter_context(nc.psum_tensor("PARENA", [128, 4096], F32))
            self.sp = 0
            self.pp = 0
            self.sem_pool = [es.enter_context(nc.semaphore(f"S{i}")) for i in range(90)]
            self.sem_rr = 0
            self.sem_phys = {}
            self.sem_base = {}
            self.phys_cnt = [0] * 90
            self.phase_idx = 0

    def dram(self, name, shape, dtype, kind="Internal"):
        kind = self.kinds.get(name, kind)
        if name not in self.hbm:
            self.hbm[name] = self.nc.dram_tensor(name, list(shape), dtype, kind=kind).ap()
        return self.hbm[name]

    def sbuf(self, name, shape, dtype):
        if name in self.tiles:
            return self.tiles[name]
        if self.fused and not name.startswith("c_"):
            rows, n = shape
            nf = n if dtype == F32 else (n + 1) // 2
            assert self.sp + nf <= ARENA_F32, f"SBUF arena overflow at {name} ({self.sp}+{nf})"
            v = self.arena[:, self.sp:self.sp + nf]
            self.sp += nf
            if dtype != F32:
                v = v.bitcast(dtype)[:, 0:n]
            if rows != 128:
                v = v[0:rows, :]
            self.tiles[name] = v
        else:
            self.tiles[name] = self.es.enter_context(
                self.nc.sbuf_tensor(name, list(shape), dtype))
        return self.tiles[name]

    def psum(self, name, shape, dtype=F32):
        if name in self.psums:
            return self.psums[name]
        if self.fused:
            rows, n = shape
            assert dtype == F32
            nal = ((n + 511) // 512) * 512
            assert self.pp + nal <= 4096, f"PSUM arena overflow at {name}"
            v = self.parena[:, self.pp:self.pp + n]
            self.pp += nal
            if rows != 128:
                v = v[0:rows, :]
            self.psums[name] = v
        else:
            self.psums[name] = self.es.enter_context(
                self.nc.psum_tensor(name, list(shape), dtype))
        return self.psums[name]

    def sem(self, name):
        if name not in self.sems:
            if self.fused:
                if name in GLOBAL_SEMS:
                    idx = 86 + GLOBAL_SEMS.index(name)
                else:
                    idx = self.sem_rr % 86
                    self.sem_rr += 1
                self.sem_phys[name] = idx
                self.sem_base[name] = self.phys_cnt[idx]
                self.sems[name] = self.sem_pool[idx]
            else:
                self.sems[name] = self.es.enter_context(self.nc.semaphore(name))
            self.cnt[name] = 0
        return self.sems[name]

    def end_phase(self):
        assert self.fused
        pidx = self.phase_idx
        for e in ENGINES:
            if e in ("vector", "scalar", "gpsimd"):
                self.drain(e)
            if e != "tensor":
                self.sem_inc(e, "bar", 1)
        for e in ENGINES:
            self.wait(e, "bar", 4 * (pidx + 1))
        for name in [n for n in self.sems if n not in GLOBAL_SEMS]:
            assert self.phys_cnt[self.sem_phys[name]] < 60000, f"sem value overflow risk {name}"
            del self.sems[name]
            del self.cnt[name]
            del self.sem_base[name]
            del self.sem_phys[name]
        for k in [k for k in self.tiles if not k.startswith("c_")]:
            del self.tiles[k]
        self.psums = {}
        self.sp = 0
        self.pp = 0
        self.phase_idx += 1

    def inc(self, name, n=1):
        self.sem(name)
        self.cnt[name] += n
        if self.fused:
            self.phys_cnt[self.sem_phys[name]] += n
        return self.cnt[name]

    def val(self, name):
        self.sem(name)
        return self.cnt[name]

    def emit(self, engine, f):
        self.prog[engine].append(f)

    def wait(self, engine, semname, value):
        s = self.sem(semname)
        if self.fused:
            value = value + self.sem_base[semname]
        self.emit(engine, lambda eng, _s=s, _v=value: eng.wait_ge(_s, _v))

    def op(self, engine, fname, *args, inc=None, **kw):
        if inc is not None:
            tot = self.inc(inc[0], inc[1])
            s = self.sems[inc[0]]

            def f(eng, _f=fname, _a=args, _k=kw, _s=s, _n=inc[1]):
                getattr(eng, _f)(*_a, **_k).then_inc(_s, _n)
            self.emit(engine, f)
            return tot

        def f(eng, _f=fname, _a=args, _k=kw):
            getattr(eng, _f)(*_a, **_k)
        self.emit(engine, f)
        return None

    def dma(self, engine, dst, src, sem="ld", **kw):
        return self.op(engine, "dma_start", inc=(sem, 16), out=dst, in_=src, **kw)

    def drain(self, engine):
        self.emit(engine, lambda eng: eng.drain())

    def sem_inc(self, engine, semname, n=1):
        tot = self.inc(semname, n)
        s = self.sems[semname]
        assert engine != "tensor", "standalone sem_inc on PE is device-fatal"
        self.emit(engine, lambda eng, _s=s, _n=n: eng.sem_inc(_s, _n))
        return tot


def finalize(ctx, block):
    @block.sync
    def _(eng):
        for f in ctx.prog["sync"]:
            f(eng)

    @block.vector
    def _(eng):
        for f in ctx.prog["vector"]:
            f(eng)

    @block.scalar
    def _(eng):
        for f in ctx.prog["scalar"]:
            f(eng)

    @block.tensor
    def _(eng):
        for f in ctx.prog["tensor"]:
            f(eng)

    @block.gpsimd
    def _(eng):
        for f in ctx.prog["gpsimd"]:
            f(eng)


def setup_common(c):
    iot = c.sbuf("c_iot", [128, 128], F32)
    idn = c.sbuf("c_idnf", [128, 128], F32)
    idnb = c.sbuf("c_idnb", [128, 128], BF16)
    onesr = c.sbuf("c_onesr", [1, 128], F32)
    ones128 = c.sbuf("c_ones128", [128, 1], F32)
    sclr = c.sbuf("c_sclr", [1, 128], F32)
    c.op("gpsimd", "iota", iot[:], pattern=[[1, 128]], base=0,
         channel_multiplier=-1, allow_small_or_imprecise_dtypes=True, inc=("g_idn", 1))
    c.wait("vector", "g_idn", 1)
    c.op("vector", "tensor_scalar", idn[:], iot[:], 0.0, None, Op.is_equal)
    c.op("vector", "tensor_scalar", idnb[:], iot[:], 0.0, None, Op.is_equal)
    c.op("vector", "memset", onesr[:], 1.0)
    c.op("vector", "memset", ones128[:], 1.0)
    c.op("vector", "memset", sclr[:], -0.125)
    eps = c.sbuf("c_eps", [128, 1], F32)
    c.op("vector", "memset", eps[:], 1e-5)
    c.drain("vector")
    c.sem_inc("vector", "v_init", 1)



def ph_qkv(c, scratch_kind="ExternalOutput"):
    x = c.dram("x_in", [T, D], F32, kind="ExternalInput")
    wq = c.dram("L0_attn_w_qkv", [D, 3 * D], F32, kind="ExternalInput")
    bqt_h = c.dram("b_qkv_t", [128, 24], F32, kind="ExternalInput")
    bqr_h = c.dram("b_qkv_r", [1, 3 * D], F32, kind="ExternalInput")
    Qf = c.dram("Qf", [D, T], BF16, kind=scratch_kind)
    Kf = c.dram("Kf", [D, T], BF16, kind=scratch_kind)
    Vt = c.dram("Vt", [T, D], BF16, kind=scratch_kind)

    idn = c.tiles["c_idnf"]
    onesr = c.tiles["c_onesr"]
    wsb = c.sbuf("q_w", [128, 8 * 3072], BF16)
    bqt = c.sbuf("q_bt", [128, 24], F32)
    bvrow = c.sbuf("q_bvrow", [1, D], F32)
    vbias = c.sbuf("q_vbias", [128, D], F32)
    xtok = [c.sbuf(f"q_xtok{i}", [128, 4 * D], F32) for i in range(2)]
    xfm = c.sbuf("q_xfm", [128, 8 * TC], BF16)
    stg = [c.sbuf(f"q_stg{i}", [128, TC], BF16) for i in range(2)]
    vstg = [c.sbuf(f"q_vstg{i}", [128, D], BF16) for i in range(2)]

    ps_tr = [c.psum(f"q_pstr{i}", [128, 128], F32) for i in range(2)]
    ps_mm = [c.psum(f"q_psmm{i}", [128, TC], F32) for i in range(2)]
    ps_v = [c.psum(f"q_psv{i}", [128, D], F32) for i in range(2)]

    for kc in range(8):
        c.dma("gpsimd", wsb[:, kc * 3072:(kc + 1) * 3072],
              wq[kc * 128:(kc + 1) * 128, :], sem="q_w")
    c.dma("sync", bqt[:], bqt_h, sem="q_b")
    c.dma("sync", bvrow[:], bqr_h[:, 2 * D:3 * D], sem="q_br")

    c.wait("tensor", "v_init", 1)
    c.wait("tensor", "q_br", 16)
    for half in range(2):
        c.op("tensor", "matmul", ps_v[0][:, half * 512:(half + 1) * 512],
             onesr[:], bvrow[:, half * 512:(half + 1) * 512],
             start=True, stop=True, inc=("q_pev", 1) if half == 1 else None)
    c.wait("vector", "q_pev", 1)
    c.op("vector", "tensor_copy", vbias[:], ps_v[0][:])
    c.drain("vector")
    c.sem_inc("vector", "q_vv", 1)

    c.wait("scalar", "q_b", 16)
    c.wait("tensor", "q_w", 16 * 8)

    jtr = 0
    gm = 0
    gv = 0
    for tc in range(NTC):
        if tc >= 2:
            c.wait("sync", "q_pet", 32 * (tc - 1))
        src = x[tc * TC:(tc + 1) * TC, :].rearrange("(i p) f -> p i f", p=128)
        c.dma("sync", xtok[tc % 2][:].rearrange("p (i f) -> p i f", i=4), src, sem="q_x")

        c.wait("tensor", "q_x", 16 * (tc + 1))
        for i in range(4):
            for fc in range(8):
                if jtr >= 2:
                    c.wait("tensor", "q_vx", jtr - 1)
                c.op("tensor", "transpose", ps_tr[jtr % 2][:],
                     xtok[tc % 2][:, i * D + fc * 128:i * D + (fc + 1) * 128],
                     idn[:], inc=("q_pet", 1))
                c.wait("vector", "q_pet", jtr + 1)
                c.op("vector", "tensor_copy",
                     xfm[:, fc * TC + i * 128:fc * TC + (i + 1) * 128],
                     ps_tr[jtr % 2][:])
                c.drain("vector")
                c.sem_inc("vector", "q_vx", 1)
                jtr += 1

        for m in range(16):
            if gm >= 2:
                c.wait("tensor", "q_a", gm - 1)
            for kc in range(8):
                c.op("tensor", "matmul", ps_mm[gm % 2][:],
                     wsb[:, kc * 3072 + m * 128:kc * 3072 + (m + 1) * 128],
                     xfm[:, kc * TC:(kc + 1) * TC],
                     start=(kc == 0), stop=(kc == 7),
                     inc=("q_pem", 1) if kc == 7 else None)
            c.wait("scalar", "q_pem", gm + 1)
            if gm >= 2:
                c.wait("scalar", "q_st", 16 * (gm - 1))
            c.op("scalar", "add", stg[gm % 2][:], ps_mm[gm % 2][:],
                 bqt[:, m:m + 1])
            c.drain("scalar")
            c.sem_inc("scalar", "q_a", 1)
            c.wait("sync", "q_a", gm + 1)
            dst = (Qf if m < 8 else Kf)
            mm = m if m < 8 else m - 8
            c.dma("sync", dst[mm * 128:(mm + 1) * 128, tc * TC:(tc + 1) * TC],
                  stg[gm % 2][:], sem="q_st")
            gm += 1

        for i in range(4):
            if gv >= 2:
                c.wait("tensor", "q_vv", 1 + (gv - 1))
            for kc in range(8):
                for half in range(2):
                    c.op("tensor", "matmul",
                         ps_v[gv % 2][:, half * 512:(half + 1) * 512],
                         xfm[:, kc * TC + i * 128:kc * TC + (i + 1) * 128],
                         wsb[:, kc * 3072 + 2048 + half * 512:kc * 3072 + 2048 + (half + 1) * 512],
                         start=(kc == 0), stop=(kc == 7),
                         inc=("q_pev", 1) if (kc == 7 and half == 1) else None)
            c.wait("vector", "q_pev", 1 + gv + 1)
            if gv >= 2:
                c.wait("vector", "q_stv", 16 * (gv - 1))
            c.op("vector", "tensor_tensor", vstg[gv % 2][:], ps_v[gv % 2][:],
                 vbias[:], Op.add)
            c.drain("vector")
            c.sem_inc("vector", "q_vv", 1)
            c.wait("sync", "q_vv", 1 + gv + 1)
            c.dma("sync", Vt[tc * TC + i * 128:tc * TC + (i + 1) * 128, :],
                  vstg[gv % 2][:], sem="q_stv")
            gv += 1

    c.wait("sync", "q_st", 16 * 128)
    c.wait("sync", "q_stv", 16 * 32)



def ph_attn(c):
    Qf = c.dram("Qf", [D, T], BF16)
    Kf = c.dram("Kf", [D, T], BF16)
    Vt = c.dram("Vt", [T, D], BF16)
    attnf = c.dram("attnf", [D, T], BF16)

    onesr = c.tiles["c_onesr"]
    sclr = c.tiles["c_sclr"]
    qsb = [c.sbuf(f"a_q{i}", [64, T], BF16) for i in range(2)]
    ksb = [c.sbuf(f"a_k{i}", [64, T], BF16) for i in range(2)]
    vaug = [c.sbuf(f"a_v{i}", [128, 32 * 65], BF16) for i in range(2)]
    ssb = [c.sbuf(f"a_s{i}", [128, 16384], BF16) for i in range(2)]
    t_mx = c.sbuf("a_mx", [128, 1], F32)
    t_mxr = c.sbuf("a_mxr", [1, 128], F32)
    t_negc = c.sbuf("a_negc", [1, 1], F32)
    t_bias = [c.sbuf(f"a_bias{i}", [128, 1], F32) for i in range(2)]
    t_r = c.sbuf("a_r", [1, TC], F32)
    t_rb = c.sbuf("a_rb", [64, TC], F32)
    astg = [c.sbuf(f"a_stg{i}", [64, TC], BF16) for i in range(2)]
    pst = [c.psum(f"a_ps{i}", [128, 1024], F32) for i in range(4)]

    for i in range(2):
        c.op("vector", "memset", vaug[i][:, 64:32 * 65:65], 1.0)
    c.drain("vector")
    c.sem_inc("vector", "a_vone", 1)
    c.wait("gpsimd", "a_vone", 1)

    NIT = H * NTC
    for it in range(NIT + 1):
        if it < NIT:
            h, qc = divmod(it, NTC)
            pair = it % 2
            tA0, tA1 = pst[pair * 2], pst[pair * 2 + 1]
            if qc == 0:
                hb = h % 2
                if h >= 2:
                    c.wait("sync", "a_av", (h - 1) * NTC)
                    c.wait("gpsimd", "a_av", (h - 1) * NTC)
                    c.wait("sync", "a_qk", (h - 1) * NTC * 16)
                c.dma("sync", qsb[hb][:], Qf[64 * h:64 * (h + 1), :], sem="a_ld")
                c.dma("sync", ksb[hb][:], Kf[64 * h:64 * (h + 1), :], sem="a_ld")
                vdst = vaug[hb][:].rearrange("p (kc f) -> p kc f", kc=32)[:, :, 0:64]
                vsrc = Vt[:, 64 * h:64 * (h + 1)].rearrange("(kc p) f -> p kc f", p=128)
                c.dma("gpsimd", vdst, vsrc, sem="a_ld")
            hb = h % 2
            c.wait("tensor", "a_ld", 48 * (h + 1))
            for g in range(16):
                if g < 2:
                    if it >= 2:
                        c.wait("tensor", "a_ev", (it - 1) * 16)
                        c.wait("tensor", "a_fin", it - 1)
                else:
                    c.wait("tensor", "a_ev", it * 16 + g - 1)
                for half in range(2):
                    kc = 2 * g + half
                    c.op("tensor", "matmul", (tA0 if g % 2 == 0 else tA1)[:, half * 512:(half + 1) * 512],
                         ksb[hb][:, kc * 128:(kc + 1) * 128],
                         qsb[hb][:, qc * 512:(qc + 1) * 512],
                         start=True, stop=True,
                         inc=("a_qk", 1) if half == 1 else None)
            for g in range(16):
                c.wait("vector", "a_qk", it * 16 + g + 1)
                if g == 0 and it >= 2:
                    c.wait("vector", "a_av", it - 1)
                c.op("vector", "tensor_copy", ssb[pair][:, g * 1024:(g + 1) * 1024],
                     (tA0 if g % 2 == 0 else tA1)[:], inc=("a_ev", 1))
            c.drain("vector")
            c.sem_inc("vector", "a_evd", 1)
            c.wait("gpsimd", "a_evd", it + 1)
            c.op("gpsimd", "tensor_reduce", t_negc[:], ssb[pair][:], AX.XYZWC, Op.max)
            c.drain("gpsimd")
            c.sem_inc("gpsimd", "a_g", 1)
            c.wait("tensor", "a_g", it + 1)
            c.op("tensor", "matmul", tA0[:, 0:1], sclr[:], t_negc[:],
                 start=True, stop=True, inc=("a_pb1", 1))
            c.wait("vector", "a_pb1", it + 1)
            c.op("vector", "tensor_copy", t_bias[pair][:], tA0[:, 0:1])
            c.drain("vector")
            c.sem_inc("vector", "a_vb", 1)
            c.wait("scalar", "a_vb", it + 1)
            c.op("scalar", "activation", ssb[pair][:], ssb[pair][:], AF.Exp,
                 bias=t_bias[pair][:, 0:1], scale=0.125)
            c.drain("scalar")
            c.sem_inc("scalar", "a_exp", 1)

        if it >= 1:
            hq = it - 1
            hB, qcB = divmod(hq, NTC)
            pb = hq % 2
            tB0, tB1 = pst[pb * 2], pst[pb * 2 + 1]
            hbB = hB % 2
            c.wait("tensor", "a_exp", hq + 1)
            for kc in range(32):
                c.op("tensor", "matmul", tB0[0:65, 0:512],
                     vaug[hbB][:, kc * 65:kc * 65 + 65],
                     ssb[pb][:, kc * 512:(kc + 1) * 512],
                     start=(kc == 0), stop=(kc == 31),
                     inc=("a_av", 1) if kc == 31 else None)
            c.wait("vector", "a_av", hq + 1)
            c.op("vector", "reciprocal", t_r[:], tB0[64:65, 0:512])
            c.drain("vector")
            c.sem_inc("vector", "a_vr", 1)
            c.wait("tensor", "a_vr", hq + 1)
            c.op("tensor", "matmul", tB1[0:64, 0:512], onesr[:, 0:64], t_r[:],
                 start=True, stop=True, inc=("a_pb2", 1))
            c.wait("vector", "a_pb2", hq + 1)
            c.op("vector", "tensor_copy", t_rb[:], tB1[0:64, 0:512])
            c.drain("vector")
            if hq >= 2:
                c.wait("vector", "a_st", 16 * (hq - 1))
            c.op("vector", "tensor_tensor", astg[hq % 2][:], tB0[0:64, 0:512],
                 t_rb[:], Op.mult)
            c.drain("vector")
            c.sem_inc("vector", "a_fin", 1)
            c.wait("sync", "a_fin", hq + 1)
            c.dma("sync", attnf[64 * hB:64 * (hB + 1), qcB * 512:(qcB + 1) * 512],
                  astg[hq % 2][:], sem="a_st")

    c.wait("sync", "a_st", 16 * NIT)



def emit_ln_tail(c, P, qi, rsb, tiles, out_dst, gslice=None, bslice=None,
                 rsb_ready=None, bfstg=None, bf_dst=None, eps=1e-5, nfo=8):
    (sq1, mrow, sq2, vrow, lnrow, rstdrow, xstg, ps_s0, ps_s1, ps_b0, ps_b1) = tiles
    ones128 = c.tiles["c_ones128"]
    onesr = c.tiles["c_onesr"]
    for fo in range(nfo):
        if rsb_ready is not None:
            c.wait("scalar", rsb_ready[0], rsb_ready[1](fo))
        if qi * nfo + fo >= 1:
            c.wait("scalar", f"{P}_psq", qi * nfo + fo)
        c.op("scalar", "activation", sq1[:], rsb[:, fo * 512:(fo + 1) * 512], AF.Square)
        c.drain("scalar")
        c.sem_inc("scalar", f"{P}_sq", 1)
        if rsb_ready is not None:
            c.wait("tensor", rsb_ready[0], rsb_ready[1](fo))
        if qi > 0 or fo > 0:
            c.wait("tensor", f"{P}_row", qi)
        c.op("tensor", "matmul", ps_s0[0:1, :], ones128[:], rsb[:, fo * 512:(fo + 1) * 512],
             start=(fo == 0), stop=(fo == nfo - 1),
             inc=(f"{P}_ps", 1) if fo == nfo - 1 else None)
        c.wait("tensor", f"{P}_sq", qi * nfo + fo + 1)
        c.op("tensor", "matmul", ps_s1[0:1, :], ones128[:], sq1[:],
             start=(fo == 0), stop=(fo == nfo - 1),
             inc=(f"{P}_psq", 1))
    c.wait("scalar", f"{P}_ps", qi + 1)
    c.op("scalar", "mul", mrow[0:1, :], ps_s0[0:1, :], 1.0 / D)
    c.drain("scalar")
    c.sem_inc("scalar", f"{P}_m", 1)
    c.wait("vector", f"{P}_m", qi + 1)
    c.wait("vector", f"{P}_psq", nfo * (qi + 1))
    c.op("vector", "tensor_mul", sq2[0:1, :], mrow[0:1, :], mrow[0:1, :])
    c.drain("vector")
    c.op("vector", "scalar_tensor_tensor", vrow[0:1, :], ps_s1[0:1, :], 1.0 / D,
         sq2[0:1, :], Op.mult, Op.subtract)
    c.drain("vector")
    c.sem_inc("vector", f"{P}_v", 1)
    c.wait("scalar", f"{P}_v", qi + 1)
    c.op("scalar", "activation", lnrow[0:1, :], vrow[0:1, :], AF.Ln,
         bias=c.tiles["c_eps"][0:1, 0:1])
    c.drain("scalar")
    c.op("scalar", "activation", rstdrow[0:1, :], lnrow[0:1, :], AF.Exp, scale=-0.5)
    c.drain("scalar")
    c.sem_inc("scalar", f"{P}_row", 1)
    c.wait("tensor", f"{P}_row", qi + 1)
    c.op("tensor", "matmul", ps_b0[:, 0:512], onesr[:], mrow[0:1, :],
         start=True, stop=True)
    c.op("tensor", "matmul", ps_b1[:, 0:512], onesr[:], rstdrow[0:1, :],
         start=True, stop=True, inc=(f"{P}_bc", 1))
    for fo in range(nfo):
        c.wait("vector", f"{P}_bc", qi + 1)
        if out_dst is not None and qi * nfo + fo >= 2:
            c.wait("vector", f"{P}_st", 16 * (qi * nfo + fo - 1))
        st = xstg[(qi * nfo + fo) % 2]
        if bf_dst is not None:
            final = bf_dst(fo)
        elif bfstg is not None:
            final = bfstg[(qi * nfo + fo) % 2]
        else:
            final = st
        c.op("vector", "tensor_tensor", st[:], rsb[:, fo * 512:(fo + 1) * 512],
             ps_b0[:, 0:512], Op.subtract)
        c.drain("vector")
        if gslice is not None:
            c.op("vector", "tensor_tensor", st[:], st[:], ps_b1[:, 0:512], Op.mult)
            c.drain("vector")
            c.op("vector", "scalar_tensor_tensor", final[:], st[:], gslice(fo),
                 bslice(fo).broadcast_to([128, 512]), Op.mult, Op.add)
            c.drain("vector")
        else:
            c.op("vector", "tensor_tensor", final[:], st[:], ps_b1[:, 0:512], Op.mult)
            c.drain("vector")
        c.sem_inc("vector", f"{P}_va", 1)
        if out_dst is not None:
            c.wait("sync", f"{P}_va", qi * nfo + fo + 1)
            c.dma("sync", out_dst(fo), final[:], sem=f"{P}_st")
    c.wait("tensor", f"{P}_va", (qi + 1) * nfo)


def ln_tiles(c, P):
    return (
        c.sbuf(f"{P}_sq1", [128, 512], F32),
        c.sbuf(f"{P}_mrow", [1, 512], F32),
        c.sbuf(f"{P}_sq2", [1, 512], F32),
        c.sbuf(f"{P}_vrow", [1, 512], F32),
        c.sbuf(f"{P}_lnrow", [1, 512], F32),
        c.sbuf(f"{P}_rstd", [1, 512], F32),
        [c.sbuf(f"{P}_xstg{i}", [128, 512], F32) for i in range(2)],
        c.psum(f"{P}_pss0", [128, 512], F32),
        c.psum(f"{P}_pss1", [128, 512], F32),
        c.psum(f"{P}_psb0", [128, 512], F32),
        c.psum(f"{P}_psb1", [128, 512], F32),
    )



def ph_proj(c):
    x = c.dram("x_in", [T, D], F32, kind="ExternalInput")
    wp = c.dram("L0_attn_w_proj", [D, D], F32, kind="ExternalInput")
    bp_h = c.dram("b_proj_t", [128, 8], F32, kind="ExternalInput")
    attnf = c.dram("attnf", [D, T], BF16)
    x1 = c.dram("x1", [D, T], F32)

    idn = c.tiles["c_idnf"]
    wsb = c.sbuf("p3_w", [128, 8 * D], BF16)
    bpt = c.sbuf("p3_b", [128, 8], F32)
    xtok = [c.sbuf(f"p3_xtok{i}", [128, 4 * D], F32) for i in range(2)]
    att = [c.sbuf(f"p3_att{i}", [128, 8 * 512], BF16) for i in range(2)]
    xf32 = [c.sbuf(f"p3_xf{i}", [128, 8 * 512], F32) for i in range(2)]
    rsb = c.sbuf("p3_rsb", [128, 8 * 512], F32)
    lt = ln_tiles(c, "p3")
    ps_b0, ps_b1 = lt[-2], lt[-1]
    ps_p = [c.psum(f"p3_psp{i}", [128, 512], F32) for i in range(2)]

    for kc in range(8):
        c.dma("gpsimd", wsb[:, kc * D:(kc + 1) * D], wp[kc * 128:(kc + 1) * 128, :],
              sem="p3_w")
    c.dma("sync", bpt[:], bp_h, sem="p3_b")
    c.wait("tensor", "p3_w", 16 * 8)
    c.wait("vector", "p3_b", 16)
    c.wait("tensor", "v_init", 1)

    jtr = 0
    for qc in range(NTC):
        if qc >= 2:
            c.wait("sync", "p3_tr", 32 * (qc - 1))
            c.wait("sync", "p3_pm", 8 * (qc - 1))
        xs = x[qc * TC:(qc + 1) * TC, :].rearrange("(i p) f -> p i f", p=128)
        c.dma("sync", xtok[qc % 2][:].rearrange("p (i f) -> p i f", i=4), xs, sem="p3_x")
        asrc = attnf[:, qc * TC:(qc + 1) * TC].rearrange("(kc p) t -> p kc t", p=128)
        c.dma("sync", att[qc % 2][:].rearrange("p (kc t) -> p kc t", kc=8), asrc,
              sem="p3_a")

        c.wait("tensor", "p3_x", 16 * (qc + 1))
        if qc >= 1:
            c.wait("tensor", "p3_va", qc * 8)
        for i in range(4):
            for fc in range(8):
                tt = (ps_b0 if jtr % 2 == 0 else ps_b1)
                if jtr >= 2:
                    c.wait("tensor", "p3_vx", jtr - 1)
                c.op("tensor", "transpose", tt[:, 0:128],
                     xtok[qc % 2][:, i * D + fc * 128:i * D + (fc + 1) * 128],
                     idn[:], inc=("p3_tr", 1))
                c.wait("vector", "p3_tr", jtr + 1)
                c.op("vector", "tensor_copy",
                     xf32[qc % 2][:, fc * 512 + i * 128:fc * 512 + (i + 1) * 128],
                     tt[:, 0:128])
                c.drain("vector")
                c.sem_inc("vector", "p3_vx", 1)
                jtr += 1

        c.wait("tensor", "p3_a", 16 * (qc + 1))
        for fo in range(8):
            g = qc * 8 + fo
            if g >= 2:
                c.wait("tensor", "p3_vr", g - 1)
            for kc in range(8):
                c.op("tensor", "matmul", ps_p[fo % 2][:],
                     wsb[:, kc * D + fo * 128:kc * D + (fo + 1) * 128],
                     att[qc % 2][:, kc * 512:(kc + 1) * 512],
                     start=(kc == 0), stop=(kc == 7),
                     inc=("p3_pm", 1) if kc == 7 else None)
            c.wait("vector", "p3_pm", g + 1)
            c.op("vector", "scalar_tensor_tensor", rsb[:, fo * 512:(fo + 1) * 512],
                 ps_p[fo % 2][:], bpt[:, fo:fo + 1],
                 xf32[qc % 2][:, fo * 512:(fo + 1) * 512], Op.add, Op.add)
            c.drain("vector")
            c.sem_inc("vector", "p3_vr", 1)

        emit_ln_tail(
            c, "p3", qc, rsb, lt,
            out_dst=lambda fo, _qc=qc: x1[:, _qc * TC:(_qc + 1) * TC]
                .rearrange("(fc p) t -> fc p t", p=128)[fo],
            rsb_ready=("p3_vr", lambda fo, _qc=qc: _qc * 8 + fo + 1),
        )

    c.wait("sync", "p3_st", 16 * 64)



def ph_ffnA(c):
    x1 = c.dram("x1", [D, T], F32)
    w1h = c.dram("w1T", [D, FH], F32, kind="ExternalInput")
    w2h = c.dram("w2T", [D, FH], F32, kind="ExternalInput")
    s1h = c.dram("w1s_t", [128, 32], F32, kind="ExternalInput")
    s2h = c.dram("w2s_t", [128, 32], F32, kind="ExternalInput")
    hA = c.dram("hA", [FH, T], BF16)

    w1 = c.sbuf("f4_w1", [128, 8 * FH], BF16)
    w2 = c.sbuf("f4_w2", [128, 8 * FH], BF16)
    s1 = c.sbuf("f4_s1", [128, 32], F32)
    s2 = c.sbuf("f4_s2", [128, 32], F32)
    x1b = [c.sbuf(f"f4_x{i}", [128, 8 * 512], BF16) for i in range(2)]
    tA = c.sbuf("f4_tA", [128, 512], F32)
    tB = c.sbuf("f4_tB", [128, 512], F32)
    hstg = [c.sbuf(f"f4_h{i}", [128, 512], BF16) for i in range(2)]
    psA = [c.psum(f"f4_psA{i}", [128, 512], F32) for i in range(2)]
    psB = [c.psum(f"f4_psB{i}", [128, 512], F32) for i in range(2)]

    for kc in range(8):
        c.dma("gpsimd", w1[:, kc * FH:(kc + 1) * FH], w1h[kc * 128:(kc + 1) * 128, :], sem="f4_w")
        c.dma("gpsimd", w2[:, kc * FH:(kc + 1) * FH], w2h[kc * 128:(kc + 1) * 128, :], sem="f4_w")
    c.dma("sync", s1[:], s1h, sem="f4_s")
    c.dma("sync", s2[:], s2h, sem="f4_s")
    c.wait("tensor", "f4_w", 16 * 16)
    c.wait("scalar", "f4_s", 32)

    g = 0
    for qc in range(NTC):
        if qc >= 2:
            c.wait("gpsimd", "f4_pm", 32 * (qc - 1))
        xs = x1[:, qc * TC:(qc + 1) * TC].rearrange("(kc p) t -> p kc t", p=128)
        c.dma("gpsimd", x1b[qc % 2][:].rearrange("p (kc t) -> p kc t", kc=8), xs, sem="f4_x")
        c.wait("tensor", "f4_x", 16 * (qc + 1))
        for m in range(32):
            if g >= 2:
                c.wait("tensor", "f4_v", g - 1)
            for kc in range(8):
                c.op("tensor", "matmul", psA[g % 2][:],
                     w1[:, kc * FH + m * 128:kc * FH + (m + 1) * 128],
                     x1b[qc % 2][:, kc * 512:(kc + 1) * 512],
                     start=(kc == 0), stop=(kc == 7))
                c.op("tensor", "matmul", psB[g % 2][:],
                     w2[:, kc * FH + m * 128:kc * FH + (m + 1) * 128],
                     x1b[qc % 2][:, kc * 512:(kc + 1) * 512],
                     start=(kc == 0), stop=(kc == 7),
                     inc=("f4_pm", 1) if kc == 7 else None)
            c.wait("scalar", "f4_pm", g + 1)
            c.wait("scalar", "f4_v", g)
            c.op("scalar", "activation", tA[:], psA[g % 2][:], AF.Silu,
                 scale=s1[:, m:m + 1])
            c.drain("scalar")
            c.op("scalar", "mul", tB[:], psB[g % 2][:], s2[:, m:m + 1])
            c.drain("scalar")
            c.sem_inc("scalar", "f4_a", 1)
            c.wait("vector", "f4_a", g + 1)
            if g >= 2:
                c.wait("vector", "f4_st", 16 * (g - 1))
            c.op("vector", "tensor_tensor", hstg[g % 2][:], tA[:], tB[:], Op.mult)
            c.drain("vector")
            c.sem_inc("vector", "f4_v", 1)
            c.wait("sync", "f4_v", g + 1)
            c.dma("sync", hA[m * 128:(m + 1) * 128, qc * TC:(qc + 1) * TC],
                  hstg[g % 2][:], sem="f4_st")
            g += 1
    c.wait("sync", "f4_st", 16 * 256)



def ph_hmid(c, P, win_name, s_name, hin_name, hout_name):
    wb_h = c.dram(win_name, [32, 32, 128, 128], F32, kind="ExternalInput")
    sh = c.dram(s_name, [128, 32], F32, kind="ExternalInput")
    hIn = c.dram(hin_name, [FH, T], BF16)
    hOut = c.dram(hout_name, [FH, T], BF16)

    hAt = c.sbuf(f"{P}_hin", [128, 32 * 2048], BF16)
    wblk = [c.sbuf(f"{P}_w{i}", [128, 32 * 128], BF16) for i in range(2)]
    ssb = c.sbuf(f"{P}_s", [128, 32], F32)
    hstg = [c.sbuf(f"{P}_h{i}", [128, 512], BF16) for i in range(2)]
    ps = [c.psum(f"{P}_ps{i}", [128, 512], F32) for i in range(2)]

    c.dma("sync", ssb[:], sh, sem=f"{P}_sld")
    c.wait("scalar", f"{P}_sld", 16)

    g = 0
    for half in range(2):
        if half >= 1:
            c.wait("sync", f"{P}_pm", 128 * half)
        hs = hIn[:, half * 2048:(half + 1) * 2048].rearrange("(kc p) t -> p kc t", p=128)
        c.dma("sync", hAt[:].rearrange("p (kc t) -> p kc t", kc=32), hs, sem=f"{P}_hx")
        c.wait("gpsimd", f"{P}_hx", 16 * (half + 1))
        for m in range(32):
            gm = half * 32 + m
            if gm >= 2:
                c.wait("gpsimd", f"{P}_pm", 4 * (gm - 1))
            c.dma("gpsimd", wblk[gm % 2][:].rearrange("p (kc mo) -> p kc mo", kc=32),
                  wb_h[m].rearrange("kc p mo -> p kc mo"), sem=f"{P}_w")
            c.wait("tensor", f"{P}_w", 16 * (gm + 1))
            for n in range(4):
                if g >= 2:
                    c.wait("tensor", f"{P}_a", g - 1)
                for kc in range(32):
                    c.op("tensor", "matmul", ps[g % 2][:],
                         wblk[gm % 2][:, kc * 128:(kc + 1) * 128],
                         hAt[:, kc * 2048 + n * 512:kc * 2048 + (n + 1) * 512],
                         start=(kc == 0), stop=(kc == 31),
                         inc=(f"{P}_pm", 1) if kc == 31 else None)
                c.wait("scalar", f"{P}_pm", g + 1)
                if g >= 2:
                    c.wait("scalar", f"{P}_st", 16 * (g - 1))
                c.op("scalar", "activation", hstg[g % 2][:], ps[g % 2][:], AF.Gelu,
                     scale=ssb[:, m:m + 1])
                c.drain("scalar")
                c.sem_inc("scalar", f"{P}_a", 1)
                c.wait("sync", f"{P}_a", g + 1)
                c.dma("sync", hOut[m * 128:(m + 1) * 128,
                                   half * 2048 + n * 512:half * 2048 + (n + 1) * 512],
                      hstg[g % 2][:], sem=f"{P}_st")
                g += 1
    c.wait("sync", f"{P}_st", 16 * 256)


def ph_h1(c):
    ph_hmid(c, "p5", "h1b", "h1s_t", "hA", "hB")


def ph_h2(c):
    ph_hmid(c, "p6", "h2b", "h2s_t", "hB", "hC")



def ph_w3ln(c):
    w3h = c.dram("w3T", [FH, D], F32, kind="ExternalInput")
    s3h = c.dram("w3s_t", [128, 8], F32, kind="ExternalInput")
    hC = c.dram("hC", [FH, T], BF16)
    x1 = c.dram("x1", [D, T], F32)
    x2 = c.dram("x2", [D, T], F32)

    w3 = c.sbuf("p7_w3", [128, 32 * D], BF16)
    s3 = c.sbuf("p7_s3", [128, 8], F32)
    hCt = [c.sbuf("p7_hc0", [128, 32 * 512], BF16)] * 2
    x1f = [c.sbuf("p7_x10", [128, 8 * 512], F32)] * 2
    rsb = c.sbuf("p7_rsb", [128, 8 * 512], F32)
    lt = ln_tiles(c, "p7")
    ps_p = [c.psum(f"p7_psp{i}", [128, 512], F32) for i in range(2)]

    for kc in range(32):
        c.dma("gpsimd", w3[:, kc * D:(kc + 1) * D], w3h[kc * 128:(kc + 1) * 128, :],
              sem="p7_w")
    c.dma("sync", s3[:], s3h, sem="p7_s")
    c.wait("tensor", "p7_w", 16 * 32)
    c.wait("vector", "p7_s", 16)
    c.wait("tensor", "v_init", 1)

    for qc in range(NTC):
        if qc >= 1:
            c.wait("sync", "p7_pm", 8 * qc)
            c.wait("sync", "p7_vr", 8 * qc)
        hs = hC[:, qc * TC:(qc + 1) * TC].rearrange("(kc p) t -> p kc t", p=128)
        c.dma("sync", hCt[qc % 2][:].rearrange("p (kc t) -> p kc t", kc=32), hs,
              sem="p7_hx")
        xs = x1[:, qc * TC:(qc + 1) * TC].rearrange("(kc p) t -> p kc t", p=128)
        c.dma("sync", x1f[qc % 2][:].rearrange("p (kc t) -> p kc t", kc=8), xs,
              sem="p7_x1")
        c.wait("tensor", "p7_hx", 16 * (qc + 1))
        c.wait("vector", "p7_x1", 16 * (qc + 1))
        for fo in range(8):
            g = qc * 8 + fo
            if g >= 2:
                c.wait("tensor", "p7_vr", g - 1)
            for kc in range(32):
                c.op("tensor", "matmul", ps_p[fo % 2][:],
                     w3[:, kc * D + fo * 128:kc * D + (fo + 1) * 128],
                     hCt[qc % 2][:, kc * 512:(kc + 1) * 512],
                     start=(kc == 0), stop=(kc == 31),
                     inc=("p7_pm", 1) if kc == 31 else None)
            c.wait("vector", "p7_pm", g + 1)
            c.op("vector", "scalar_tensor_tensor", rsb[:, fo * 512:(fo + 1) * 512],
                 ps_p[fo % 2][:], s3[:, fo:fo + 1],
                 x1f[qc % 2][:, fo * 512:(fo + 1) * 512], Op.mult, Op.add)
            c.drain("vector")
            c.sem_inc("vector", "p7_vr", 1)
        emit_ln_tail(
            c, "p7", qc, rsb, lt,
            out_dst=lambda fo, _qc=qc: x2[:, _qc * TC:(_qc + 1) * TC]
                .rearrange("(fc p) t -> fc p t", p=128)[fo],
            rsb_ready=("p7_vr", lambda fo, _qc=qc: _qc * 8 + fo + 1),
        )
    c.wait("sync", "p7_st", 16 * 64)



def ph_lnwto(c):
    x2 = c.dram("x2", [D, T], F32)
    wtoh = c.dram("W_toT", [D, 2 * FR], F32, kind="ExternalInput")
    g1h = c.dram("ln1g_t", [128, 8], F32, kind="ExternalInput")
    b1h = c.dram("ln1b_t", [128, 8], F32, kind="ExternalInput")
    frh = c.dram("freq2", [128, 2], F32, kind="ExternalInput")
    ubuf = c.dram("ubuf", [128, 4 * T], F32)

    wto = c.sbuf("p8_wto", [128, 8 * 512], BF16)
    g1 = c.sbuf("p8_g1", [128, 8], F32)
    b1 = c.sbuf("p8_b1", [128, 8], F32)
    fr2 = c.sbuf("p8_fr", [128, 2], F32)
    om = c.sbuf("p8_om", [128, 2], F32)
    c01 = c.sbuf("p8_c01", [128, 2], F32)
    s01 = c.sbuf("p8_s01", [128, 2], F32)
    ns01 = c.sbuf("p8_ns01", [128, 2], F32)
    halfpi = c.sbuf("p8_hpi", [128, 1], F32)
    x2f = [c.sbuf(f"p8_x2{i}", [128, 8 * 512], F32) for i in range(2)]
    lnx = c.sbuf("p8_lnx", [128, 8 * T], BF16)
    tra = c.sbuf("p8_tra", [128, 512], F32)
    trb = c.sbuf("p8_trb", [128, 512], F32)
    ustg = [c.sbuf(f"p8_u{i}", [128, 4 * 512], F32) for i in range(2)]
    lt = ln_tiles(c, "p8")
    ps_w = [c.psum(f"p8_psw{i}", [128, 512], F32) for i in range(4)]

    for kc in range(8):
        c.dma("gpsimd", wto[:, kc * 512:(kc + 1) * 512],
              wtoh[kc * 128:(kc + 1) * 128, :], sem="p8_w")
    c.dma("sync", g1[:], g1h, sem="p8_g")
    c.dma("sync", b1[:], b1h, sem="p8_g")
    c.dma("sync", fr2[:], frh, sem="p8_g")
    c.wait("scalar", "p8_g", 48)
    c.wait("scalar", "v_init", 1)
    c.op("scalar", "activation", om[:], fr2[:], AF.Tanh)
    c.drain("scalar")
    c.op("scalar", "mul", om[:], om[:], 0.1)
    c.drain("scalar")
    c.op("vector", "memset", halfpi[:], 1.5707963267948966)
    c.drain("vector")
    c.sem_inc("vector", "p8_hp", 1)
    c.wait("scalar", "p8_hp", 1)
    c.op("scalar", "activation", c01[:], om[:], AF.Sin, bias=halfpi[:, 0:1])
    c.drain("scalar")
    c.op("scalar", "activation", s01[:], om[:], AF.Sin)
    c.drain("scalar")
    c.op("scalar", "mul", c01[:], c01[:], 0.1)
    c.drain("scalar")
    c.op("scalar", "mul", s01[:], s01[:], 0.1)
    c.drain("scalar")
    c.op("scalar", "mul", ns01[:], s01[:], -1.0)
    c.drain("scalar")
    c.sem_inc("scalar", "p8_cs", 1)
    c.wait("vector", "p8_cs", 1)
    c.wait("vector", "p8_g", 48)
    c.wait("tensor", "p8_w", 16 * 8)
    c.wait("tensor", "v_init", 1)

    for qc in range(NTC):
        xs = x2[:, qc * TC:(qc + 1) * TC].rearrange("(kc p) t -> p kc t", p=128)
        c.dma("sync", x2f[qc % 2][:].rearrange("p (kc t) -> p kc t", kc=8), xs,
              sem="p8_x")
        c.wait("vector", "p8_x", 16 * (qc + 1))
        c.wait("scalar", "p8_x", 16 * (qc + 1))
        c.wait("tensor", "p8_x", 16 * (qc + 1))
        emit_ln_tail(
            c, "p8l", qc, x2f[qc % 2], lt, out_dst=None,
            gslice=lambda fo: g1[:, fo:fo + 1],
            bslice=lambda fo: b1[:, fo:fo + 1],
            bf_dst=lambda fo, _qc=qc: lnx[:, fo * T + _qc * 512:fo * T + (_qc + 1) * 512],
        )
        c.drain("vector")
        c.sem_inc("vector", "p8_lnq", 1)
        if qc < NTC - 1:
            c.wait("sync", "p8l_va", (qc + 1) * 8)

    for qc in range(NTC):
        c.wait("tensor", "p8_lnq", NTC)
        for m in range(4):
            if qc >= 1:
                c.wait("tensor", "p8_rot", 4 * qc)
            for kc in range(8):
                c.op("tensor", "matmul", ps_w[m][:],
                     wto[:, kc * 512 + m * 128:kc * 512 + (m + 1) * 128],
                     lnx[:, kc * T + qc * 512:kc * T + (qc + 1) * 512],
                     start=(kc == 0), stop=(kc == 7),
                     inc=("p8_pm", 1) if kc == 7 else None)
        for h in range(2):
            c.wait("scalar", "p8_pm", 4 * (qc + 1))
            c.wait("scalar", "p8_rot", 4 * qc + 2 * h)
            if qc >= 2:
                c.wait("scalar", "p8_ust", 16 * (qc - 1))
            c.op("scalar", "mul", tra[:], ps_w[h][:], c01[:, h:h + 1])
            c.drain("scalar")
            c.op("scalar", "mul", trb[:], ps_w[h][:], s01[:, h:h + 1])
            c.drain("scalar")
            c.sem_inc("scalar", "p8_ra", 1)
            c.wait("vector", "p8_ra", 2 * qc + h + 1)
            us = ustg[qc % 2]
            c.op("vector", "scalar_tensor_tensor", us[:, h:4 * 512:4],
                 ps_w[2 + h][:], ns01[:, h:h + 1], tra[:], Op.mult, Op.add)
            c.drain("vector")
            c.op("vector", "scalar_tensor_tensor", us[:, 2 + h:4 * 512:4],
                 ps_w[2 + h][:], c01[:, h:h + 1], trb[:], Op.mult, Op.add)
            c.drain("vector")
            c.sem_inc("vector", "p8_rot", 2)
        c.wait("sync", "p8_rot", 4 * (qc + 1))
        c.dma("sync", ubuf[:, qc * 2048:(qc + 1) * 2048], ustg[qc % 2][:],
              sem="p8_ust")
    c.wait("sync", "p8_ust", 16 * NTC)



def ph_scan(c):
    ubuf = c.dram("ubuf", [128, 4 * T], F32)
    ldh = c.dram("ldec2", [128, 2], F32, kind="ExternalInput")
    frh = c.dram("freq2", [128, 2], F32, kind="ExternalInput")
    spec = c.dram("spec", [4 * 128, T], BF16)

    ld2 = c.sbuf("p9_ld", [128, 2], F32)
    fr2 = c.sbuf("p9_fr", [128, 2], F32)
    dec = c.sbuf("p9_dec", [128, 2], F32)
    om = c.sbuf("p9_om", [128, 2], F32)
    cs = c.sbuf("p9_cs", [128, 2], F32)
    sn = c.sbuf("p9_sn", [128, 2], F32)
    cd2 = c.sbuf("p9_cd2", [128, 2], F32)
    sd2 = c.sbuf("p9_sd2", [128, 2], F32)
    CD4 = c.sbuf("p9_cd4", [128, 4], F32)
    SD4 = c.sbuf("p9_sd4", [128, 4], F32)
    halfpi = c.sbuf("p9_hpi", [128, 1], F32)
    ut = [c.sbuf(f"p9_u{i}", [128, 4 * 512], F32) for i in range(2)]
    hist = c.sbuf("p9_hist", [128, 4 * (T + 1)], F32)
    tmpa = c.sbuf("p9_ta", [128, 4], F32)
    tmpb = c.sbuf("p9_tb", [128, 4], F32)
    sstg = [c.sbuf(f"p9_ss{i}", [128, 4 * 512], BF16) for i in range(2)]

    c.dma("sync", ld2[:], ldh, sem="p9_g")
    c.dma("sync", fr2[:], frh, sem="p9_g")
    c.wait("scalar", "p9_g", 32)
    c.wait("scalar", "v_init", 1)
    c.op("scalar", "activation", dec[:], ld2[:], AF.Sigmoid)
    c.drain("scalar")
    c.op("scalar", "activation", om[:], fr2[:], AF.Tanh)
    c.drain("scalar")
    c.op("scalar", "mul", om[:], om[:], 0.1)
    c.drain("scalar")
    c.sem_inc("scalar", "p9_a1", 1)
    c.wait("vector", "p9_a1", 1)
    c.op("vector", "memset", halfpi[:], 1.5707963267948966)
    c.drain("vector")
    c.sem_inc("vector", "p9_v1", 1)
    c.wait("scalar", "p9_v1", 1)
    c.op("scalar", "activation", cs[:], om[:], AF.Sin, bias=halfpi[:, 0:1])
    c.drain("scalar")
    c.op("scalar", "activation", sn[:], om[:], AF.Sin)
    c.drain("scalar")
    c.sem_inc("scalar", "p9_a2", 1)
    c.wait("vector", "p9_a2", 1)
    c.op("vector", "tensor_mul", cd2[:], cs[:], dec[:])
    c.drain("vector")
    c.op("vector", "tensor_mul", sd2[:], sn[:], dec[:])
    c.drain("vector")
    c.op("vector", "tensor_copy", CD4[:, 0:2], cd2[:])
    c.op("vector", "tensor_copy", CD4[:, 2:4], cd2[:])
    c.op("vector", "tensor_copy", SD4[:, 2:4], sd2[:])
    c.drain("vector")
    c.op("vector", "tensor_scalar_mul", SD4[:, 0:2], sd2[:], -1.0)
    c.drain("vector")
    c.op("vector", "memset", hist[:, 0:4], 0.0)
    c.drain("vector")
    c.sem_inc("vector", "p9_k", 1)
    c.wait("gpsimd", "p9_k", 1)

    for qc in range(NTC):
        if qc >= 2:
            c.wait("sync", "p9_gc", qc - 1)
        c.dma("sync", ut[qc % 2][:], ubuf[:, qc * 2048:(qc + 1) * 2048], sem="p9_u")
        c.wait("gpsimd", "p9_u", 16 * (qc + 1))
        for tl in range(512):
            t = qc * 512 + tl
            prev = hist[:, 4 * t:4 * t + 4]
            prev_sw = prev.rearrange("p (b a) -> p b a", b=2)[:, ::-1, :]
            cur = hist[:, 4 * (t + 1):4 * (t + 1) + 4]
            c.op("gpsimd", "tensor_tensor",
                 tmpb[:].rearrange("p (b a) -> p b a", b=2), prev_sw,
                 SD4[:].rearrange("p (b a) -> p b a", b=2), Op.mult)
            c.op("gpsimd", "tensor_tensor", tmpa[:], prev, CD4[:], Op.mult)
            c.op("gpsimd", "tensor_add", tmpa[:], tmpa[:], tmpb[:])
            c.op("gpsimd", "tensor_add", tmpa[:], tmpa[:], ut[qc % 2][:, 4 * tl:4 * tl + 4])
            c.op("gpsimd", "tensor_scalar", cur, tmpa[:], 10.0, -10.0, Op.min, Op.max)
        c.drain("gpsimd")
        c.sem_inc("gpsimd", "p9_gc", 1)
        c.wait("vector", "p9_gc", qc + 1)
        if qc >= 2:
            c.wait("vector", "p9_st", 16 * (qc - 1))
        for j in range(4):
            hs = hist[:, 4 * (qc * 512 + 1) + j:4 * (qc * 512 + 512 + 1):4]
            c.op("vector", "tensor_copy", sstg[qc % 2][:, j * 512:(j + 1) * 512], hs)
        c.drain("vector")
        c.sem_inc("vector", "p9_cv", 1)
        c.wait("sync", "p9_cv", qc + 1)
        c.dma("sync",
              spec[:, qc * 512:(qc + 1) * 512].rearrange("(m p) t -> p m t", p=128),
              sstg[qc % 2][:].rearrange("p (m t) -> p m t", m=4), sem="p9_st")
    c.wait("sync", "p9_st", 16 * NTC)



def ph_wfrom_ffn1(c):
    spec = c.dram("spec", [4 * 128, T], BF16)
    x2 = c.dram("x2", [D, T], F32)
    wfh = c.dram("W_fromT", [2 * FR, D], F32, kind="ExternalInput")
    g3h = c.dram("ln3g_t", [128, 8], F32, kind="ExternalInput")
    b3h = c.dram("ln3b_t", [128, 8], F32, kind="ExternalInput")
    f1h = c.dram("ffn1T", [D, FH], F32, kind="ExternalInput")
    bf1h = c.dram("bffn1_t", [128, 32], F32, kind="ExternalInput")
    r3d = c.dram("r3", [D, T], F32)
    gh = c.dram("gh", [FH, T], BF16)

    wfrom = c.sbuf("pa_wf", [128, 4 * D], BF16)
    f1 = c.sbuf("pa_f1", [128, 8 * FH], BF16)
    g3 = c.sbuf("pa_g3", [128, 8], F32)
    b3 = c.sbuf("pa_b3", [128, 8], F32)
    bf1 = c.sbuf("pa_bf1", [128, 32], F32)
    spc = [c.sbuf(f"pa_sp{i}", [128, 4 * 512], BF16) for i in range(2)]
    x2f = [c.sbuf("pa_x20", [128, 8 * 512], F32)] * 2
    rsb = c.sbuf("pa_rsb", [128, 8 * 512], F32)
    x3b = c.sbuf("pa_x3b", [128, 8 * 512], BF16)
    gstg = [c.sbuf(f"pa_g{i}", [128, 512], BF16) for i in range(2)]
    lt = ln_tiles(c, "pa")
    ps_p = [c.psum(f"pa_psp{i}", [128, 512], F32) for i in range(2)]
    ps_f = [c.psum(f"pa_psf{i}", [128, 512], F32) for i in range(2)]

    for kc in range(4):
        c.dma("gpsimd", wfrom[:, kc * D:(kc + 1) * D], wfh[kc * 128:(kc + 1) * 128, :],
              sem="pa_w")
    for kc in range(8):
        c.dma("gpsimd", f1[:, kc * FH:(kc + 1) * FH], f1h[kc * 128:(kc + 1) * 128, :],
              sem="pa_w")
    c.dma("sync", g3[:], g3h, sem="pa_g")
    c.dma("sync", b3[:], b3h, sem="pa_g")
    c.dma("sync", bf1[:], bf1h, sem="pa_g")
    c.wait("tensor", "pa_w", 16 * 12)
    c.wait("vector", "pa_g", 48)
    c.wait("scalar", "pa_g", 48)
    c.wait("tensor", "v_init", 1)

    for qc in range(NTC):
        if qc >= 2:
            c.wait("sync", "pa_pm", 8 * (qc - 1))
        if qc >= 1:
            c.wait("sync", "pa_vr", 8 * qc)
        ss = spec[:, qc * TC:(qc + 1) * TC].rearrange("(kc p) t -> p kc t", p=128)
        c.dma("sync", spc[qc % 2][:].rearrange("p (kc t) -> p kc t", kc=4), ss,
              sem="pa_sx")
        xs = x2[:, qc * TC:(qc + 1) * TC].rearrange("(kc p) t -> p kc t", p=128)
        c.dma("sync", x2f[qc % 2][:].rearrange("p (kc t) -> p kc t", kc=8), xs,
              sem="pa_xx")
        c.wait("tensor", "pa_sx", 16 * (qc + 1))
        c.wait("vector", "pa_xx", 16 * (qc + 1))
        for fo in range(8):
            g = qc * 8 + fo
            if g >= 2:
                c.wait("tensor", "pa_vr", g - 1)
            for kc in range(4):
                c.op("tensor", "matmul", ps_p[fo % 2][:],
                     wfrom[:, kc * D + fo * 128:kc * D + (fo + 1) * 128],
                     spc[qc % 2][:, kc * 512:(kc + 1) * 512],
                     start=(kc == 0), stop=(kc == 3),
                     inc=("pa_pm", 1) if kc == 3 else None)
            c.wait("vector", "pa_pm", g + 1)
            if g >= 2:
                c.wait("vector", "pa_r3st", 16 * (g - 1))
            c.op("vector", "scalar_tensor_tensor", rsb[:, fo * 512:(fo + 1) * 512],
                 ps_p[fo % 2][:], 0.1,
                 x2f[qc % 2][:, fo * 512:(fo + 1) * 512], Op.mult, Op.add)
            c.drain("vector")
            c.sem_inc("vector", "pa_vr", 1)
            c.wait("sync", "pa_vr", g + 1)
            c.dma("sync", r3d[:, qc * TC:(qc + 1) * TC]
                  .rearrange("(fc p) t -> fc p t", p=128)[fo],
                  rsb[:, fo * 512:(fo + 1) * 512], sem="pa_r3st")
        emit_ln_tail(
            c, "pa", qc, rsb, lt, out_dst=None,
            gslice=lambda fo: g3[:, fo:fo + 1],
            bslice=lambda fo: b3[:, fo:fo + 1],
            bf_dst=lambda fo: x3b[:, fo * 512:(fo + 1) * 512],
            rsb_ready=("pa_vr", lambda fo, _qc=qc: _qc * 8 + fo + 1),
        )
        for m in range(32):
            gg = qc * 32 + m
            if gg >= 2:
                c.wait("tensor", "pa_a", gg - 1)
            c.wait("tensor", "pa_va", (qc + 1) * 8)
            for kc in range(8):
                c.op("tensor", "matmul", ps_f[m % 2][:],
                     f1[:, kc * FH + m * 128:kc * FH + (m + 1) * 128],
                     x3b[:, kc * 512:(kc + 1) * 512],
                     start=(kc == 0), stop=(kc == 7),
                     inc=("pa_fm", 1) if kc == 7 else None)
            c.wait("scalar", "pa_fm", gg + 1)
            if gg >= 2:
                c.wait("scalar", "pa_gst", 16 * (gg - 1))
            c.op("scalar", "activation", gstg[gg % 2][:], ps_f[m % 2][:], AF.Gelu,
                 bias=bf1[:, m:m + 1], scale=1.0)
            c.drain("scalar")
            c.sem_inc("scalar", "pa_a", 1)
            c.wait("sync", "pa_a", gg + 1)
            c.dma("sync", gh[m * 128:(m + 1) * 128, qc * TC:(qc + 1) * TC],
                  gstg[gg % 2][:], sem="pa_gst")
        c.wait("vector", "pa_fm", (qc + 1) * 32)
    c.wait("sync", "pa_gst", 16 * 256)
    c.wait("sync", "pa_r3st", 16 * 64)



def ph_ffn2ln(c):
    gh = c.dram("gh", [FH, T], BF16)
    x2 = c.dram("x2", [D, T], F32)
    r3d = c.dram("r3", [D, T], F32)
    f2h = c.dram("ffn2T", [FH, D], F32, kind="ExternalInput")
    bf2h = c.dram("bffn2_t", [128, 8], F32, kind="ExternalInput")
    x4 = c.dram("x4", [D, T], F32)

    f2 = c.sbuf("pb_f2", [128, 32 * D], BF16)
    bf2 = c.sbuf("pb_bf2", [128, 8], F32)
    b2s = c.sbuf("pb_b2s", [128, 8], F32)
    ght = [c.sbuf("pb_gh0", [128, 32 * 512], BF16)] * 2
    x2f = [c.sbuf("pb_x20", [128, 8 * 512], F32)] * 2
    r3f = [c.sbuf("pb_r30", [128, 8 * 512], F32)] * 2
    rsb = c.sbuf("pb_rsb", [128, 8 * 512], F32)
    tps = c.sbuf("pb_tps", [128, 512], F32)
    lt = ln_tiles(c, "pb")
    ps_p = [c.psum(f"pb_psp{i}", [128, 512], F32) for i in range(2)]

    for kc in range(32):
        c.dma("gpsimd", f2[:, kc * D:(kc + 1) * D], f2h[kc * 128:(kc + 1) * 128, :],
              sem="pb_w")
    c.dma("sync", bf2[:], bf2h, sem="pb_g")
    c.wait("scalar", "pb_g", 16)
    c.wait("scalar", "v_init", 1)
    c.op("scalar", "mul", b2s[:], bf2[:], 0.1)
    c.drain("scalar")
    c.sem_inc("scalar", "pb_b", 1)
    c.wait("tensor", "pb_w", 16 * 32)
    c.wait("vector", "pb_b", 1)
    c.wait("tensor", "v_init", 1)

    for qc in range(NTC):
        if qc >= 1:
            c.wait("sync", "pb_pm", 8 * qc)
            c.wait("sync", "pb_vr", 8 * qc)
        gs = gh[:, qc * TC:(qc + 1) * TC].rearrange("(kc p) t -> p kc t", p=128)
        c.dma("sync", ght[qc % 2][:].rearrange("p (kc t) -> p kc t", kc=32), gs,
              sem="pb_gx")
        xs = x2[:, qc * TC:(qc + 1) * TC].rearrange("(kc p) t -> p kc t", p=128)
        c.dma("sync", x2f[qc % 2][:].rearrange("p (kc t) -> p kc t", kc=8), xs,
              sem="pb_xx")
        rs = r3d[:, qc * TC:(qc + 1) * TC].rearrange("(kc p) t -> p kc t", p=128)
        c.dma("sync", r3f[qc % 2][:].rearrange("p (kc t) -> p kc t", kc=8), rs,
              sem="pb_rx")
        c.wait("tensor", "pb_gx", 16 * (qc + 1))
        c.wait("vector", "pb_xx", 16 * (qc + 1))
        c.wait("vector", "pb_rx", 16 * (qc + 1))
        for fo in range(8):
            g = qc * 8 + fo
            if g >= 2:
                c.wait("tensor", "pb_vr", g - 1)
            for kc in range(32):
                c.op("tensor", "matmul", ps_p[fo % 2][:],
                     f2[:, kc * D + fo * 128:kc * D + (fo + 1) * 128],
                     ght[qc % 2][:, kc * 512:(kc + 1) * 512],
                     start=(kc == 0), stop=(kc == 31),
                     inc=("pb_pm", 1) if kc == 31 else None)
            c.wait("vector", "pb_pm", g + 1)
            c.op("vector", "scalar_tensor_tensor", tps[:],
                 ps_p[fo % 2][:], 0.1,
                 r3f[qc % 2][:, fo * 512:(fo + 1) * 512], Op.mult, Op.add)
            c.drain("vector")
            c.op("vector", "scalar_tensor_tensor", rsb[:, fo * 512:(fo + 1) * 512],
                 tps[:], b2s[:, fo:fo + 1],
                 x2f[qc % 2][:, fo * 512:(fo + 1) * 512], Op.add, Op.add)
            c.drain("vector")
            c.sem_inc("vector", "pb_vr", 1)
        emit_ln_tail(
            c, "pb", qc, rsb, lt,
            out_dst=lambda fo, _qc=qc: x4[:, _qc * TC:(_qc + 1) * TC]
                .rearrange("(fc p) t -> fc p t", p=128)[fo],
            rsb_ready=("pb_vr", lambda fo, _qc=qc: _qc * 8 + fo + 1),
        )
    c.wait("sync", "pb_st", 16 * 64)



def ph_l1ffn(c, upto="full"):
    x4 = c.dram("x4", [D, T], F32)
    kh = c.dram("L1_ffn_kernel", [D, D], F32, kind="ExternalInput")
    ah = c.dram("L1_ffn_a", [D, 16], F32, kind="ExternalInput")
    bh = c.dram("L1_ffn_b", [16, D], F32, kind="ExternalInput")
    bih = c.dram("bias_t", [128, 8], F32, kind="ExternalInput")
    x5 = c.dram("x5", [D, T], F32)

    kf = c.sbuf("pc_kf", [128, 8 * D], F32)
    wq = c.sbuf("pc_wq", [128, 8 * D], BF16)
    asb = c.sbuf("pc_a", [128, 8 * 16], BF16)
    bsb = c.sbuf("pc_b", [16, D], BF16)
    bi = c.sbuf("pc_bi", [128, 8], F32)
    amax = c.sbuf("pc_amax", [1, 1], F32)
    srow = c.sbuf("pc_srow", [1, 1], F32)
    rrow = c.sbuf("pc_rrow", [1, 1], F32)
    sc128 = c.sbuf("pc_sc", [128, 1], F32)
    rc128 = c.sbuf("pc_rc", [128, 1], F32)
    qt = c.sbuf("pc_qt", [128, D], F32)
    x4b = [c.sbuf(f"pc_x4b{i}", [128, 8 * 512], BF16) for i in range(2)]
    x4f = [c.sbuf(f"pc_x4f{i}", [128, 8 * 512], F32) for i in range(2)]
    t1 = c.sbuf("pc_t1", [16, 512], BF16)
    rsb = c.sbuf("pc_rsb", [128, 8 * 512], F32)
    lt = ln_tiles(c, "pc")
    ps_p = [c.psum(f"pc_psp{i}", [128, 512], F32) for i in range(2)]
    ps_l = c.psum("pc_psl", [128, 512], F32)
    onesr = None

    for kc in range(8):
        c.dma("sync", kf[:, kc * D:(kc + 1) * D], kh[kc * 128:(kc + 1) * 128, :],
              sem="pc_k")
    c.dma("gpsimd", asb[:].rearrange("p (kc r) -> p kc r", kc=8),
          ah.rearrange("(kc p) r -> p kc r", p=128), sem="pc_k")
    c.dma("gpsimd", bsb[:], bh, sem="pc_k")
    c.dma("sync", bi[:], bih, sem="pc_k")
    c.wait("gpsimd", "pc_k", 16 * 11)
    c.op("gpsimd", "tensor_reduce", amax[:], kf[:], AX.XYZWC, Op.abs_max)
    c.drain("gpsimd")
    c.sem_inc("gpsimd", "pc_am", 1)
    c.wait("scalar", "pc_am", 1)
    c.op("scalar", "mul", srow[:], amax[:], 1.0 / 7.0)
    c.drain("scalar")
    c.sem_inc("scalar", "pc_s", 1)
    c.wait("vector", "pc_s", 1)
    c.op("vector", "reciprocal", rrow[:], srow[:])
    c.drain("vector")
    c.sem_inc("vector", "pc_r", 1)
    c.wait("tensor", "v_init", 1)
    c.wait("tensor", "pc_r", 1)
    onesr = c.tiles["c_onesr"]
    c.op("tensor", "matmul", ps_l[:, 0:1], onesr[:], srow[:], start=True, stop=True)
    c.op("tensor", "matmul", ps_l[:, 1:2], onesr[:], rrow[:], start=True, stop=True,
         inc=("pc_pb", 1))
    c.wait("vector", "pc_pb", 1)
    c.op("vector", "tensor_copy", sc128[:], ps_l[:, 0:1])
    c.op("vector", "tensor_copy", rc128[:], ps_l[:, 1:2])
    c.drain("vector")
    for kc in range(8):
        c.op("vector", "tensor_scalar", qt[:], kf[:, kc * D:(kc + 1) * D],
             rc128[:, 0:1], None, Op.mult)
        c.drain("vector")
        c.op("vector", "tensor_scalar", qt[:], qt[:], 12582912.0, None, Op.add)
        c.drain("vector")
        c.op("vector", "tensor_scalar", qt[:], qt[:], 12582912.0, None, Op.subtract)
        c.drain("vector")
        c.op("vector", "tensor_scalar", qt[:], qt[:], 7.0, -8.0, Op.min, Op.max)
        c.drain("vector")
        c.op("vector", "tensor_scalar", wq[:, kc * D:(kc + 1) * D], qt[:],
             sc128[:, 0:1], None, Op.mult)
        c.drain("vector")
    c.sem_inc("vector", "pc_q", 1)
    c.wait("tensor", "pc_q", 1)
    if upto == "quant":
        wqd = c.dram("wq_dbg", [D, D], BF16, kind="ExternalOutput")
        dbg = c.dram("dbg", [128, 8], F32, kind="ExternalOutput")
        c.wait("sync", "pc_q", 1)
        for kc in range(8):
            c.dma("sync", wqd[kc * 128:(kc + 1) * 128, :], wq[:, kc * D:(kc + 1) * D],
                  sem="pc_dbg")
        dbt = c.sbuf("pc_dbt", [128, 8], F32)
        c.op("vector", "memset", dbt[:], 0.0)
        c.drain("vector")
        c.op("vector", "tensor_copy", dbt[0:1, 0:1], amax[:])
        c.op("vector", "tensor_copy", dbt[0:1, 1:2], srow[:])
        c.op("vector", "tensor_copy", dbt[0:1, 2:3], rrow[:])
        c.op("vector", "tensor_copy", dbt[:, 3:4], sc128[:])
        c.op("vector", "tensor_copy", dbt[:, 4:5], rc128[:])
        c.drain("vector")
        c.sem_inc("vector", "pc_dv", 1)
        c.wait("sync", "pc_dv", 1)
        c.dma("sync", dbg, dbt[:], sem="pc_dbg")
        c.wait("sync", "pc_dbg", 16 * 9)
        return

    for qc in range(NTC):
        if qc >= 2:
            c.wait("gpsimd", "pc_pm", 8 * (qc - 1))
            c.wait("sync", "pc_pm", 8 * (qc - 1))
        xs = x4[:, qc * TC:(qc + 1) * TC].rearrange("(kc p) t -> p kc t", p=128)
        c.dma("gpsimd", x4b[qc % 2][:].rearrange("p (kc t) -> p kc t", kc=8), xs,
              sem="pc_xb")
        c.dma("sync", x4f[qc % 2][:].rearrange("p (kc t) -> p kc t", kc=8), xs,
              sem="pc_xf")
        c.wait("tensor", "pc_xb", 16 * (qc + 1))
        c.wait("vector", "pc_xf", 16 * (qc + 1))
        if qc >= 1:
            c.wait("tensor", "pc_l1v", qc)
        for kc in range(8):
            c.op("tensor", "matmul", ps_l[0:16, 0:512],
                 asb[:, kc * 16:(kc + 1) * 16],
                 x4b[qc % 2][:, kc * 512:(kc + 1) * 512],
                 start=(kc == 0), stop=(kc == 7),
                 inc=("pc_l1", 1) if kc == 7 else None)
        c.wait("vector", "pc_l1", qc + 1)
        if qc >= 1:
            c.wait("vector", "pc_pm", 8 * qc)
        c.op("vector", "tensor_copy", t1[:], ps_l[0:16, 0:512])
        c.drain("vector")
        c.sem_inc("vector", "pc_l1v", 1)
        c.wait("tensor", "pc_l1v", qc + 1)
        if upto == "lora1":
            continue
        for fo in range(8):
            g = qc * 8 + fo
            if g >= 2:
                c.wait("tensor", "pc_vr", g - 1)
            for kc in range(8):
                c.op("tensor", "matmul", ps_p[fo % 2][:],
                     wq[:, kc * D + fo * 128:kc * D + (fo + 1) * 128],
                     x4b[qc % 2][:, kc * 512:(kc + 1) * 512],
                     start=(kc == 0), stop=False)
            c.op("tensor", "matmul", ps_p[fo % 2][:],
                 bsb[:, fo * 128:(fo + 1) * 128], t1[:],
                 start=False, stop=True, inc=("pc_pm", 1))
            c.wait("vector", "pc_pm", g + 1)
            c.op("vector", "scalar_tensor_tensor", rsb[:, fo * 512:(fo + 1) * 512],
                 ps_p[fo % 2][:], bi[:, fo:fo + 1],
                 x4f[qc % 2][:, fo * 512:(fo + 1) * 512], Op.add, Op.add)
            c.drain("vector")
            c.sem_inc("vector", "pc_vr", 1)
        emit_ln_tail(
            c, "pc", qc, rsb, lt,
            out_dst=lambda fo, _qc=qc: x5[:, _qc * TC:(_qc + 1) * TC]
                .rearrange("(fc p) t -> fc p t", p=128)[fo],
            rsb_ready=("pc_vr", lambda fo, _qc=qc: _qc * 8 + fo + 1),
        )
    if upto == "full":
        c.wait("sync", "pc_st", 16 * 64)



def ph_outT(c):
    x5 = c.dram("x5", [D, T], F32)
    out = c.dram("y_out", [T, D], F32, kind="ExternalOutput")

    idn = c.tiles["c_idnf"]
    x5f = [c.sbuf(f"pd_x5{i}", [128, 8 * 512], F32) for i in range(2)]
    ostg = [c.sbuf(f"pd_o{i}", [128, D], F32) for i in range(2)]
    ps_t = [c.psum(f"pd_ps{i}", [128, 128], F32) for i in range(2)]

    c.wait("tensor", "v_init", 1)
    j = 0
    gi = 0
    for qc in range(NTC):
        if qc >= 2:
            c.wait("sync", "pd_tr", 32 * (qc - 1))
        xs = x5[:, qc * TC:(qc + 1) * TC].rearrange("(kc p) t -> p kc t", p=128)
        c.dma("sync", x5f[qc % 2][:].rearrange("p (kc t) -> p kc t", kc=8), xs,
              sem="pd_x")
        c.wait("tensor", "pd_x", 16 * (qc + 1))
        for i in range(4):
            for fc in range(8):
                if j >= 2:
                    c.wait("tensor", "pd_v", j - 1)
                c.op("tensor", "transpose", ps_t[j % 2][:],
                     x5f[qc % 2][:, fc * 512 + i * 128:fc * 512 + (i + 1) * 128],
                     idn[:], inc=("pd_tr", 1))
                c.wait("vector", "pd_tr", j + 1)
                if gi >= 2 and fc == 0:
                    c.wait("vector", "pd_st", 16 * (gi - 1))
                c.op("vector", "tensor_copy",
                     ostg[gi % 2][:, fc * 128:(fc + 1) * 128], ps_t[j % 2][:])
                c.drain("vector")
                c.sem_inc("vector", "pd_v", 1)
                j += 1
            c.wait("sync", "pd_v", j)
            c.dma("sync", out[qc * TC + i * 128:qc * TC + (i + 1) * 128, :],
                  ostg[gi % 2][:], sem="pd_st")
            gi += 1
    c.wait("sync", "pd_st", 16 * 32)



def build_program(phase_fns, kinds=None):
    nc = bass.Bass("TRN2", target_bir_lowering=False)
    es = contextlib.ExitStack()
    c = Ctx(nc, es, kinds)
    setup_common(c)
    for fn in phase_fns:
        fn(c)
    block = es.enter_context(nc.Block())
    finalize(c, block)
    es.close()
    return nc


def build_fused(nphases=None, out_kinds=None):
    nc = bass.Bass("TRN2", target_bir_lowering=False)
    es = contextlib.ExitStack()
    kinds = dict(out_kinds or {"y_out": "ExternalOutput"})
    c = Ctx(nc, es, kinds, fused=True)
    setup_common(c)
    for name, fn, ins, outs in PHASES[:nphases]:
        fn(c)
        c.end_phase()
    block = es.enter_context(nc.Block())
    finalize(c, block)
    es.close()
    return nc


def host_layouts(inputs):
    f32 = np.float32
    g = {k: np.asarray(v, f32) for k, v in inputs.items() if k != "L0_attn_x"}
    out = {}
    out["L0_attn_w_qkv"] = np.ascontiguousarray(g["L0_attn_w_qkv"])
    out["b_qkv_t"] = np.ascontiguousarray(g["L0_attn_b_qkv"].reshape(24, 128).T)
    out["b_qkv_r"] = np.ascontiguousarray(g["L0_attn_b_qkv"].reshape(1, 3 * D))
    out["L0_attn_w_proj"] = np.ascontiguousarray(g["L0_attn_w_proj"])
    out["b_proj_t"] = np.ascontiguousarray(g["L0_attn_b_proj"].reshape(8, 128).T)
    out["w1T"] = np.ascontiguousarray(g["L0_ffn_w1_w"].T)
    out["w2T"] = np.ascontiguousarray(g["L0_ffn_w2_w"].T)
    out["w1s_t"] = np.ascontiguousarray(g["L0_ffn_w1_s"].reshape(32, 128).T)
    out["w2s_t"] = np.ascontiguousarray(g["L0_ffn_w2_s"].reshape(32, 128).T)
    out["h1b"] = np.ascontiguousarray(
        g["L0_ffn_h1_w"].T.reshape(32, 128, 32, 128).transpose(2, 0, 1, 3))
    out["h2b"] = np.ascontiguousarray(
        g["L0_ffn_h2_w"].T.reshape(32, 128, 32, 128).transpose(2, 0, 1, 3))
    out["h1s_t"] = np.ascontiguousarray(g["L0_ffn_h1_s"].reshape(32, 128).T)
    out["h2s_t"] = np.ascontiguousarray(g["L0_ffn_h2_s"].reshape(32, 128).T)
    out["w3T"] = np.ascontiguousarray(g["L0_ffn_w3_w"].T)
    out["w3s_t"] = np.ascontiguousarray(g["L0_ffn_w3_s"].reshape(8, 128).T)
    out["W_toT"] = np.ascontiguousarray(g["L1_attn_W_to"].T)
    out["W_fromT"] = np.ascontiguousarray(g["L1_attn_W_from"].T)
    out["freq2"] = np.ascontiguousarray(g["L1_attn_frequencies"].reshape(2, 128).T)
    out["ldec2"] = np.ascontiguousarray(g["L1_attn_log_decay"].reshape(2, 128).T)
    out["ln1g_t"] = np.ascontiguousarray(g["L1_attn_ln1_g"].reshape(8, 128).T)
    out["ln1b_t"] = np.ascontiguousarray(g["L1_attn_ln1_b"].reshape(8, 128).T)
    out["ln3g_t"] = np.ascontiguousarray(g["L1_attn_ln3_g"].reshape(8, 128).T)
    out["ln3b_t"] = np.ascontiguousarray(g["L1_attn_ln3_b"].reshape(8, 128).T)
    out["ffn1T"] = np.ascontiguousarray(g["L1_attn_W_ffn1"].T)
    out["bffn1_t"] = np.ascontiguousarray(g["L1_attn_b_ffn1"].reshape(32, 128).T)
    out["ffn2T"] = np.ascontiguousarray(g["L1_attn_W_ffn2"].T)
    out["bffn2_t"] = np.ascontiguousarray(g["L1_attn_b_ffn2"].reshape(8, 128).T)
    out["L1_ffn_kernel"] = np.ascontiguousarray(g["L1_ffn_kernel"])
    out["L1_ffn_a"] = np.ascontiguousarray(g["L1_ffn_a"])
    out["L1_ffn_b"] = np.ascontiguousarray(g["L1_ffn_b"])
    out["bias_t"] = np.ascontiguousarray(g["L1_ffn_bias"].reshape(8, 128).T)
    return out


PHASES = [
    ("qkv", ph_qkv, [], ["Qf", "Kf", "Vt"]),
    ("attn", ph_attn, ["Qf", "Kf", "Vt"], ["attnf"]),
    ("proj", ph_proj, ["attnf"], ["x1"]),
    ("ffnA", ph_ffnA, ["x1"], ["hA"]),
    ("h1", ph_h1, ["hA"], ["hB"]),
    ("h2", ph_h2, ["hB"], ["hC"]),
    ("w3ln", ph_w3ln, ["hC", "x1"], ["x2"]),
    ("lnwto", ph_lnwto, ["x2"], ["ubuf"]),
    ("scan", ph_scan, ["ubuf"], ["spec"]),
    ("wfrom", ph_wfrom_ffn1, ["spec", "x2"], ["r3", "gh"]),
    ("ffn2", ph_ffn2ln, ["gh", "x2", "r3"], ["x4"]),
    ("l1ffn", ph_l1ffn, ["x4"], ["x5"]),
    ("outT", ph_outT, ["x5"], ["y_out"]),
]

PHASE_EXT = {
    "qkv": ["x_in", "L0_attn_w_qkv", "b_qkv_t", "b_qkv_r"],
    "attn": [],
    "proj": ["x_in", "L0_attn_w_proj", "b_proj_t"],
    "ffnA": ["w1T", "w2T", "w1s_t", "w2s_t"],
    "h1": ["h1b", "h1s_t"],
    "h2": ["h2b", "h2s_t"],
    "w3ln": ["w3T", "w3s_t"],
    "lnwto": ["W_toT", "ln1g_t", "ln1b_t", "freq2"],
    "scan": ["ldec2", "freq2"],
    "wfrom": ["W_fromT", "ln3g_t", "ln3b_t", "ffn1T", "bffn1_t"],
    "ffn2": ["ffn2T", "bffn2_t"],
    "l1ffn": ["L1_ffn_kernel", "L1_ffn_a", "L1_ffn_b", "bias_t"],
    "outT": [],
}


FUSED = True


def kernel(**inputs):
    xfull = np.ascontiguousarray(np.asarray(inputs["L0_attn_x"], np.float32))
    hl = host_layouts(inputs)
    xs = [np.ascontiguousarray(xfull[b]) for b in range(NCORES)]

    if FUSED:
        nc = build_fused()
        ext = sorted({k for v in PHASE_EXT.values() for k in v})
        in_maps = []
        for b in range(NCORES):
            m = {k: hl[k] for k in ext if k != "x_in"}
            m["x_in"] = xs[b]
            in_maps.append(m)
        res = run_bass_kernel_spmd(nc, in_maps, core_ids=list(range(NCORES)))
        out = np.stack([res.results[b]["y_out"] for b in range(NCORES)], axis=0)
        return out.astype(np.float32)

    state = [dict() for _ in range(NCORES)]
    for b in range(NCORES):
        state[b]["x_in"] = xs[b]

    for name, fn, ins, outs in PHASES:
        kinds = {k: "ExternalInput" for k in ins}
        kinds.update({k: "ExternalOutput" for k in outs})
        nc = build_program([fn], kinds)
        in_maps = []
        for b in range(NCORES):
            m = {}
            for k in PHASE_EXT[name]:
                m[k] = state[b][k] if k in state[b] else hl[k]
            for k in ins:
                m[k] = state[b][k]
            in_maps.append(m)
        res = run_bass_kernel_spmd(nc, in_maps, core_ids=list(range(NCORES)))
        for b in range(NCORES):
            for k in outs:
                state[b][k] = res.results[b][k]

    out = np.stack([state[b]["y_out"] for b in range(NCORES)], axis=0)
    return out.astype(np.float32)


# version 05
# speedup vs baseline: 1.1399x; 1.1363x over previous
import contextlib
import numpy as np
import ml_dtypes
import concourse.bass as bass
import concourse.mybir as mybir
from concourse.bass_utils import run_bass_kernel_spmd
from concourse.alu_op_type import AluOpType as Op

F32 = mybir.dt.float32
BF16 = mybir.dt.bfloat16
AF = mybir.ActivationFunctionType
AX = mybir.AxisListType

T = 4096
D = 1024
H = 16
HD = 64
FH = 4096
FR = 256
NCORES = 8
TC = 512
NTC = T // TC
ENGINES = ("sync", "vector", "scalar", "tensor", "gpsimd")


ARENA_F32 = 49800
GLOBAL_SEMS = ("g_idn", "v_init", "bar", "bar2")


class Ctx:
    def __init__(self, nc, es, kinds=None, fused=False):
        self.nc = nc
        self.es = es
        self.kinds = kinds or {}
        self.fused = fused
        self.prog = {e: [] for e in ENGINES}
        self.hbm = {}
        self.tiles = {}
        self.psums = {}
        self.sems = {}
        self.cnt = {}
        if fused:
            self.arena = es.enter_context(nc.sbuf_tensor("ARENA", [128, ARENA_F32], F32))
            self.parena = es.enter_context(nc.psum_tensor("PARENA", [128, 4096], F32))
            self.sp = 0
            self.pp = 0
            self.sem_pool = [es.enter_context(nc.semaphore(f"S{i}")) for i in range(90)]
            self.sem_rr = 0
            self.sem_phys = {}
            self.sem_base = {}
            self.phys_cnt = [0] * 90
            self.phase_idx = 0

    def dram(self, name, shape, dtype, kind="Internal"):
        kind = self.kinds.get(name, kind)
        if name not in self.hbm:
            self.hbm[name] = self.nc.dram_tensor(name, list(shape), dtype, kind=kind).ap()
        return self.hbm[name]

    def sbuf(self, name, shape, dtype):
        if name in self.tiles:
            return self.tiles[name]
        if self.fused and not name.startswith("c_"):
            rows, n = shape
            nf = n if dtype == F32 else (n + 1) // 2
            assert self.sp + nf <= ARENA_F32, f"SBUF arena overflow at {name} ({self.sp}+{nf})"
            v = self.arena[:, self.sp:self.sp + nf]
            self.sp += nf
            if dtype != F32:
                v = v.bitcast(dtype)[:, 0:n]
            if rows != 128:
                v = v[0:rows, :]
            self.tiles[name] = v
        else:
            self.tiles[name] = self.es.enter_context(
                self.nc.sbuf_tensor(name, list(shape), dtype))
        return self.tiles[name]

    def psum(self, name, shape, dtype=F32):
        if name in self.psums:
            return self.psums[name]
        if self.fused:
            rows, n = shape
            assert dtype == F32
            nal = ((n + 511) // 512) * 512
            assert self.pp + nal <= 4096, f"PSUM arena overflow at {name}"
            v = self.parena[:, self.pp:self.pp + n]
            self.pp += nal
            if rows != 128:
                v = v[0:rows, :]
            self.psums[name] = v
        else:
            self.psums[name] = self.es.enter_context(
                self.nc.psum_tensor(name, list(shape), dtype))
        return self.psums[name]

    def sem(self, name):
        if name not in self.sems:
            if self.fused:
                if name in GLOBAL_SEMS:
                    idx = 86 + GLOBAL_SEMS.index(name)
                else:
                    idx = self.sem_rr % 86
                    self.sem_rr += 1
                self.sem_phys[name] = idx
                self.sem_base[name] = self.phys_cnt[idx]
                self.sems[name] = self.sem_pool[idx]
            else:
                self.sems[name] = self.es.enter_context(self.nc.semaphore(name))
            self.cnt[name] = 0
        return self.sems[name]

    def end_phase(self):
        assert self.fused
        pidx = self.phase_idx
        for e in ENGINES:
            if e in ("vector", "scalar", "gpsimd"):
                self.drain(e)
            if e != "tensor":
                self.sem_inc(e, "bar", 1)
        for e in ENGINES:
            self.wait(e, "bar", 4 * (pidx + 1))
        for name in [n for n in self.sems if n not in GLOBAL_SEMS]:
            assert self.phys_cnt[self.sem_phys[name]] < 60000, f"sem value overflow risk {name}"
            del self.sems[name]
            del self.cnt[name]
            del self.sem_base[name]
            del self.sem_phys[name]
        for k in [k for k in self.tiles if not k.startswith("c_")]:
            del self.tiles[k]
        self.psums = {}
        self.sp = 0
        self.pp = 0
        self.phase_idx += 1

    def inc(self, name, n=1):
        self.sem(name)
        self.cnt[name] += n
        if self.fused:
            self.phys_cnt[self.sem_phys[name]] += n
        return self.cnt[name]

    def val(self, name):
        self.sem(name)
        return self.cnt[name]

    def emit(self, engine, f):
        self.prog[engine].append(f)

    def wait(self, engine, semname, value):
        s = self.sem(semname)
        if self.fused:
            value = value + self.sem_base[semname]
        self.emit(engine, lambda eng, _s=s, _v=value: eng.wait_ge(_s, _v))

    def op(self, engine, fname, *args, inc=None, **kw):
        if inc is not None:
            tot = self.inc(inc[0], inc[1])
            s = self.sems[inc[0]]

            def f(eng, _f=fname, _a=args, _k=kw, _s=s, _n=inc[1]):
                getattr(eng, _f)(*_a, **_k).then_inc(_s, _n)
            self.emit(engine, f)
            return tot

        def f(eng, _f=fname, _a=args, _k=kw):
            getattr(eng, _f)(*_a, **_k)
        self.emit(engine, f)
        return None

    def dma(self, engine, dst, src, sem="ld", **kw):
        return self.op(engine, "dma_start", inc=(sem, 16), out=dst, in_=src, **kw)

    def drain(self, engine):
        self.emit(engine, lambda eng: eng.drain())

    def sem_inc(self, engine, semname, n=1):
        tot = self.inc(semname, n)
        s = self.sems[semname]
        assert engine != "tensor", "standalone sem_inc on PE is device-fatal"
        self.emit(engine, lambda eng, _s=s, _n=n: eng.sem_inc(_s, _n))
        return tot


def finalize(ctx, block):
    @block.sync
    def _(eng):
        for f in ctx.prog["sync"]:
            f(eng)

    @block.vector
    def _(eng):
        for f in ctx.prog["vector"]:
            f(eng)

    @block.scalar
    def _(eng):
        for f in ctx.prog["scalar"]:
            f(eng)

    @block.tensor
    def _(eng):
        for f in ctx.prog["tensor"]:
            f(eng)

    @block.gpsimd
    def _(eng):
        for f in ctx.prog["gpsimd"]:
            f(eng)


def setup_common(c):
    iot = c.sbuf("c_iot", [128, 128], F32)
    idn = c.sbuf("c_idnf", [128, 128], F32)
    idnb = c.sbuf("c_idnb", [128, 128], BF16)
    onesr = c.sbuf("c_onesr", [1, 128], F32)
    ones128 = c.sbuf("c_ones128", [128, 1], F32)
    sclr = c.sbuf("c_sclr", [1, 128], F32)
    c.op("gpsimd", "iota", iot[:], pattern=[[1, 128]], base=0,
         channel_multiplier=-1, allow_small_or_imprecise_dtypes=True, inc=("g_idn", 1))
    c.wait("vector", "g_idn", 1)
    c.op("vector", "tensor_scalar", idn[:], iot[:], 0.0, None, Op.is_equal)
    c.op("vector", "tensor_scalar", idnb[:], iot[:], 0.0, None, Op.is_equal)
    c.op("vector", "memset", onesr[:], 1.0)
    c.op("vector", "memset", ones128[:], 1.0)
    c.op("vector", "memset", sclr[:], -0.125)
    eps = c.sbuf("c_eps", [128, 1], F32)
    c.op("vector", "memset", eps[:], 1e-5)
    c.drain("vector")
    c.sem_inc("vector", "v_init", 1)



def ph_qkv(c, scratch_kind="ExternalOutput"):
    x = c.dram("x_in", [T, D], F32, kind="ExternalInput")
    wq = c.dram("L0_attn_w_qkv", [D, 3 * D], F32, kind="ExternalInput")
    bqt_h = c.dram("b_qkv_t", [128, 24], F32, kind="ExternalInput")
    bqr_h = c.dram("b_qkv_r", [1, 3 * D], F32, kind="ExternalInput")
    Qf = c.dram("Qf", [D, T], BF16, kind=scratch_kind)
    Kf = c.dram("Kf", [D, T], BF16, kind=scratch_kind)
    Vt = c.dram("Vt", [T, D], BF16, kind=scratch_kind)

    idn = c.tiles["c_idnf"]
    onesr = c.tiles["c_onesr"]
    wsb = c.sbuf("q_w", [128, 8 * 3072], BF16)
    bqt = c.sbuf("q_bt", [128, 24], F32)
    bvrow = c.sbuf("q_bvrow", [1, D], F32)
    vbias = c.sbuf("q_vbias", [128, D], F32)
    xtok = [c.sbuf(f"q_xtok{i}", [128, 4 * D], F32) for i in range(2)]
    xfm = c.sbuf("q_xfm", [128, 8 * TC], BF16)
    stg = [c.sbuf(f"q_stg{i}", [128, TC], BF16) for i in range(2)]
    vstg = [c.sbuf(f"q_vstg{i}", [128, D], BF16) for i in range(2)]

    ps_tr = [c.psum(f"q_pstr{i}", [128, 128], F32) for i in range(2)]
    ps_mm = [c.psum(f"q_psmm{i}", [128, TC], F32) for i in range(2)]
    ps_v = [c.psum(f"q_psv{i}", [128, D], F32) for i in range(2)]

    for kc in range(8):
        c.dma("gpsimd", wsb[:, kc * 3072:(kc + 1) * 3072],
              wq[kc * 128:(kc + 1) * 128, :], sem="q_w")
    c.dma("sync", bqt[:], bqt_h, sem="q_b")
    c.dma("sync", bvrow[:], bqr_h[:, 2 * D:3 * D], sem="q_br")

    c.wait("tensor", "v_init", 1)
    c.wait("tensor", "q_br", 16)
    for half in range(2):
        c.op("tensor", "matmul", ps_v[0][:, half * 512:(half + 1) * 512],
             onesr[:], bvrow[:, half * 512:(half + 1) * 512],
             start=True, stop=True, inc=("q_pev", 1) if half == 1 else None)
    c.wait("vector", "q_pev", 1)
    c.op("vector", "tensor_copy", vbias[:], ps_v[0][:])
    c.drain("vector")
    c.sem_inc("vector", "q_vv", 1)

    c.wait("scalar", "q_b", 16)
    c.wait("tensor", "q_w", 16 * 8)

    jtr = 0
    gm = 0
    gv = 0
    for tc in range(NTC):
        if tc >= 2:
            c.wait("sync", "q_pet", 32 * (tc - 1))
        src = x[tc * TC:(tc + 1) * TC, :].rearrange("(i p) f -> p i f", p=128)
        c.dma("sync", xtok[tc % 2][:].rearrange("p (i f) -> p i f", i=4), src, sem="q_x")

        c.wait("tensor", "q_x", 16 * (tc + 1))
        for i in range(4):
            for fc in range(8):
                if jtr >= 2:
                    c.wait("tensor", "q_vx", jtr - 1)
                c.op("tensor", "transpose", ps_tr[jtr % 2][:],
                     xtok[tc % 2][:, i * D + fc * 128:i * D + (fc + 1) * 128],
                     idn[:], inc=("q_pet", 1))
                c.wait("vector", "q_pet", jtr + 1)
                c.op("vector", "tensor_copy",
                     xfm[:, fc * TC + i * 128:fc * TC + (i + 1) * 128],
                     ps_tr[jtr % 2][:])
                c.drain("vector")
                c.sem_inc("vector", "q_vx", 1)
                jtr += 1

        for m in range(16):
            if gm >= 2:
                c.wait("tensor", "q_a", gm - 1)
            for kc in range(8):
                c.op("tensor", "matmul", ps_mm[gm % 2][:],
                     wsb[:, kc * 3072 + m * 128:kc * 3072 + (m + 1) * 128],
                     xfm[:, kc * TC:(kc + 1) * TC],
                     start=(kc == 0), stop=(kc == 7),
                     inc=("q_pem", 1) if kc == 7 else None)
            c.wait("scalar", "q_pem", gm + 1)
            if gm >= 2:
                c.wait("scalar", "q_st", 16 * (gm - 1))
            c.op("scalar", "add", stg[gm % 2][:], ps_mm[gm % 2][:],
                 bqt[:, m:m + 1])
            c.drain("scalar")
            c.sem_inc("scalar", "q_a", 1)
            c.wait("sync", "q_a", gm + 1)
            dst = (Qf if m < 8 else Kf)
            mm = m if m < 8 else m - 8
            c.dma("sync", dst[mm * 128:(mm + 1) * 128, tc * TC:(tc + 1) * TC],
                  stg[gm % 2][:], sem="q_st")
            gm += 1

        for i in range(4):
            if gv >= 2:
                c.wait("tensor", "q_vv", 1 + (gv - 1))
            for kc in range(8):
                for half in range(2):
                    c.op("tensor", "matmul",
                         ps_v[gv % 2][:, half * 512:(half + 1) * 512],
                         xfm[:, kc * TC + i * 128:kc * TC + (i + 1) * 128],
                         wsb[:, kc * 3072 + 2048 + half * 512:kc * 3072 + 2048 + (half + 1) * 512],
                         start=(kc == 0), stop=(kc == 7),
                         inc=("q_pev", 1) if (kc == 7 and half == 1) else None)
            c.wait("vector", "q_pev", 1 + gv + 1)
            if gv >= 2:
                c.wait("vector", "q_stv", 16 * (gv - 1))
            c.op("vector", "tensor_tensor", vstg[gv % 2][:], ps_v[gv % 2][:],
                 vbias[:], Op.add)
            c.drain("vector")
            c.sem_inc("vector", "q_vv", 1)
            c.wait("sync", "q_vv", 1 + gv + 1)
            c.dma("sync", Vt[tc * TC + i * 128:tc * TC + (i + 1) * 128, :],
                  vstg[gv % 2][:], sem="q_stv")
            gv += 1

    c.wait("sync", "q_st", 16 * 128)
    c.wait("sync", "q_stv", 16 * 32)



def ph_attn(c):
    Qf = c.dram("Qf", [D, T], BF16)
    Kf = c.dram("Kf", [D, T], BF16)
    Vt = c.dram("Vt", [T, D], BF16)
    attnf = c.dram("attnf", [D, T], BF16)

    onesr = c.tiles["c_onesr"]
    ones128 = c.tiles["c_ones128"]
    sclr = c.tiles["c_sclr"]
    qsb = [c.sbuf(f"a_q{i}", [64, T], BF16) for i in range(2)]
    ksb = [c.sbuf(f"a_k{i}", [64, T], BF16) for i in range(2)]
    vaug = [c.sbuf(f"a_v{i}", [128, 32 * 65], BF16) for i in range(2)]
    ssb = [c.sbuf(f"a_s{i}", [128, 16384], BF16) for i in range(2)]
    sqt = c.sbuf("a_sqt", [64, T], F32)
    nmx = c.sbuf("a_nmx", [1, 2], F32)
    ntmp = c.sbuf("a_ntmp", [1, 4], F32)
    negb = [c.sbuf(f"a_negb{i}", [128, 1], F32) for i in range(2)]
    t_r = c.sbuf("a_r", [1, TC], F32)
    t_rb = c.sbuf("a_rb", [64, TC], F32)
    astg = [c.sbuf(f"a_stg{i}", [64, TC], BF16) for i in range(2)]
    R = [c.psum(f"a_R{i}", [128, 1024], F32) for i in range(3)]
    avp = c.psum("a_av", [128, 512], F32)
    nbp = c.psum("a_nb", [128, 512], F32)

    for i in range(2):
        c.op("vector", "memset", vaug[i][:, 64:32 * 65:65], 1.0)
    c.drain("vector")
    c.sem_inc("vector", "a_vone", 1)
    c.wait("gpsimd", "a_vone", 1)

    NIT = H * NTC
    for it in range(NIT + 1):
        if it < NIT:
            h, qc = divmod(it, NTC)
            pair = it % 2
            hb = h % 2
            if qc == 0:
                if h >= 2:
                    c.wait("sync", "a_av", (h - 1) * NTC)
                    c.wait("gpsimd", "a_av", (h - 1) * NTC)
                    c.wait("sync", "a_qk", (h - 1) * NTC * 16)
                c.dma("sync", qsb[hb][:], Qf[64 * h:64 * (h + 1), :], sem="a_ld")
                c.dma("sync", ksb[hb][:], Kf[64 * h:64 * (h + 1), :], sem="a_ld")
                vdst = vaug[hb][:].rearrange("p (kc f) -> p kc f", kc=32)[:, :, 0:64]
                vsrc = Vt[:, 64 * h:64 * (h + 1)].rearrange("(kc p) f -> p kc f", p=128)
                c.dma("gpsimd", vdst, vsrc, sem="a_ld")
                c.wait("scalar", "a_ld", 48 * (h + 1))
                if h >= 1:
                    c.wait("scalar", "a_nb", h)
                    c.wait("vector", "a_lnd", h)
                c.op("vector", "memset", nmx[:], 0.0)
                c.drain("vector")
                c.sem_inc("vector", "a_nz", 1)
                nping = 0
                for side in range(2):
                    src = qsb[hb] if side == 0 else ksb[hb]
                    if side == 1:
                        c.wait("scalar", "a_nm", 16 * h + 8)
                    c.op("scalar", "activation", sqt[:], src[:], AF.Square)
                    c.drain("scalar")
                    c.sem_inc("scalar", "a_sq", 1)
                    c.wait("tensor", "a_sq", 2 * h + side + 1)
                    if it >= 1:
                        c.wait("tensor", "a_fin", it - 1)
                    c.wait("tensor", "a_nz", h + 1)
                    for n in range(8):
                        if nping >= 1:
                            c.wait("tensor", "a_nr", 16 * h + nping)
                        c.op("tensor", "matmul", nbp[0:1, 0:512], ones128[0:64, 0:1],
                             sqt[:, n * 512:(n + 1) * 512], start=True, stop=True,
                             inc=("a_nm", 1))
                        nping += 1
                        c.wait("vector", "a_nm", 16 * h + nping)
                        c.op("vector", "tensor_reduce", ntmp[0:1, 0:1], nbp[0:1, 0:512],
                             AX.X, Op.max)
                        c.drain("vector")
                        c.op("vector", "tensor_tensor", nmx[0:1, side:side + 1],
                             nmx[0:1, side:side + 1], ntmp[0:1, 0:1], Op.max)
                        c.drain("vector")
                        c.sem_inc("vector", "a_nr", 1)
                c.wait("scalar", "a_nr", 16 * (h + 1))
                c.op("scalar", "activation", ntmp[0:1, 2:3], nmx[0:1, 0:1], AF.Ln)
                c.drain("scalar")
                c.op("scalar", "activation", ntmp[0:1, 3:4], nmx[0:1, 1:2], AF.Ln)
                c.drain("scalar")
                c.sem_inc("scalar", "a_lnd", 1)
                c.wait("vector", "a_lnd", h + 1)
                c.op("vector", "tensor_add", ntmp[0:1, 0:1], ntmp[0:1, 2:3], ntmp[0:1, 3:4])
                c.drain("vector")
                c.sem_inc("vector", "a_sumd", 1)
                c.wait("scalar", "a_sumd", h + 1)
                if h >= 1:
                    c.wait("scalar", "a_pb1", h)
                c.op("scalar", "activation", ntmp[0:1, 1:2], ntmp[0:1, 0:1], AF.Exp,
                     scale=0.5)
                c.drain("scalar")
                c.sem_inc("scalar", "a_cb", 1)
                c.wait("tensor", "a_cb", h + 1)
                c.op("tensor", "matmul", nbp[:, 0:1], sclr[:], ntmp[0:1, 1:2],
                     start=True, stop=True, inc=("a_pb1", 1))
                c.wait("vector", "a_pb1", h + 1)
                if h >= 2:
                    c.wait("vector", "a_xv", 128 * (h - 1))
                c.op("vector", "tensor_copy", negb[hb][:], nbp[:, 0:1])
                c.drain("vector")
                c.sem_inc("vector", "a_nb", 1)
            c.wait("tensor", "a_ld", 48 * (h + 1))
            for gg in range(16):
                eg = it * 16 + gg
                r = eg % 3
                if eg >= 3:
                    c.wait("tensor", "a_xv", eg - 2)
                for kk in range(2):
                    kc = 2 * gg + kk
                    c.op("tensor", "matmul", R[r][:, kk * 512:(kk + 1) * 512],
                         ksb[hb][:, kc * 128:(kc + 1) * 128],
                         qsb[hb][:, qc * 512:(qc + 1) * 512],
                         start=True, stop=True,
                         inc=("a_qk", 1) if kk == 1 else None)
                c.wait("scalar", "a_qk", eg + 1)
                if gg == 0:
                    c.wait("scalar", "a_nb", h + 1)
                    if it >= 2:
                        c.wait("scalar", "a_av", it - 1)
                c.op("scalar", "activation", ssb[pair][:, gg * 1024:(gg + 1) * 1024],
                     R[r][:], AF.Exp, bias=negb[hb][:, 0:1], scale=0.125)
                c.drain("scalar")
                c.sem_inc("scalar", "a_xv", 1)

        if it >= 1:
            hq = it - 1
            hB, qcB = divmod(hq, NTC)
            pb = hq % 2
            hbB = hB % 2
            c.wait("tensor", "a_xv", hq * 16 + 16)
            if hq >= 1:
                c.wait("tensor", "a_fin", hq)
            for kc in range(32):
                c.op("tensor", "matmul", avp[0:65, 0:512],
                     vaug[hbB][:, kc * 65:kc * 65 + 65],
                     ssb[pb][:, kc * 512:(kc + 1) * 512],
                     start=(kc == 0), stop=(kc == 31),
                     inc=("a_av", 1) if kc == 31 else None)
            c.wait("vector", "a_av", hq + 1)
            c.op("vector", "reciprocal", t_r[:], avp[64:65, 0:512])
            c.drain("vector")
            c.sem_inc("vector", "a_vr", 1)
            c.wait("tensor", "a_vr", hq + 1)
            c.wait("tensor", "a_rb", hq)
            c.op("tensor", "matmul", nbp[0:64, 0:512], onesr[:, 0:64], t_r[:],
                 start=True, stop=True, inc=("a_pb2", 1))
            c.wait("vector", "a_pb2", hq + 1)
            c.op("vector", "tensor_copy", t_rb[:], nbp[0:64, 0:512])
            c.drain("vector")
            c.sem_inc("vector", "a_rb", 1)
            if hq >= 2:
                c.wait("vector", "a_st", 16 * (hq - 1))
            c.op("vector", "tensor_tensor", astg[hq % 2][:], avp[0:64, 0:512],
                 t_rb[:], Op.mult)
            c.drain("vector")
            c.sem_inc("vector", "a_fin", 1)
            c.wait("sync", "a_fin", hq + 1)
            c.dma("sync", attnf[64 * hB:64 * (hB + 1), qcB * 512:(qcB + 1) * 512],
                  astg[hq % 2][:], sem="a_st")

    c.wait("sync", "a_st", 16 * NIT)



def emit_ln_tail(c, P, qi, rsb, tiles, out_dst, gslice=None, bslice=None,
                 rsb_ready=None, bfstg=None, bf_dst=None, eps=1e-5, nfo=8):
    (sq1, mrow, sq2, vrow, lnrow, rstdrow, xstg, ps_s0, ps_s1, ps_b0, ps_b1) = tiles
    ones128 = c.tiles["c_ones128"]
    onesr = c.tiles["c_onesr"]
    for fo in range(nfo):
        if rsb_ready is not None:
            c.wait("scalar", rsb_ready[0], rsb_ready[1](fo))
        if qi * nfo + fo >= 1:
            c.wait("scalar", f"{P}_psq", qi * nfo + fo)
        c.op("scalar", "activation", sq1[:], rsb[:, fo * 512:(fo + 1) * 512], AF.Square)
        c.drain("scalar")
        c.sem_inc("scalar", f"{P}_sq", 1)
        if rsb_ready is not None:
            c.wait("tensor", rsb_ready[0], rsb_ready[1](fo))
        if qi > 0 or fo > 0:
            c.wait("tensor", f"{P}_row", qi)
        c.op("tensor", "matmul", ps_s0[0:1, :], ones128[:], rsb[:, fo * 512:(fo + 1) * 512],
             start=(fo == 0), stop=(fo == nfo - 1),
             inc=(f"{P}_ps", 1) if fo == nfo - 1 else None)
        c.wait("tensor", f"{P}_sq", qi * nfo + fo + 1)
        c.op("tensor", "matmul", ps_s1[0:1, :], ones128[:], sq1[:],
             start=(fo == 0), stop=(fo == nfo - 1),
             inc=(f"{P}_psq", 1))
    c.wait("scalar", f"{P}_ps", qi + 1)
    c.op("scalar", "mul", mrow[0:1, :], ps_s0[0:1, :], 1.0 / D)
    c.drain("scalar")
    c.sem_inc("scalar", f"{P}_m", 1)
    c.wait("vector", f"{P}_m", qi + 1)
    c.wait("vector", f"{P}_psq", nfo * (qi + 1))
    c.op("vector", "tensor_mul", sq2[0:1, :], mrow[0:1, :], mrow[0:1, :])
    c.drain("vector")
    c.op("vector", "scalar_tensor_tensor", vrow[0:1, :], ps_s1[0:1, :], 1.0 / D,
         sq2[0:1, :], Op.mult, Op.subtract)
    c.drain("vector")
    c.sem_inc("vector", f"{P}_v", 1)
    c.wait("scalar", f"{P}_v", qi + 1)
    c.op("scalar", "activation", lnrow[0:1, :], vrow[0:1, :], AF.Ln,
         bias=c.tiles["c_eps"][0:1, 0:1])
    c.drain("scalar")
    c.op("scalar", "activation", rstdrow[0:1, :], lnrow[0:1, :], AF.Exp, scale=-0.5)
    c.drain("scalar")
    c.sem_inc("scalar", f"{P}_row", 1)
    c.wait("tensor", f"{P}_row", qi + 1)
    c.op("tensor", "matmul", ps_b0[:, 0:512], onesr[:], mrow[0:1, :],
         start=True, stop=True)
    c.op("tensor", "matmul", ps_b1[:, 0:512], onesr[:], rstdrow[0:1, :],
         start=True, stop=True, inc=(f"{P}_bc", 1))
    for fo in range(nfo):
        c.wait("vector", f"{P}_bc", qi + 1)
        if out_dst is not None and qi * nfo + fo >= 2:
            c.wait("vector", f"{P}_st", 16 * (qi * nfo + fo - 1))
        st = xstg[(qi * nfo + fo) % 2]
        if bf_dst is not None:
            final = bf_dst(fo)
        elif bfstg is not None:
            final = bfstg[(qi * nfo + fo) % 2]
        else:
            final = st
        c.op("vector", "tensor_tensor", st[:], rsb[:, fo * 512:(fo + 1) * 512],
             ps_b0[:, 0:512], Op.subtract)
        c.drain("vector")
        if gslice is not None:
            c.op("vector", "tensor_tensor", st[:], st[:], ps_b1[:, 0:512], Op.mult)
            c.drain("vector")
            c.op("vector", "scalar_tensor_tensor", final[:], st[:], gslice(fo),
                 bslice(fo).broadcast_to([128, 512]), Op.mult, Op.add)
            c.drain("vector")
        else:
            c.op("vector", "tensor_tensor", final[:], st[:], ps_b1[:, 0:512], Op.mult)
            c.drain("vector")
        c.sem_inc("vector", f"{P}_va", 1)
        if out_dst is not None:
            c.wait("sync", f"{P}_va", qi * nfo + fo + 1)
            c.dma("sync", out_dst(fo), final[:], sem=f"{P}_st")
    c.wait("tensor", f"{P}_va", (qi + 1) * nfo)


def ln_tiles(c, P):
    return (
        c.sbuf(f"{P}_sq1", [128, 512], F32),
        c.sbuf(f"{P}_mrow", [1, 512], F32),
        c.sbuf(f"{P}_sq2", [1, 512], F32),
        c.sbuf(f"{P}_vrow", [1, 512], F32),
        c.sbuf(f"{P}_lnrow", [1, 512], F32),
        c.sbuf(f"{P}_rstd", [1, 512], F32),
        [c.sbuf(f"{P}_xstg{i}", [128, 512], F32) for i in range(2)],
        c.psum(f"{P}_pss0", [128, 512], F32),
        c.psum(f"{P}_pss1", [128, 512], F32),
        c.psum(f"{P}_psb0", [128, 512], F32),
        c.psum(f"{P}_psb1", [128, 512], F32),
    )



def ph_proj(c):
    x = c.dram("x_in", [T, D], F32, kind="ExternalInput")
    wp = c.dram("L0_attn_w_proj", [D, D], F32, kind="ExternalInput")
    bp_h = c.dram("b_proj_t", [128, 8], F32, kind="ExternalInput")
    attnf = c.dram("attnf", [D, T], BF16)
    x1 = c.dram("x1", [D, T], F32)

    idn = c.tiles["c_idnf"]
    wsb = c.sbuf("p3_w", [128, 8 * D], BF16)
    bpt = c.sbuf("p3_b", [128, 8], F32)
    xtok = [c.sbuf(f"p3_xtok{i}", [128, 4 * D], F32) for i in range(2)]
    att = [c.sbuf(f"p3_att{i}", [128, 8 * 512], BF16) for i in range(2)]
    xf32 = [c.sbuf(f"p3_xf{i}", [128, 8 * 512], F32) for i in range(2)]
    rsb = c.sbuf("p3_rsb", [128, 8 * 512], F32)
    lt = ln_tiles(c, "p3")
    ps_b0, ps_b1 = lt[-2], lt[-1]
    ps_p = [c.psum(f"p3_psp{i}", [128, 512], F32) for i in range(2)]

    for kc in range(8):
        c.dma("gpsimd", wsb[:, kc * D:(kc + 1) * D], wp[kc * 128:(kc + 1) * 128, :],
              sem="p3_w")
    c.dma("sync", bpt[:], bp_h, sem="p3_b")
    c.wait("tensor", "p3_w", 16 * 8)
    c.wait("vector", "p3_b", 16)
    c.wait("tensor", "v_init", 1)

    jtr = 0
    for qc in range(NTC):
        if qc >= 2:
            c.wait("sync", "p3_tr", 32 * (qc - 1))
            c.wait("sync", "p3_pm", 8 * (qc - 1))
        xs = x[qc * TC:(qc + 1) * TC, :].rearrange("(i p) f -> p i f", p=128)
        c.dma("sync", xtok[qc % 2][:].rearrange("p (i f) -> p i f", i=4), xs, sem="p3_x")
        asrc = attnf[:, qc * TC:(qc + 1) * TC].rearrange("(kc p) t -> p kc t", p=128)
        c.dma("sync", att[qc % 2][:].rearrange("p (kc t) -> p kc t", kc=8), asrc,
              sem="p3_a")

        c.wait("tensor", "p3_x", 16 * (qc + 1))
        if qc >= 1:
            c.wait("tensor", "p3_va", qc * 8)
        for i in range(4):
            for fc in range(8):
                tt = (ps_b0 if jtr % 2 == 0 else ps_b1)
                if jtr >= 2:
                    c.wait("tensor", "p3_vx", jtr - 1)
                c.op("tensor", "transpose", tt[:, 0:128],
                     xtok[qc % 2][:, i * D + fc * 128:i * D + (fc + 1) * 128],
                     idn[:], inc=("p3_tr", 1))
                c.wait("vector", "p3_tr", jtr + 1)
                c.op("vector", "tensor_copy",
                     xf32[qc % 2][:, fc * 512 + i * 128:fc * 512 + (i + 1) * 128],
                     tt[:, 0:128])
                c.drain("vector")
                c.sem_inc("vector", "p3_vx", 1)
                jtr += 1

        c.wait("tensor", "p3_a", 16 * (qc + 1))
        for fo in range(8):
            g = qc * 8 + fo
            if g >= 2:
                c.wait("tensor", "p3_vr", g - 1)
            for kc in range(8):
                c.op("tensor", "matmul", ps_p[fo % 2][:],
                     wsb[:, kc * D + fo * 128:kc * D + (fo + 1) * 128],
                     att[qc % 2][:, kc * 512:(kc + 1) * 512],
                     start=(kc == 0), stop=(kc == 7),
                     inc=("p3_pm", 1) if kc == 7 else None)
            c.wait("vector", "p3_pm", g + 1)
            c.op("vector", "scalar_tensor_tensor", rsb[:, fo * 512:(fo + 1) * 512],
                 ps_p[fo % 2][:], bpt[:, fo:fo + 1],
                 xf32[qc % 2][:, fo * 512:(fo + 1) * 512], Op.add, Op.add)
            c.drain("vector")
            c.sem_inc("vector", "p3_vr", 1)

        emit_ln_tail(
            c, "p3", qc, rsb, lt,
            out_dst=lambda fo, _qc=qc: x1[:, _qc * TC:(_qc + 1) * TC]
                .rearrange("(fc p) t -> fc p t", p=128)[fo],
            rsb_ready=("p3_vr", lambda fo, _qc=qc: _qc * 8 + fo + 1),
        )

    c.wait("sync", "p3_st", 16 * 64)



def ph_ffnA(c):
    x1 = c.dram("x1", [D, T], F32)
    w1h = c.dram("w1T", [D, FH], F32, kind="ExternalInput")
    w2h = c.dram("w2T", [D, FH], F32, kind="ExternalInput")
    s1h = c.dram("w1s_t", [128, 32], F32, kind="ExternalInput")
    s2h = c.dram("w2s_t", [128, 32], F32, kind="ExternalInput")
    hA = c.dram("hA", [FH, T], BF16)

    w1 = c.sbuf("f4_w1", [128, 8 * FH], BF16)
    w2 = c.sbuf("f4_w2", [128, 8 * FH], BF16)
    s1 = c.sbuf("f4_s1", [128, 32], F32)
    s2 = c.sbuf("f4_s2", [128, 32], F32)
    x1b = [c.sbuf(f"f4_x{i}", [128, 8 * 512], BF16) for i in range(2)]
    tA = c.sbuf("f4_tA", [128, 512], F32)
    tB = c.sbuf("f4_tB", [128, 512], F32)
    hstg = [c.sbuf(f"f4_h{i}", [128, 512], BF16) for i in range(2)]
    psA = [c.psum(f"f4_psA{i}", [128, 512], F32) for i in range(2)]
    psB = [c.psum(f"f4_psB{i}", [128, 512], F32) for i in range(2)]

    for kc in range(8):
        c.dma("gpsimd", w1[:, kc * FH:(kc + 1) * FH], w1h[kc * 128:(kc + 1) * 128, :], sem="f4_w")
        c.dma("gpsimd", w2[:, kc * FH:(kc + 1) * FH], w2h[kc * 128:(kc + 1) * 128, :], sem="f4_w")
    c.dma("sync", s1[:], s1h, sem="f4_s")
    c.dma("sync", s2[:], s2h, sem="f4_s")
    c.wait("tensor", "f4_w", 16 * 16)
    c.wait("scalar", "f4_s", 32)

    g = 0
    for qc in range(NTC):
        if qc >= 2:
            c.wait("gpsimd", "f4_pm", 32 * (qc - 1))
        xs = x1[:, qc * TC:(qc + 1) * TC].rearrange("(kc p) t -> p kc t", p=128)
        c.dma("gpsimd", x1b[qc % 2][:].rearrange("p (kc t) -> p kc t", kc=8), xs, sem="f4_x")
        c.wait("tensor", "f4_x", 16 * (qc + 1))
        for m in range(32):
            if g >= 2:
                c.wait("tensor", "f4_v", g - 1)
            for kc in range(8):
                c.op("tensor", "matmul", psA[g % 2][:],
                     w1[:, kc * FH + m * 128:kc * FH + (m + 1) * 128],
                     x1b[qc % 2][:, kc * 512:(kc + 1) * 512],
                     start=(kc == 0), stop=(kc == 7))
                c.op("tensor", "matmul", psB[g % 2][:],
                     w2[:, kc * FH + m * 128:kc * FH + (m + 1) * 128],
                     x1b[qc % 2][:, kc * 512:(kc + 1) * 512],
                     start=(kc == 0), stop=(kc == 7),
                     inc=("f4_pm", 1) if kc == 7 else None)
            c.wait("scalar", "f4_pm", g + 1)
            c.wait("scalar", "f4_v", g)
            c.op("scalar", "activation", tA[:], psA[g % 2][:], AF.Silu,
                 scale=s1[:, m:m + 1])
            c.drain("scalar")
            c.op("scalar", "mul", tB[:], psB[g % 2][:], s2[:, m:m + 1])
            c.drain("scalar")
            c.sem_inc("scalar", "f4_a", 1)
            c.wait("vector", "f4_a", g + 1)
            if g >= 2:
                c.wait("vector", "f4_st", 16 * (g - 1))
            c.op("vector", "tensor_tensor", hstg[g % 2][:], tA[:], tB[:], Op.mult)
            c.drain("vector")
            c.sem_inc("vector", "f4_v", 1)
            c.wait("sync", "f4_v", g + 1)
            c.dma("sync", hA[m * 128:(m + 1) * 128, qc * TC:(qc + 1) * TC],
                  hstg[g % 2][:], sem="f4_st")
            g += 1
    c.wait("sync", "f4_st", 16 * 256)



def ph_hmid(c, P, win_name, s_name, hin_name, hout_name):
    wb_h = c.dram(win_name, [32, 32, 128, 128], F32, kind="ExternalInput")
    sh = c.dram(s_name, [128, 32], F32, kind="ExternalInput")
    hIn = c.dram(hin_name, [FH, T], BF16)
    hOut = c.dram(hout_name, [FH, T], BF16)

    hAt = c.sbuf(f"{P}_hin", [128, 32 * 2048], BF16)
    wblk = [c.sbuf(f"{P}_w{i}", [128, 32 * 128], BF16) for i in range(2)]
    ssb = c.sbuf(f"{P}_s", [128, 32], F32)
    hstg = [c.sbuf(f"{P}_h{i}", [128, 512], BF16) for i in range(2)]
    ps = [c.psum(f"{P}_ps{i}", [128, 512], F32) for i in range(2)]

    c.dma("sync", ssb[:], sh, sem=f"{P}_sld")
    c.wait("scalar", f"{P}_sld", 16)

    g = 0
    for half in range(2):
        if half >= 1:
            c.wait("sync", f"{P}_pm", 128 * half)
        hs = hIn[:, half * 2048:(half + 1) * 2048].rearrange("(kc p) t -> p kc t", p=128)
        c.dma("sync", hAt[:].rearrange("p (kc t) -> p kc t", kc=32), hs, sem=f"{P}_hx")
        c.wait("gpsimd", f"{P}_hx", 16 * (half + 1))
        for m in range(32):
            gm = half * 32 + m
            if gm >= 2:
                c.wait("gpsimd", f"{P}_pm", 4 * (gm - 1))
            c.dma("gpsimd", wblk[gm % 2][:].rearrange("p (kc mo) -> p kc mo", kc=32),
                  wb_h[m].rearrange("kc p mo -> p kc mo"), sem=f"{P}_w")
            c.wait("tensor", f"{P}_w", 16 * (gm + 1))
            for n in range(4):
                if g >= 2:
                    c.wait("tensor", f"{P}_a", g - 1)
                for kc in range(32):
                    c.op("tensor", "matmul", ps[g % 2][:],
                         wblk[gm % 2][:, kc * 128:(kc + 1) * 128],
                         hAt[:, kc * 2048 + n * 512:kc * 2048 + (n + 1) * 512],
                         start=(kc == 0), stop=(kc == 31),
                         inc=(f"{P}_pm", 1) if kc == 31 else None)
                c.wait("scalar", f"{P}_pm", g + 1)
                if g >= 2:
                    c.wait("scalar", f"{P}_st", 16 * (g - 1))
                c.op("scalar", "activation", hstg[g % 2][:], ps[g % 2][:], AF.Gelu,
                     scale=ssb[:, m:m + 1])
                c.drain("scalar")
                c.sem_inc("scalar", f"{P}_a", 1)
                c.wait("sync", f"{P}_a", g + 1)
                c.dma("sync", hOut[m * 128:(m + 1) * 128,
                                   half * 2048 + n * 512:half * 2048 + (n + 1) * 512],
                      hstg[g % 2][:], sem=f"{P}_st")
                g += 1
    c.wait("sync", f"{P}_st", 16 * 256)


def ph_h1(c):
    ph_hmid(c, "p5", "h1b", "h1s_t", "hA", "hB")


def ph_h2(c):
    ph_hmid(c, "p6", "h2b", "h2s_t", "hB", "hC")



def ph_w3ln(c):
    w3h = c.dram("w3T", [FH, D], F32, kind="ExternalInput")
    s3h = c.dram("w3s_t", [128, 8], F32, kind="ExternalInput")
    hC = c.dram("hC", [FH, T], BF16)
    x1 = c.dram("x1", [D, T], F32)
    x2 = c.dram("x2", [D, T], F32)

    w3 = c.sbuf("p7_w3", [128, 32 * D], BF16)
    s3 = c.sbuf("p7_s3", [128, 8], F32)
    hCt = [c.sbuf(f"p7_hc{i}", [128, 32 * 512], BF16) for i in range(2)]
    x1f = [c.sbuf("p7_x10", [128, 8 * 512], F32)] * 2
    rsb = c.sbuf("p7_rsb", [128, 8 * 512], F32)
    lt = ln_tiles(c, "p7")
    ps_p = [c.psum(f"p7_psp{i}", [128, 512], F32) for i in range(2)]

    for kc in range(32):
        c.dma("gpsimd", w3[:, kc * D:(kc + 1) * D], w3h[kc * 128:(kc + 1) * 128, :],
              sem="p7_w")
    c.dma("sync", s3[:], s3h, sem="p7_s")
    c.wait("tensor", "p7_w", 16 * 32)
    c.wait("vector", "p7_s", 16)
    c.wait("tensor", "v_init", 1)

    for qc in range(NTC):
        if qc >= 2:
            c.wait("sync", "p7_pm", 8 * (qc - 1))
        if qc >= 1:
            c.wait("sync", "p7_vr", 8 * qc)
        hs = hC[:, qc * TC:(qc + 1) * TC].rearrange("(kc p) t -> p kc t", p=128)
        c.dma("sync", hCt[qc % 2][:].rearrange("p (kc t) -> p kc t", kc=32), hs,
              sem="p7_hx")
        xs = x1[:, qc * TC:(qc + 1) * TC].rearrange("(kc p) t -> p kc t", p=128)
        c.dma("sync", x1f[qc % 2][:].rearrange("p (kc t) -> p kc t", kc=8), xs,
              sem="p7_x1")
        c.wait("tensor", "p7_hx", 16 * (qc + 1))
        c.wait("vector", "p7_x1", 16 * (qc + 1))
        for fo in range(8):
            g = qc * 8 + fo
            if g >= 2:
                c.wait("tensor", "p7_vr", g - 1)
            for kc in range(32):
                c.op("tensor", "matmul", ps_p[fo % 2][:],
                     w3[:, kc * D + fo * 128:kc * D + (fo + 1) * 128],
                     hCt[qc % 2][:, kc * 512:(kc + 1) * 512],
                     start=(kc == 0), stop=(kc == 31),
                     inc=("p7_pm", 1) if kc == 31 else None)
            c.wait("vector", "p7_pm", g + 1)
            c.op("vector", "scalar_tensor_tensor", rsb[:, fo * 512:(fo + 1) * 512],
                 ps_p[fo % 2][:], s3[:, fo:fo + 1],
                 x1f[qc % 2][:, fo * 512:(fo + 1) * 512], Op.mult, Op.add)
            c.drain("vector")
            c.sem_inc("vector", "p7_vr", 1)
        emit_ln_tail(
            c, "p7", qc, rsb, lt,
            out_dst=lambda fo, _qc=qc: x2[:, _qc * TC:(_qc + 1) * TC]
                .rearrange("(fc p) t -> fc p t", p=128)[fo],
            rsb_ready=("p7_vr", lambda fo, _qc=qc: _qc * 8 + fo + 1),
        )
    c.wait("sync", "p7_st", 16 * 64)



def ph_lnwto(c):
    x2 = c.dram("x2", [D, T], F32)
    wtoh = c.dram("W_toT", [D, 2 * FR], F32, kind="ExternalInput")
    g1h = c.dram("ln1g_t", [128, 8], F32, kind="ExternalInput")
    b1h = c.dram("ln1b_t", [128, 8], F32, kind="ExternalInput")
    frh = c.dram("freq2", [128, 2], F32, kind="ExternalInput")
    ubuf = c.dram("ubuf", [128, 4 * T], F32)

    wto = c.sbuf("p8_wto", [128, 8 * 512], BF16)
    g1 = c.sbuf("p8_g1", [128, 8], F32)
    b1 = c.sbuf("p8_b1", [128, 8], F32)
    fr2 = c.sbuf("p8_fr", [128, 2], F32)
    om = c.sbuf("p8_om", [128, 2], F32)
    c01 = c.sbuf("p8_c01", [128, 2], F32)
    s01 = c.sbuf("p8_s01", [128, 2], F32)
    ns01 = c.sbuf("p8_ns01", [128, 2], F32)
    halfpi = c.sbuf("p8_hpi", [128, 1], F32)
    x2f = [c.sbuf(f"p8_x2{i}", [128, 8 * 512], F32) for i in range(2)]
    lnx = c.sbuf("p8_lnx", [128, 8 * T], BF16)
    tra = c.sbuf("p8_tra", [128, 512], F32)
    trb = c.sbuf("p8_trb", [128, 512], F32)
    ustg = [c.sbuf(f"p8_u{i}", [128, 4 * 512], F32) for i in range(2)]
    lt = ln_tiles(c, "p8")
    ps_w = [c.psum(f"p8_psw{i}", [128, 512], F32) for i in range(4)]

    for kc in range(8):
        c.dma("gpsimd", wto[:, kc * 512:(kc + 1) * 512],
              wtoh[kc * 128:(kc + 1) * 128, :], sem="p8_w")
    c.dma("sync", g1[:], g1h, sem="p8_g")
    c.dma("sync", b1[:], b1h, sem="p8_g")
    c.dma("sync", fr2[:], frh, sem="p8_g")
    c.wait("scalar", "p8_g", 48)
    c.wait("scalar", "v_init", 1)
    c.op("scalar", "activation", om[:], fr2[:], AF.Tanh)
    c.drain("scalar")
    c.op("scalar", "mul", om[:], om[:], 0.1)
    c.drain("scalar")
    c.op("vector", "memset", halfpi[:], 1.5707963267948966)
    c.drain("vector")
    c.sem_inc("vector", "p8_hp", 1)
    c.wait("scalar", "p8_hp", 1)
    c.op("scalar", "activation", c01[:], om[:], AF.Sin, bias=halfpi[:, 0:1])
    c.drain("scalar")
    c.op("scalar", "activation", s01[:], om[:], AF.Sin)
    c.drain("scalar")
    c.op("scalar", "mul", c01[:], c01[:], 0.1)
    c.drain("scalar")
    c.op("scalar", "mul", s01[:], s01[:], 0.1)
    c.drain("scalar")
    c.op("scalar", "mul", ns01[:], s01[:], -1.0)
    c.drain("scalar")
    c.sem_inc("scalar", "p8_cs", 1)
    c.wait("vector", "p8_cs", 1)
    c.wait("vector", "p8_g", 48)
    c.wait("tensor", "p8_w", 16 * 8)
    c.wait("tensor", "v_init", 1)

    for qc in range(NTC):
        xs = x2[:, qc * TC:(qc + 1) * TC].rearrange("(kc p) t -> p kc t", p=128)
        c.dma("sync", x2f[qc % 2][:].rearrange("p (kc t) -> p kc t", kc=8), xs,
              sem="p8_x")
        c.wait("vector", "p8_x", 16 * (qc + 1))
        c.wait("scalar", "p8_x", 16 * (qc + 1))
        c.wait("tensor", "p8_x", 16 * (qc + 1))
        emit_ln_tail(
            c, "p8l", qc, x2f[qc % 2], lt, out_dst=None,
            gslice=lambda fo: g1[:, fo:fo + 1],
            bslice=lambda fo: b1[:, fo:fo + 1],
            bf_dst=lambda fo, _qc=qc: lnx[:, fo * T + _qc * 512:fo * T + (_qc + 1) * 512],
        )
        c.drain("vector")
        c.sem_inc("vector", "p8_lnq", 1)
        if qc < NTC - 1:
            c.wait("sync", "p8l_va", (qc + 1) * 8)

    for qc in range(NTC):
        c.wait("tensor", "p8_lnq", NTC)
        for m in range(4):
            if qc >= 1:
                c.wait("tensor", "p8_rot", 4 * qc)
            for kc in range(8):
                c.op("tensor", "matmul", ps_w[m][:],
                     wto[:, kc * 512 + m * 128:kc * 512 + (m + 1) * 128],
                     lnx[:, kc * T + qc * 512:kc * T + (qc + 1) * 512],
                     start=(kc == 0), stop=(kc == 7),
                     inc=("p8_pm", 1) if kc == 7 else None)
        for h in range(2):
            c.wait("scalar", "p8_pm", 4 * (qc + 1))
            c.wait("scalar", "p8_rot", 4 * qc + 2 * h)
            if qc >= 2:
                c.wait("scalar", "p8_ust", 16 * (qc - 1))
            c.op("scalar", "mul", tra[:], ps_w[h][:], c01[:, h:h + 1])
            c.drain("scalar")
            c.op("scalar", "mul", trb[:], ps_w[h][:], s01[:, h:h + 1])
            c.drain("scalar")
            c.sem_inc("scalar", "p8_ra", 1)
            c.wait("vector", "p8_ra", 2 * qc + h + 1)
            us = ustg[qc % 2]
            c.op("vector", "scalar_tensor_tensor", us[:, h:4 * 512:4],
                 ps_w[2 + h][:], ns01[:, h:h + 1], tra[:], Op.mult, Op.add)
            c.drain("vector")
            c.op("vector", "scalar_tensor_tensor", us[:, 2 + h:4 * 512:4],
                 ps_w[2 + h][:], c01[:, h:h + 1], trb[:], Op.mult, Op.add)
            c.drain("vector")
            c.sem_inc("vector", "p8_rot", 2)
        c.wait("sync", "p8_rot", 4 * (qc + 1))
        c.dma("sync", ubuf[:, qc * 2048:(qc + 1) * 2048], ustg[qc % 2][:],
              sem="p8_ust")
    c.wait("sync", "p8_ust", 16 * NTC)



def ph_scan(c):
    ubuf = c.dram("ubuf", [128, 4 * T], F32)
    ldh = c.dram("ldec2", [128, 2], F32, kind="ExternalInput")
    frh = c.dram("freq2", [128, 2], F32, kind="ExternalInput")
    spec = c.dram("spec", [4 * 128, T], BF16)

    ld2 = c.sbuf("p9_ld", [128, 2], F32)
    fr2 = c.sbuf("p9_fr", [128, 2], F32)
    dec = c.sbuf("p9_dec", [128, 2], F32)
    om = c.sbuf("p9_om", [128, 2], F32)
    cs = c.sbuf("p9_cs", [128, 2], F32)
    sn = c.sbuf("p9_sn", [128, 2], F32)
    cd2 = c.sbuf("p9_cd2", [128, 2], F32)
    sd2 = c.sbuf("p9_sd2", [128, 2], F32)
    CD4 = c.sbuf("p9_cd4", [128, 4], F32)
    SD4 = c.sbuf("p9_sd4", [128, 4], F32)
    halfpi = c.sbuf("p9_hpi", [128, 1], F32)
    ut = [c.sbuf(f"p9_u{i}", [128, 4 * 512], F32) for i in range(2)]
    hist = c.sbuf("p9_hist", [128, 4 * (T + 1)], F32)
    tmpa = c.sbuf("p9_ta", [128, 4], F32)
    tmpb = c.sbuf("p9_tb", [128, 4], F32)
    sstg = [c.sbuf(f"p9_ss{i}", [128, 4 * 512], BF16) for i in range(2)]

    c.dma("sync", ld2[:], ldh, sem="p9_g")
    c.dma("sync", fr2[:], frh, sem="p9_g")
    c.wait("scalar", "p9_g", 32)
    c.wait("scalar", "v_init", 1)
    c.op("scalar", "activation", dec[:], ld2[:], AF.Sigmoid)
    c.drain("scalar")
    c.op("scalar", "activation", om[:], fr2[:], AF.Tanh)
    c.drain("scalar")
    c.op("scalar", "mul", om[:], om[:], 0.1)
    c.drain("scalar")
    c.sem_inc("scalar", "p9_a1", 1)
    c.wait("vector", "p9_a1", 1)
    c.op("vector", "memset", halfpi[:], 1.5707963267948966)
    c.drain("vector")
    c.sem_inc("vector", "p9_v1", 1)
    c.wait("scalar", "p9_v1", 1)
    c.op("scalar", "activation", cs[:], om[:], AF.Sin, bias=halfpi[:, 0:1])
    c.drain("scalar")
    c.op("scalar", "activation", sn[:], om[:], AF.Sin)
    c.drain("scalar")
    c.sem_inc("scalar", "p9_a2", 1)
    c.wait("vector", "p9_a2", 1)
    c.op("vector", "tensor_mul", cd2[:], cs[:], dec[:])
    c.drain("vector")
    c.op("vector", "tensor_mul", sd2[:], sn[:], dec[:])
    c.drain("vector")
    c.op("vector", "tensor_copy", CD4[:, 0:2], cd2[:])
    c.op("vector", "tensor_copy", CD4[:, 2:4], cd2[:])
    c.op("vector", "tensor_copy", SD4[:, 2:4], sd2[:])
    c.drain("vector")
    c.op("vector", "tensor_scalar_mul", SD4[:, 0:2], sd2[:], -1.0)
    c.drain("vector")
    c.op("vector", "memset", hist[:, 0:4], 0.0)
    c.drain("vector")
    c.sem_inc("vector", "p9_k", 1)
    c.wait("gpsimd", "p9_k", 1)

    for qc in range(NTC):
        if qc >= 2:
            c.wait("sync", "p9_gc", qc - 1)
        c.dma("sync", ut[qc % 2][:], ubuf[:, qc * 2048:(qc + 1) * 2048], sem="p9_u")
        c.wait("gpsimd", "p9_u", 16 * (qc + 1))
        for tl in range(512):
            t = qc * 512 + tl
            prev = hist[:, 4 * t:4 * t + 4]
            prev_sw = prev.rearrange("p (b a) -> p b a", b=2)[:, ::-1, :]
            cur = hist[:, 4 * (t + 1):4 * (t + 1) + 4]
            c.op("gpsimd", "tensor_tensor",
                 tmpb[:].rearrange("p (b a) -> p b a", b=2), prev_sw,
                 SD4[:].rearrange("p (b a) -> p b a", b=2), Op.mult)
            c.op("gpsimd", "tensor_tensor", tmpa[:], prev, CD4[:], Op.mult)
            c.op("gpsimd", "tensor_add", tmpa[:], tmpa[:], tmpb[:])
            c.op("gpsimd", "tensor_add", tmpa[:], tmpa[:], ut[qc % 2][:, 4 * tl:4 * tl + 4])
            c.op("gpsimd", "tensor_scalar", cur, tmpa[:], 10.0, -10.0, Op.min, Op.max)
        c.drain("gpsimd")
        c.sem_inc("gpsimd", "p9_gc", 1)
        c.wait("vector", "p9_gc", qc + 1)
        if qc >= 2:
            c.wait("vector", "p9_st", 16 * (qc - 1))
        for j in range(4):
            hs = hist[:, 4 * (qc * 512 + 1) + j:4 * (qc * 512 + 512 + 1):4]
            c.op("vector", "tensor_copy", sstg[qc % 2][:, j * 512:(j + 1) * 512], hs)
        c.drain("vector")
        c.sem_inc("vector", "p9_cv", 1)
        c.wait("sync", "p9_cv", qc + 1)
        c.dma("sync",
              spec[:, qc * 512:(qc + 1) * 512].rearrange("(m p) t -> p m t", p=128),
              sstg[qc % 2][:].rearrange("p (m t) -> p m t", m=4), sem="p9_st")
    c.wait("sync", "p9_st", 16 * NTC)



def ph_wfrom_ffn1(c):
    spec = c.dram("spec", [4 * 128, T], BF16)
    x2 = c.dram("x2", [D, T], F32)
    wfh = c.dram("W_fromT", [2 * FR, D], F32, kind="ExternalInput")
    g3h = c.dram("ln3g_t", [128, 8], F32, kind="ExternalInput")
    b3h = c.dram("ln3b_t", [128, 8], F32, kind="ExternalInput")
    f1h = c.dram("ffn1T", [D, FH], F32, kind="ExternalInput")
    bf1h = c.dram("bffn1_t", [128, 32], F32, kind="ExternalInput")
    r3d = c.dram("r3", [D, T], F32)
    gh = c.dram("gh", [FH, T], BF16)

    wfrom = c.sbuf("pa_wf", [128, 4 * D], BF16)
    f1 = c.sbuf("pa_f1", [128, 8 * FH], BF16)
    g3 = c.sbuf("pa_g3", [128, 8], F32)
    b3 = c.sbuf("pa_b3", [128, 8], F32)
    bf1 = c.sbuf("pa_bf1", [128, 32], F32)
    spc = [c.sbuf(f"pa_sp{i}", [128, 4 * 512], BF16) for i in range(2)]
    x2f = [c.sbuf("pa_x20", [128, 8 * 512], F32)] * 2
    rsb = c.sbuf("pa_rsb", [128, 8 * 512], F32)
    x3b = c.sbuf("pa_x3b", [128, 8 * 512], BF16)
    gstg = [c.sbuf(f"pa_g{i}", [128, 512], BF16) for i in range(2)]
    lt = ln_tiles(c, "pa")
    ps_p = [c.psum(f"pa_psp{i}", [128, 512], F32) for i in range(2)]
    ps_f = [c.psum(f"pa_psf{i}", [128, 512], F32) for i in range(2)]

    for kc in range(4):
        c.dma("gpsimd", wfrom[:, kc * D:(kc + 1) * D], wfh[kc * 128:(kc + 1) * 128, :],
              sem="pa_w")
    for kc in range(8):
        c.dma("gpsimd", f1[:, kc * FH:(kc + 1) * FH], f1h[kc * 128:(kc + 1) * 128, :],
              sem="pa_w")
    c.dma("sync", g3[:], g3h, sem="pa_g")
    c.dma("sync", b3[:], b3h, sem="pa_g")
    c.dma("sync", bf1[:], bf1h, sem="pa_g")
    c.wait("tensor", "pa_w", 16 * 12)
    c.wait("vector", "pa_g", 48)
    c.wait("scalar", "pa_g", 48)
    c.wait("tensor", "v_init", 1)

    for qc in range(NTC):
        if qc >= 2:
            c.wait("sync", "pa_pm", 8 * (qc - 1))
        if qc >= 1:
            c.wait("sync", "pa_vr", 8 * qc)
        ss = spec[:, qc * TC:(qc + 1) * TC].rearrange("(kc p) t -> p kc t", p=128)
        c.dma("sync", spc[qc % 2][:].rearrange("p (kc t) -> p kc t", kc=4), ss,
              sem="pa_sx")
        xs = x2[:, qc * TC:(qc + 1) * TC].rearrange("(kc p) t -> p kc t", p=128)
        c.dma("sync", x2f[qc % 2][:].rearrange("p (kc t) -> p kc t", kc=8), xs,
              sem="pa_xx")
        c.wait("tensor", "pa_sx", 16 * (qc + 1))
        c.wait("vector", "pa_xx", 16 * (qc + 1))
        for fo in range(8):
            g = qc * 8 + fo
            if g >= 2:
                c.wait("tensor", "pa_vr", g - 1)
            for kc in range(4):
                c.op("tensor", "matmul", ps_p[fo % 2][:],
                     wfrom[:, kc * D + fo * 128:kc * D + (fo + 1) * 128],
                     spc[qc % 2][:, kc * 512:(kc + 1) * 512],
                     start=(kc == 0), stop=(kc == 3),
                     inc=("pa_pm", 1) if kc == 3 else None)
            c.wait("vector", "pa_pm", g + 1)
            if g >= 2:
                c.wait("vector", "pa_r3st", 16 * (g - 1))
            c.op("vector", "scalar_tensor_tensor", rsb[:, fo * 512:(fo + 1) * 512],
                 ps_p[fo % 2][:], 0.1,
                 x2f[qc % 2][:, fo * 512:(fo + 1) * 512], Op.mult, Op.add)
            c.drain("vector")
            c.sem_inc("vector", "pa_vr", 1)
            c.wait("sync", "pa_vr", g + 1)
            c.dma("sync", r3d[:, qc * TC:(qc + 1) * TC]
                  .rearrange("(fc p) t -> fc p t", p=128)[fo],
                  rsb[:, fo * 512:(fo + 1) * 512], sem="pa_r3st")
        emit_ln_tail(
            c, "pa", qc, rsb, lt, out_dst=None,
            gslice=lambda fo: g3[:, fo:fo + 1],
            bslice=lambda fo: b3[:, fo:fo + 1],
            bf_dst=lambda fo: x3b[:, fo * 512:(fo + 1) * 512],
            rsb_ready=("pa_vr", lambda fo, _qc=qc: _qc * 8 + fo + 1),
        )
        for m in range(32):
            gg = qc * 32 + m
            if gg >= 2:
                c.wait("tensor", "pa_a", gg - 1)
            c.wait("tensor", "pa_va", (qc + 1) * 8)
            for kc in range(8):
                c.op("tensor", "matmul", ps_f[m % 2][:],
                     f1[:, kc * FH + m * 128:kc * FH + (m + 1) * 128],
                     x3b[:, kc * 512:(kc + 1) * 512],
                     start=(kc == 0), stop=(kc == 7),
                     inc=("pa_fm", 1) if kc == 7 else None)
            c.wait("scalar", "pa_fm", gg + 1)
            if gg >= 2:
                c.wait("scalar", "pa_gst", 16 * (gg - 1))
            c.op("scalar", "activation", gstg[gg % 2][:], ps_f[m % 2][:], AF.Gelu,
                 bias=bf1[:, m:m + 1], scale=1.0)
            c.drain("scalar")
            c.sem_inc("scalar", "pa_a", 1)
            c.wait("sync", "pa_a", gg + 1)
            c.dma("sync", gh[m * 128:(m + 1) * 128, qc * TC:(qc + 1) * TC],
                  gstg[gg % 2][:], sem="pa_gst")
        c.wait("vector", "pa_fm", (qc + 1) * 32)
    c.wait("sync", "pa_gst", 16 * 256)
    c.wait("sync", "pa_r3st", 16 * 64)



def ph_ffn2ln(c):
    gh = c.dram("gh", [FH, T], BF16)
    x2 = c.dram("x2", [D, T], F32)
    r3d = c.dram("r3", [D, T], F32)
    f2h = c.dram("ffn2T", [FH, D], F32, kind="ExternalInput")
    bf2h = c.dram("bffn2_t", [128, 8], F32, kind="ExternalInput")
    x4 = c.dram("x4", [D, T], F32)

    f2 = c.sbuf("pb_f2", [128, 32 * D], BF16)
    bf2 = c.sbuf("pb_bf2", [128, 8], F32)
    b2s = c.sbuf("pb_b2s", [128, 8], F32)
    ght = [c.sbuf(f"pb_gh{i}", [128, 32 * 512], BF16) for i in range(2)]
    x2f = [c.sbuf("pb_x20", [128, 8 * 512], F32)] * 2
    r3f = [c.sbuf("pb_r30", [128, 8 * 512], F32)] * 2
    rsb = c.sbuf("pb_rsb", [128, 8 * 512], F32)
    tps = c.sbuf("pb_tps", [128, 512], F32)
    lt = ln_tiles(c, "pb")
    ps_p = [c.psum(f"pb_psp{i}", [128, 512], F32) for i in range(2)]

    for kc in range(32):
        c.dma("gpsimd", f2[:, kc * D:(kc + 1) * D], f2h[kc * 128:(kc + 1) * 128, :],
              sem="pb_w")
    c.dma("sync", bf2[:], bf2h, sem="pb_g")
    c.wait("scalar", "pb_g", 16)
    c.wait("scalar", "v_init", 1)
    c.op("scalar", "mul", b2s[:], bf2[:], 0.1)
    c.drain("scalar")
    c.sem_inc("scalar", "pb_b", 1)
    c.wait("tensor", "pb_w", 16 * 32)
    c.wait("vector", "pb_b", 1)
    c.wait("tensor", "v_init", 1)

    for qc in range(NTC):
        if qc >= 2:
            c.wait("sync", "pb_pm", 8 * (qc - 1))
        if qc >= 1:
            c.wait("sync", "pb_vr", 8 * qc)
        gs = gh[:, qc * TC:(qc + 1) * TC].rearrange("(kc p) t -> p kc t", p=128)
        c.dma("sync", ght[qc % 2][:].rearrange("p (kc t) -> p kc t", kc=32), gs,
              sem="pb_gx")
        xs = x2[:, qc * TC:(qc + 1) * TC].rearrange("(kc p) t -> p kc t", p=128)
        c.dma("sync", x2f[qc % 2][:].rearrange("p (kc t) -> p kc t", kc=8), xs,
              sem="pb_xx")
        rs = r3d[:, qc * TC:(qc + 1) * TC].rearrange("(kc p) t -> p kc t", p=128)
        c.dma("sync", r3f[qc % 2][:].rearrange("p (kc t) -> p kc t", kc=8), rs,
              sem="pb_rx")
        c.wait("tensor", "pb_gx", 16 * (qc + 1))
        c.wait("vector", "pb_xx", 16 * (qc + 1))
        c.wait("vector", "pb_rx", 16 * (qc + 1))
        for fo in range(8):
            g = qc * 8 + fo
            if g >= 2:
                c.wait("tensor", "pb_vr", g - 1)
            for kc in range(32):
                c.op("tensor", "matmul", ps_p[fo % 2][:],
                     f2[:, kc * D + fo * 128:kc * D + (fo + 1) * 128],
                     ght[qc % 2][:, kc * 512:(kc + 1) * 512],
                     start=(kc == 0), stop=(kc == 31),
                     inc=("pb_pm", 1) if kc == 31 else None)
            c.wait("vector", "pb_pm", g + 1)
            c.op("vector", "scalar_tensor_tensor", tps[:],
                 ps_p[fo % 2][:], 0.1,
                 r3f[qc % 2][:, fo * 512:(fo + 1) * 512], Op.mult, Op.add)
            c.drain("vector")
            c.op("vector", "scalar_tensor_tensor", rsb[:, fo * 512:(fo + 1) * 512],
                 tps[:], b2s[:, fo:fo + 1],
                 x2f[qc % 2][:, fo * 512:(fo + 1) * 512], Op.add, Op.add)
            c.drain("vector")
            c.sem_inc("vector", "pb_vr", 1)
        emit_ln_tail(
            c, "pb", qc, rsb, lt,
            out_dst=lambda fo, _qc=qc: x4[:, _qc * TC:(_qc + 1) * TC]
                .rearrange("(fc p) t -> fc p t", p=128)[fo],
            rsb_ready=("pb_vr", lambda fo, _qc=qc: _qc * 8 + fo + 1),
        )
    c.wait("sync", "pb_st", 16 * 64)



def ph_l1ffn(c, upto="full"):
    x4 = c.dram("x4", [D, T], F32)
    kh = c.dram("L1_ffn_kernel", [D, D], F32, kind="ExternalInput")
    ah = c.dram("L1_ffn_a", [D, 16], F32, kind="ExternalInput")
    bh = c.dram("L1_ffn_b", [16, D], F32, kind="ExternalInput")
    bih = c.dram("bias_t", [128, 8], F32, kind="ExternalInput")
    x5 = c.dram("x5", [D, T], F32)

    kf = c.sbuf("pc_kf", [128, 8 * D], F32)
    wq = c.sbuf("pc_wq", [128, 8 * D], BF16)
    asb = c.sbuf("pc_a", [128, 8 * 16], BF16)
    bsb = c.sbuf("pc_b", [16, D], BF16)
    bi = c.sbuf("pc_bi", [128, 8], F32)
    amax = c.sbuf("pc_amax", [1, 1], F32)
    srow = c.sbuf("pc_srow", [1, 1], F32)
    rrow = c.sbuf("pc_rrow", [1, 1], F32)
    sc128 = c.sbuf("pc_sc", [128, 1], F32)
    rc128 = c.sbuf("pc_rc", [128, 1], F32)
    qt = c.sbuf("pc_qt", [128, D], F32)
    x4b = [c.sbuf(f"pc_x4b{i}", [128, 8 * 512], BF16) for i in range(2)]
    x4f = [c.sbuf(f"pc_x4f{i}", [128, 8 * 512], F32) for i in range(2)]
    t1 = c.sbuf("pc_t1", [16, 512], BF16)
    rsb = c.sbuf("pc_rsb", [128, 8 * 512], F32)
    lt = ln_tiles(c, "pc")
    ps_p = [c.psum(f"pc_psp{i}", [128, 512], F32) for i in range(2)]
    ps_l = c.psum("pc_psl", [128, 512], F32)
    onesr = None

    for kc in range(8):
        c.dma("sync", kf[:, kc * D:(kc + 1) * D], kh[kc * 128:(kc + 1) * 128, :],
              sem="pc_k")
    c.dma("gpsimd", asb[:].rearrange("p (kc r) -> p kc r", kc=8),
          ah.rearrange("(kc p) r -> p kc r", p=128), sem="pc_k")
    c.dma("gpsimd", bsb[:], bh, sem="pc_k")
    c.dma("sync", bi[:], bih, sem="pc_k")
    c.wait("gpsimd", "pc_k", 16 * 11)
    c.op("gpsimd", "tensor_reduce", amax[:], kf[:], AX.XYZWC, Op.abs_max)
    c.drain("gpsimd")
    c.sem_inc("gpsimd", "pc_am", 1)
    c.wait("scalar", "pc_am", 1)
    c.op("scalar", "mul", srow[:], amax[:], 1.0 / 7.0)
    c.drain("scalar")
    c.sem_inc("scalar", "pc_s", 1)
    c.wait("vector", "pc_s", 1)
    c.op("vector", "reciprocal", rrow[:], srow[:])
    c.drain("vector")
    c.sem_inc("vector", "pc_r", 1)
    c.wait("tensor", "v_init", 1)
    c.wait("tensor", "pc_r", 1)
    onesr = c.tiles["c_onesr"]
    c.op("tensor", "matmul", ps_l[:, 0:1], onesr[:], srow[:], start=True, stop=True)
    c.op("tensor", "matmul", ps_l[:, 1:2], onesr[:], rrow[:], start=True, stop=True,
         inc=("pc_pb", 1))
    c.wait("vector", "pc_pb", 1)
    c.op("vector", "tensor_copy", sc128[:], ps_l[:, 0:1])
    c.op("vector", "tensor_copy", rc128[:], ps_l[:, 1:2])
    c.drain("vector")
    for kc in range(8):
        c.op("vector", "tensor_scalar", qt[:], kf[:, kc * D:(kc + 1) * D],
             rc128[:, 0:1], None, Op.mult)
        c.drain("vector")
        c.op("vector", "tensor_scalar", qt[:], qt[:], 12582912.0, None, Op.add)
        c.drain("vector")
        c.op("vector", "tensor_scalar", qt[:], qt[:], 12582912.0, None, Op.subtract)
        c.drain("vector")
        c.op("vector", "tensor_scalar", qt[:], qt[:], 7.0, -8.0, Op.min, Op.max)
        c.drain("vector")
        c.op("vector", "tensor_scalar", wq[:, kc * D:(kc + 1) * D], qt[:],
             sc128[:, 0:1], None, Op.mult)
        c.drain("vector")
    c.sem_inc("vector", "pc_q", 1)
    c.wait("tensor", "pc_q", 1)
    if upto == "quant":
        wqd = c.dram("wq_dbg", [D, D], BF16, kind="ExternalOutput")
        dbg = c.dram("dbg", [128, 8], F32, kind="ExternalOutput")
        c.wait("sync", "pc_q", 1)
        for kc in range(8):
            c.dma("sync", wqd[kc * 128:(kc + 1) * 128, :], wq[:, kc * D:(kc + 1) * D],
                  sem="pc_dbg")
        dbt = c.sbuf("pc_dbt", [128, 8], F32)
        c.op("vector", "memset", dbt[:], 0.0)
        c.drain("vector")
        c.op("vector", "tensor_copy", dbt[0:1, 0:1], amax[:])
        c.op("vector", "tensor_copy", dbt[0:1, 1:2], srow[:])
        c.op("vector", "tensor_copy", dbt[0:1, 2:3], rrow[:])
        c.op("vector", "tensor_copy", dbt[:, 3:4], sc128[:])
        c.op("vector", "tensor_copy", dbt[:, 4:5], rc128[:])
        c.drain("vector")
        c.sem_inc("vector", "pc_dv", 1)
        c.wait("sync", "pc_dv", 1)
        c.dma("sync", dbg, dbt[:], sem="pc_dbg")
        c.wait("sync", "pc_dbg", 16 * 9)
        return

    for qc in range(NTC):
        if qc >= 2:
            c.wait("gpsimd", "pc_pm", 8 * (qc - 1))
            c.wait("sync", "pc_pm", 8 * (qc - 1))
        xs = x4[:, qc * TC:(qc + 1) * TC].rearrange("(kc p) t -> p kc t", p=128)
        c.dma("gpsimd", x4b[qc % 2][:].rearrange("p (kc t) -> p kc t", kc=8), xs,
              sem="pc_xb")
        c.dma("sync", x4f[qc % 2][:].rearrange("p (kc t) -> p kc t", kc=8), xs,
              sem="pc_xf")
        c.wait("tensor", "pc_xb", 16 * (qc + 1))
        c.wait("vector", "pc_xf", 16 * (qc + 1))
        if qc >= 1:
            c.wait("tensor", "pc_l1v", qc)
        for kc in range(8):
            c.op("tensor", "matmul", ps_l[0:16, 0:512],
                 asb[:, kc * 16:(kc + 1) * 16],
                 x4b[qc % 2][:, kc * 512:(kc + 1) * 512],
                 start=(kc == 0), stop=(kc == 7),
                 inc=("pc_l1", 1) if kc == 7 else None)
        c.wait("vector", "pc_l1", qc + 1)
        if qc >= 1:
            c.wait("vector", "pc_pm", 8 * qc)
        c.op("vector", "tensor_copy", t1[:], ps_l[0:16, 0:512])
        c.drain("vector")
        c.sem_inc("vector", "pc_l1v", 1)
        c.wait("tensor", "pc_l1v", qc + 1)
        if upto == "lora1":
            continue
        for fo in range(8):
            g = qc * 8 + fo
            if g >= 2:
                c.wait("tensor", "pc_vr", g - 1)
            for kc in range(8):
                c.op("tensor", "matmul", ps_p[fo % 2][:],
                     wq[:, kc * D + fo * 128:kc * D + (fo + 1) * 128],
                     x4b[qc % 2][:, kc * 512:(kc + 1) * 512],
                     start=(kc == 0), stop=False)
            c.op("tensor", "matmul", ps_p[fo % 2][:],
                 bsb[:, fo * 128:(fo + 1) * 128], t1[:],
                 start=False, stop=True, inc=("pc_pm", 1))
            c.wait("vector", "pc_pm", g + 1)
            c.op("vector", "scalar_tensor_tensor", rsb[:, fo * 512:(fo + 1) * 512],
                 ps_p[fo % 2][:], bi[:, fo:fo + 1],
                 x4f[qc % 2][:, fo * 512:(fo + 1) * 512], Op.add, Op.add)
            c.drain("vector")
            c.sem_inc("vector", "pc_vr", 1)
        emit_ln_tail(
            c, "pc", qc, rsb, lt,
            out_dst=lambda fo, _qc=qc: x5[:, _qc * TC:(_qc + 1) * TC]
                .rearrange("(fc p) t -> fc p t", p=128)[fo],
            rsb_ready=("pc_vr", lambda fo, _qc=qc: _qc * 8 + fo + 1),
        )
    if upto == "full":
        c.wait("sync", "pc_st", 16 * 64)



def ph_outT(c):
    x5 = c.dram("x5", [D, T], F32)
    out = c.dram("y_out", [T, D], F32, kind="ExternalOutput")

    idn = c.tiles["c_idnf"]
    x5f = [c.sbuf(f"pd_x5{i}", [128, 8 * 512], F32) for i in range(2)]
    ostg = [c.sbuf(f"pd_o{i}", [128, D], F32) for i in range(2)]
    ps_t = [c.psum(f"pd_ps{i}", [128, 128], F32) for i in range(2)]

    c.wait("tensor", "v_init", 1)
    j = 0
    gi = 0
    for qc in range(NTC):
        if qc >= 2:
            c.wait("sync", "pd_tr", 32 * (qc - 1))
        xs = x5[:, qc * TC:(qc + 1) * TC].rearrange("(kc p) t -> p kc t", p=128)
        c.dma("sync", x5f[qc % 2][:].rearrange("p (kc t) -> p kc t", kc=8), xs,
              sem="pd_x")
        c.wait("tensor", "pd_x", 16 * (qc + 1))
        for i in range(4):
            for fc in range(8):
                if j >= 2:
                    c.wait("tensor", "pd_v", j - 1)
                c.op("tensor", "transpose", ps_t[j % 2][:],
                     x5f[qc % 2][:, fc * 512 + i * 128:fc * 512 + (i + 1) * 128],
                     idn[:], inc=("pd_tr", 1))
                c.wait("vector", "pd_tr", j + 1)
                if gi >= 2 and fc == 0:
                    c.wait("vector", "pd_st", 16 * (gi - 1))
                c.op("vector", "tensor_copy",
                     ostg[gi % 2][:, fc * 128:(fc + 1) * 128], ps_t[j % 2][:])
                c.drain("vector")
                c.sem_inc("vector", "pd_v", 1)
                j += 1
            c.wait("sync", "pd_v", j)
            c.dma("sync", out[qc * TC + i * 128:qc * TC + (i + 1) * 128, :],
                  ostg[gi % 2][:], sem="pd_st")
            gi += 1
    c.wait("sync", "pd_st", 16 * 32)



def build_program(phase_fns, kinds=None):
    nc = bass.Bass("TRN2", target_bir_lowering=False)
    es = contextlib.ExitStack()
    c = Ctx(nc, es, kinds)
    setup_common(c)
    for fn in phase_fns:
        fn(c)
    block = es.enter_context(nc.Block())
    finalize(c, block)
    es.close()
    return nc


def build_fused(nphases=None, out_kinds=None):
    nc = bass.Bass("TRN2", target_bir_lowering=False)
    es = contextlib.ExitStack()
    kinds = dict(out_kinds or {"y_out": "ExternalOutput"})
    c = Ctx(nc, es, kinds, fused=True)
    setup_common(c)
    for name, fn, ins, outs in PHASES[:nphases]:
        fn(c)
        c.end_phase()
    block = es.enter_context(nc.Block())
    finalize(c, block)
    es.close()
    return nc


def host_layouts(inputs):
    f32 = np.float32
    g = {k: np.asarray(v, f32) for k, v in inputs.items() if k != "L0_attn_x"}
    out = {}
    out["L0_attn_w_qkv"] = np.ascontiguousarray(g["L0_attn_w_qkv"])
    out["b_qkv_t"] = np.ascontiguousarray(g["L0_attn_b_qkv"].reshape(24, 128).T)
    out["b_qkv_r"] = np.ascontiguousarray(g["L0_attn_b_qkv"].reshape(1, 3 * D))
    out["L0_attn_w_proj"] = np.ascontiguousarray(g["L0_attn_w_proj"])
    out["b_proj_t"] = np.ascontiguousarray(g["L0_attn_b_proj"].reshape(8, 128).T)
    out["w1T"] = np.ascontiguousarray(g["L0_ffn_w1_w"].T)
    out["w2T"] = np.ascontiguousarray(g["L0_ffn_w2_w"].T)
    out["w1s_t"] = np.ascontiguousarray(g["L0_ffn_w1_s"].reshape(32, 128).T)
    out["w2s_t"] = np.ascontiguousarray(g["L0_ffn_w2_s"].reshape(32, 128).T)
    out["h1b"] = np.ascontiguousarray(
        g["L0_ffn_h1_w"].T.reshape(32, 128, 32, 128).transpose(2, 0, 1, 3))
    out["h2b"] = np.ascontiguousarray(
        g["L0_ffn_h2_w"].T.reshape(32, 128, 32, 128).transpose(2, 0, 1, 3))
    out["h1s_t"] = np.ascontiguousarray(g["L0_ffn_h1_s"].reshape(32, 128).T)
    out["h2s_t"] = np.ascontiguousarray(g["L0_ffn_h2_s"].reshape(32, 128).T)
    out["w3T"] = np.ascontiguousarray(g["L0_ffn_w3_w"].T)
    out["w3s_t"] = np.ascontiguousarray(g["L0_ffn_w3_s"].reshape(8, 128).T)
    out["W_toT"] = np.ascontiguousarray(g["L1_attn_W_to"].T)
    out["W_fromT"] = np.ascontiguousarray(g["L1_attn_W_from"].T)
    out["freq2"] = np.ascontiguousarray(g["L1_attn_frequencies"].reshape(2, 128).T)
    out["ldec2"] = np.ascontiguousarray(g["L1_attn_log_decay"].reshape(2, 128).T)
    out["ln1g_t"] = np.ascontiguousarray(g["L1_attn_ln1_g"].reshape(8, 128).T)
    out["ln1b_t"] = np.ascontiguousarray(g["L1_attn_ln1_b"].reshape(8, 128).T)
    out["ln3g_t"] = np.ascontiguousarray(g["L1_attn_ln3_g"].reshape(8, 128).T)
    out["ln3b_t"] = np.ascontiguousarray(g["L1_attn_ln3_b"].reshape(8, 128).T)
    out["ffn1T"] = np.ascontiguousarray(g["L1_attn_W_ffn1"].T)
    out["bffn1_t"] = np.ascontiguousarray(g["L1_attn_b_ffn1"].reshape(32, 128).T)
    out["ffn2T"] = np.ascontiguousarray(g["L1_attn_W_ffn2"].T)
    out["bffn2_t"] = np.ascontiguousarray(g["L1_attn_b_ffn2"].reshape(8, 128).T)
    out["L1_ffn_kernel"] = np.ascontiguousarray(g["L1_ffn_kernel"])
    out["L1_ffn_a"] = np.ascontiguousarray(g["L1_ffn_a"])
    out["L1_ffn_b"] = np.ascontiguousarray(g["L1_ffn_b"])
    out["bias_t"] = np.ascontiguousarray(g["L1_ffn_bias"].reshape(8, 128).T)
    return out


PHASES = [
    ("qkv", ph_qkv, [], ["Qf", "Kf", "Vt"]),
    ("attn", ph_attn, ["Qf", "Kf", "Vt"], ["attnf"]),
    ("proj", ph_proj, ["attnf"], ["x1"]),
    ("ffnA", ph_ffnA, ["x1"], ["hA"]),
    ("h1", ph_h1, ["hA"], ["hB"]),
    ("h2", ph_h2, ["hB"], ["hC"]),
    ("w3ln", ph_w3ln, ["hC", "x1"], ["x2"]),
    ("lnwto", ph_lnwto, ["x2"], ["ubuf"]),
    ("scan", ph_scan, ["ubuf"], ["spec"]),
    ("wfrom", ph_wfrom_ffn1, ["spec", "x2"], ["r3", "gh"]),
    ("ffn2", ph_ffn2ln, ["gh", "x2", "r3"], ["x4"]),
    ("l1ffn", ph_l1ffn, ["x4"], ["x5"]),
    ("outT", ph_outT, ["x5"], ["y_out"]),
]

PHASE_EXT = {
    "qkv": ["x_in", "L0_attn_w_qkv", "b_qkv_t", "b_qkv_r"],
    "attn": [],
    "proj": ["x_in", "L0_attn_w_proj", "b_proj_t"],
    "ffnA": ["w1T", "w2T", "w1s_t", "w2s_t"],
    "h1": ["h1b", "h1s_t"],
    "h2": ["h2b", "h2s_t"],
    "w3ln": ["w3T", "w3s_t"],
    "lnwto": ["W_toT", "ln1g_t", "ln1b_t", "freq2"],
    "scan": ["ldec2", "freq2"],
    "wfrom": ["W_fromT", "ln3g_t", "ln3b_t", "ffn1T", "bffn1_t"],
    "ffn2": ["ffn2T", "bffn2_t"],
    "l1ffn": ["L1_ffn_kernel", "L1_ffn_a", "L1_ffn_b", "bias_t"],
    "outT": [],
}


FUSED = True


def kernel(**inputs):
    xfull = np.ascontiguousarray(np.asarray(inputs["L0_attn_x"], np.float32))
    hl = host_layouts(inputs)
    xs = [np.ascontiguousarray(xfull[b]) for b in range(NCORES)]

    if FUSED:
        nc = build_fused()
        ext = sorted({k for v in PHASE_EXT.values() for k in v})
        in_maps = []
        for b in range(NCORES):
            m = {k: hl[k] for k in ext if k != "x_in"}
            m["x_in"] = xs[b]
            in_maps.append(m)
        res = run_bass_kernel_spmd(nc, in_maps, core_ids=list(range(NCORES)))
        out = np.stack([res.results[b]["y_out"] for b in range(NCORES)], axis=0)
        return out.astype(np.float32)

    state = [dict() for _ in range(NCORES)]
    for b in range(NCORES):
        state[b]["x_in"] = xs[b]

    for name, fn, ins, outs in PHASES:
        kinds = {k: "ExternalInput" for k in ins}
        kinds.update({k: "ExternalOutput" for k in outs})
        nc = build_program([fn], kinds)
        in_maps = []
        for b in range(NCORES):
            m = {}
            for k in PHASE_EXT[name]:
                m[k] = state[b][k] if k in state[b] else hl[k]
            for k in ins:
                m[k] = state[b][k]
            in_maps.append(m)
        res = run_bass_kernel_spmd(nc, in_maps, core_ids=list(range(NCORES)))
        for b in range(NCORES):
            for k in outs:
                state[b][k] = res.results[b][k]

    out = np.stack([state[b]["y_out"] for b in range(NCORES)], axis=0)
    return out.astype(np.float32)


# version 06
# speedup vs baseline: 1.1844x; 1.0391x over previous
import contextlib
import numpy as np
import ml_dtypes
import concourse.bass as bass
import concourse.mybir as mybir
from concourse.bass_utils import run_bass_kernel_spmd
from concourse.alu_op_type import AluOpType as Op

F32 = mybir.dt.float32
BF16 = mybir.dt.bfloat16
AF = mybir.ActivationFunctionType
AX = mybir.AxisListType

T = 4096
D = 1024
H = 16
HD = 64
FH = 4096
FR = 256
NCORES = 8
TC = 512
NTC = T // TC
ENGINES = ("sync", "vector", "scalar", "tensor", "gpsimd")


ARENA_F32 = 49800
GLOBAL_SEMS = ("g_idn", "v_init", "bar", "bar2")


class Ctx:
    def __init__(self, nc, es, kinds=None, fused=False):
        self.nc = nc
        self.es = es
        self.kinds = kinds or {}
        self.fused = fused
        self.prog = {e: [] for e in ENGINES}
        self.hbm = {}
        self.tiles = {}
        self.psums = {}
        self.sems = {}
        self.cnt = {}
        if fused:
            self.arena = es.enter_context(nc.sbuf_tensor("ARENA", [128, ARENA_F32], F32))
            self.parena = es.enter_context(nc.psum_tensor("PARENA", [128, 4096], F32))
            self.sp = 0
            self.pp = 0
            self.sem_pool = [es.enter_context(nc.semaphore(f"S{i}")) for i in range(90)]
            self.sem_rr = 0
            self.sem_phys = {}
            self.sem_base = {}
            self.phys_cnt = [0] * 90
            self.phase_idx = 0

    def dram(self, name, shape, dtype, kind="Internal"):
        kind = self.kinds.get(name, kind)
        if name not in self.hbm:
            self.hbm[name] = self.nc.dram_tensor(name, list(shape), dtype, kind=kind).ap()
        return self.hbm[name]

    def sbuf(self, name, shape, dtype):
        if name in self.tiles:
            return self.tiles[name]
        if self.fused and not name.startswith("c_"):
            rows, n = shape
            nf = n if dtype == F32 else (n + 1) // 2
            assert self.sp + nf <= ARENA_F32, f"SBUF arena overflow at {name} ({self.sp}+{nf})"
            v = self.arena[:, self.sp:self.sp + nf]
            self.sp += nf
            if dtype != F32:
                v = v.bitcast(dtype)[:, 0:n]
            if rows != 128:
                v = v[0:rows, :]
            self.tiles[name] = v
        else:
            self.tiles[name] = self.es.enter_context(
                self.nc.sbuf_tensor(name, list(shape), dtype))
        return self.tiles[name]

    def psum(self, name, shape, dtype=F32):
        if name in self.psums:
            return self.psums[name]
        if self.fused:
            rows, n = shape
            assert dtype == F32
            nal = ((n + 511) // 512) * 512
            assert self.pp + nal <= 4096, f"PSUM arena overflow at {name}"
            v = self.parena[:, self.pp:self.pp + n]
            self.pp += nal
            if rows != 128:
                v = v[0:rows, :]
            self.psums[name] = v
        else:
            self.psums[name] = self.es.enter_context(
                self.nc.psum_tensor(name, list(shape), dtype))
        return self.psums[name]

    def sem(self, name):
        if name not in self.sems:
            if self.fused:
                if name in GLOBAL_SEMS:
                    idx = 86 + GLOBAL_SEMS.index(name)
                else:
                    idx = self.sem_rr % 86
                    self.sem_rr += 1
                self.sem_phys[name] = idx
                self.sem_base[name] = self.phys_cnt[idx]
                self.sems[name] = self.sem_pool[idx]
            else:
                self.sems[name] = self.es.enter_context(self.nc.semaphore(name))
            self.cnt[name] = 0
        return self.sems[name]

    def end_phase(self):
        assert self.fused
        pidx = self.phase_idx
        for e in ENGINES:
            if e in ("vector", "scalar", "gpsimd"):
                self.drain(e)
            if e != "tensor":
                self.sem_inc(e, "bar", 1)
        for e in ENGINES:
            self.wait(e, "bar", 4 * (pidx + 1))
        for name in [n for n in self.sems if n not in GLOBAL_SEMS]:
            assert self.phys_cnt[self.sem_phys[name]] < 60000, f"sem value overflow risk {name}"
            del self.sems[name]
            del self.cnt[name]
            del self.sem_base[name]
            del self.sem_phys[name]
        for k in [k for k in self.tiles if not k.startswith("c_")]:
            del self.tiles[k]
        self.psums = {}
        self.sp = 0
        self.pp = 0
        self.phase_idx += 1

    def inc(self, name, n=1):
        self.sem(name)
        self.cnt[name] += n
        if self.fused:
            self.phys_cnt[self.sem_phys[name]] += n
        return self.cnt[name]

    def val(self, name):
        self.sem(name)
        return self.cnt[name]

    def emit(self, engine, f):
        self.prog[engine].append(f)

    def wait(self, engine, semname, value):
        s = self.sem(semname)
        if self.fused:
            value = value + self.sem_base[semname]
        self.emit(engine, lambda eng, _s=s, _v=value: eng.wait_ge(_s, _v))

    def op(self, engine, fname, *args, inc=None, **kw):
        if inc is not None:
            tot = self.inc(inc[0], inc[1])
            s = self.sems[inc[0]]

            def f(eng, _f=fname, _a=args, _k=kw, _s=s, _n=inc[1]):
                getattr(eng, _f)(*_a, **_k).then_inc(_s, _n)
            self.emit(engine, f)
            return tot

        def f(eng, _f=fname, _a=args, _k=kw):
            getattr(eng, _f)(*_a, **_k)
        self.emit(engine, f)
        return None

    def dma(self, engine, dst, src, sem="ld", **kw):
        return self.op(engine, "dma_start", inc=(sem, 16), out=dst, in_=src, **kw)

    def drain(self, engine):
        self.emit(engine, lambda eng: eng.drain())

    def sem_inc(self, engine, semname, n=1):
        tot = self.inc(semname, n)
        s = self.sems[semname]
        assert engine != "tensor", "standalone sem_inc on PE is device-fatal"
        self.emit(engine, lambda eng, _s=s, _n=n: eng.sem_inc(_s, _n))
        return tot


def finalize(ctx, block):
    @block.sync
    def _(eng):
        for f in ctx.prog["sync"]:
            f(eng)

    @block.vector
    def _(eng):
        for f in ctx.prog["vector"]:
            f(eng)

    @block.scalar
    def _(eng):
        for f in ctx.prog["scalar"]:
            f(eng)

    @block.tensor
    def _(eng):
        for f in ctx.prog["tensor"]:
            f(eng)

    @block.gpsimd
    def _(eng):
        for f in ctx.prog["gpsimd"]:
            f(eng)


def setup_common(c):
    iot = c.sbuf("c_iot", [128, 128], F32)
    idn = c.sbuf("c_idnf", [128, 128], F32)
    idnb = c.sbuf("c_idnb", [128, 128], BF16)
    onesr = c.sbuf("c_onesr", [1, 128], F32)
    ones128 = c.sbuf("c_ones128", [128, 1], F32)
    sclr = c.sbuf("c_sclr", [1, 128], F32)
    c.op("gpsimd", "iota", iot[:], pattern=[[1, 128]], base=0,
         channel_multiplier=-1, allow_small_or_imprecise_dtypes=True, inc=("g_idn", 1))
    c.wait("vector", "g_idn", 1)
    c.op("vector", "tensor_scalar", idn[:], iot[:], 0.0, None, Op.is_equal)
    c.op("vector", "tensor_scalar", idnb[:], iot[:], 0.0, None, Op.is_equal)
    c.op("vector", "memset", onesr[:], 1.0)
    c.op("vector", "memset", ones128[:], 1.0)
    c.op("vector", "memset", sclr[:], -0.125)
    eps = c.sbuf("c_eps", [128, 1], F32)
    c.op("vector", "memset", eps[:], 1e-5)
    c.drain("vector")
    c.sem_inc("vector", "v_init", 1)



def ph_qkv(c, scratch_kind="ExternalOutput"):
    x = c.dram("x_in", [T, D], F32, kind="ExternalInput")
    wq = c.dram("L0_attn_w_qkv", [D, 3 * D], F32, kind="ExternalInput")
    bqt_h = c.dram("b_qkv_t", [128, 24], F32, kind="ExternalInput")
    bqr_h = c.dram("b_qkv_r", [1, 3 * D], F32, kind="ExternalInput")
    Qf = c.dram("Qf", [D, T], BF16, kind=scratch_kind)
    Kf = c.dram("Kf", [D, T], BF16, kind=scratch_kind)
    Vt = c.dram("Vt", [T, D], BF16, kind=scratch_kind)

    idn = c.tiles["c_idnf"]
    onesr = c.tiles["c_onesr"]
    wsb = c.sbuf("q_w", [128, 8 * 3072], BF16)
    bqt = c.sbuf("q_bt", [128, 24], F32)
    bvrow = c.sbuf("q_bvrow", [1, D], F32)
    vbias = c.sbuf("q_vbias", [128, D], F32)
    xtok = [c.sbuf(f"q_xtok{i}", [128, 4 * D], F32) for i in range(2)]
    xfm = c.sbuf("q_xfm", [128, 8 * TC], BF16)
    stg = [c.sbuf(f"q_stg{i}", [128, TC], BF16) for i in range(2)]
    vstg = [c.sbuf(f"q_vstg{i}", [128, D], BF16) for i in range(2)]

    ps_tr = [c.psum(f"q_pstr{i}", [128, 128], F32) for i in range(2)]
    ps_mm = [c.psum(f"q_psmm{i}", [128, TC], F32) for i in range(2)]
    ps_v = [c.psum(f"q_psv{i}", [128, D], F32) for i in range(2)]

    for kc in range(8):
        c.dma("gpsimd", wsb[:, kc * 3072:(kc + 1) * 3072],
              wq[kc * 128:(kc + 1) * 128, :], sem="q_w")
    c.dma("sync", bqt[:], bqt_h, sem="q_b")
    c.dma("sync", bvrow[:], bqr_h[:, 2 * D:3 * D], sem="q_br")

    c.wait("tensor", "v_init", 1)
    c.wait("tensor", "q_br", 16)
    for half in range(2):
        c.op("tensor", "matmul", ps_v[0][:, half * 512:(half + 1) * 512],
             onesr[:], bvrow[:, half * 512:(half + 1) * 512],
             start=True, stop=True, inc=("q_pev", 1) if half == 1 else None)
    c.wait("vector", "q_pev", 1)
    c.op("vector", "tensor_copy", vbias[:], ps_v[0][:])
    c.drain("vector")
    c.sem_inc("vector", "q_vv", 1)

    c.wait("scalar", "q_b", 16)
    c.wait("tensor", "q_w", 16 * 8)

    jtr = 0
    gm = 0
    gv = 0
    for tc in range(NTC):
        if tc >= 2:
            c.wait("sync", "q_pet", 32 * (tc - 1))
        src = x[tc * TC:(tc + 1) * TC, :].rearrange("(i p) f -> p i f", p=128)
        c.dma("sync", xtok[tc % 2][:].rearrange("p (i f) -> p i f", i=4), src, sem="q_x")

        c.wait("tensor", "q_x", 16 * (tc + 1))
        for i in range(4):
            for fc in range(8):
                if jtr >= 2:
                    c.wait("tensor", "q_vx", jtr - 1)
                c.op("tensor", "transpose", ps_tr[jtr % 2][:],
                     xtok[tc % 2][:, i * D + fc * 128:i * D + (fc + 1) * 128],
                     idn[:], inc=("q_pet", 1))
                c.wait("vector", "q_pet", jtr + 1)
                c.op("vector", "tensor_copy",
                     xfm[:, fc * TC + i * 128:fc * TC + (i + 1) * 128],
                     ps_tr[jtr % 2][:])
                c.drain("vector")
                c.sem_inc("vector", "q_vx", 1)
                jtr += 1

        for m in range(16):
            if gm >= 2:
                c.wait("tensor", "q_a", gm - 1)
            for kc in range(8):
                c.op("tensor", "matmul", ps_mm[gm % 2][:],
                     wsb[:, kc * 3072 + m * 128:kc * 3072 + (m + 1) * 128],
                     xfm[:, kc * TC:(kc + 1) * TC],
                     start=(kc == 0), stop=(kc == 7),
                     inc=("q_pem", 1) if kc == 7 else None)
            c.wait("scalar", "q_pem", gm + 1)
            if gm >= 2:
                c.wait("scalar", "q_st", 16 * (gm - 1))
            c.op("scalar", "add", stg[gm % 2][:], ps_mm[gm % 2][:],
                 bqt[:, m:m + 1])
            c.drain("scalar")
            c.sem_inc("scalar", "q_a", 1)
            c.wait("sync", "q_a", gm + 1)
            dst = (Qf if m < 8 else Kf)
            mm = m if m < 8 else m - 8
            c.dma("sync", dst[mm * 128:(mm + 1) * 128, tc * TC:(tc + 1) * TC],
                  stg[gm % 2][:], sem="q_st")
            gm += 1

        for i in range(4):
            if gv >= 2:
                c.wait("tensor", "q_vv", 1 + (gv - 1))
            for kc in range(8):
                for half in range(2):
                    c.op("tensor", "matmul",
                         ps_v[gv % 2][:, half * 512:(half + 1) * 512],
                         xfm[:, kc * TC + i * 128:kc * TC + (i + 1) * 128],
                         wsb[:, kc * 3072 + 2048 + half * 512:kc * 3072 + 2048 + (half + 1) * 512],
                         start=(kc == 0), stop=(kc == 7),
                         inc=("q_pev", 1) if (kc == 7 and half == 1) else None)
            c.wait("vector", "q_pev", 1 + gv + 1)
            if gv >= 2:
                c.wait("vector", "q_stv", 16 * (gv - 1))
            c.op("vector", "tensor_tensor", vstg[gv % 2][:], ps_v[gv % 2][:],
                 vbias[:], Op.add)
            c.drain("vector")
            c.sem_inc("vector", "q_vv", 1)
            c.wait("sync", "q_vv", 1 + gv + 1)
            c.dma("sync", Vt[tc * TC + i * 128:tc * TC + (i + 1) * 128, :],
                  vstg[gv % 2][:], sem="q_stv")
            gv += 1

    c.wait("sync", "q_st", 16 * 128)
    c.wait("sync", "q_stv", 16 * 32)



def ph_attn(c):
    Qf = c.dram("Qf", [D, T], BF16)
    Kf = c.dram("Kf", [D, T], BF16)
    Vt = c.dram("Vt", [T, D], BF16)
    attnf = c.dram("attnf", [D, T], BF16)

    onesr = c.tiles["c_onesr"]
    ones128 = c.tiles["c_ones128"]
    sclr = c.tiles["c_sclr"]
    qsb = [c.sbuf(f"a_q{i}", [64, T], BF16) for i in range(2)]
    ksb = [c.sbuf(f"a_k{i}", [64, T], BF16) for i in range(2)]
    vaug = [c.sbuf(f"a_v{i}", [128, 32 * 65], BF16) for i in range(2)]
    ssb = [c.sbuf(f"a_s{i}", [128, 16384], BF16) for i in range(2)]
    sqt = c.sbuf("a_sqt", [64, T], F32)
    nmx = c.sbuf("a_nmx", [1, 2], F32)
    ntmp = c.sbuf("a_ntmp", [1, 4], F32)
    negb = [c.sbuf(f"a_negb{i}", [128, 1], F32) for i in range(2)]
    t_r = c.sbuf("a_r", [1, TC], F32)
    t_rb = c.sbuf("a_rb", [64, TC], F32)
    astg = [c.sbuf(f"a_stg{i}", [64, TC], BF16) for i in range(2)]
    R = [c.psum(f"a_R{i}", [128, 1024], F32) for i in range(3)]
    avp = c.psum("a_av", [128, 512], F32)
    nbp = c.psum("a_nb", [128, 512], F32)

    for i in range(2):
        c.op("vector", "memset", vaug[i][:, 64:32 * 65:65], 1.0)
    c.drain("vector")
    c.sem_inc("vector", "a_vone", 1)
    c.wait("gpsimd", "a_vone", 1)

    NIT = H * NTC
    for it in range(NIT + 1):
        if it < NIT:
            h, qc = divmod(it, NTC)
            pair = it % 2
            hb = h % 2
            if qc == 0:
                if h >= 2:
                    c.wait("sync", "a_av", (h - 1) * NTC)
                    c.wait("gpsimd", "a_av", (h - 1) * NTC)
                    c.wait("sync", "a_qk", (h - 1) * NTC * 16)
                c.dma("sync", qsb[hb][:], Qf[64 * h:64 * (h + 1), :], sem="a_ld")
                c.dma("sync", ksb[hb][:], Kf[64 * h:64 * (h + 1), :], sem="a_ld")
                vdst = vaug[hb][:].rearrange("p (kc f) -> p kc f", kc=32)[:, :, 0:64]
                vsrc = Vt[:, 64 * h:64 * (h + 1)].rearrange("(kc p) f -> p kc f", p=128)
                c.dma("gpsimd", vdst, vsrc, sem="a_ld")
                c.wait("scalar", "a_ld", 48 * (h + 1))
                if h >= 1:
                    c.wait("scalar", "a_nb", h)
                    c.wait("vector", "a_lnd", h)
                c.op("vector", "memset", nmx[:], 0.0)
                c.drain("vector")
                c.sem_inc("vector", "a_nz", 1)
                nping = 0
                for side in range(2):
                    src = qsb[hb] if side == 0 else ksb[hb]
                    if side == 1:
                        c.wait("scalar", "a_nm", 16 * h + 8)
                    c.op("scalar", "activation", sqt[:], src[:], AF.Square)
                    c.drain("scalar")
                    c.sem_inc("scalar", "a_sq", 1)
                    c.wait("tensor", "a_sq", 2 * h + side + 1)
                    if it >= 1:
                        c.wait("tensor", "a_fin", it - 1)
                    c.wait("tensor", "a_nz", h + 1)
                    for n in range(8):
                        if nping >= 1:
                            c.wait("tensor", "a_nr", 16 * h + nping)
                        c.op("tensor", "matmul", nbp[0:1, 0:512], ones128[0:64, 0:1],
                             sqt[:, n * 512:(n + 1) * 512], start=True, stop=True,
                             inc=("a_nm", 1))
                        nping += 1
                        c.wait("vector", "a_nm", 16 * h + nping)
                        c.op("vector", "tensor_reduce", ntmp[0:1, 0:1], nbp[0:1, 0:512],
                             AX.X, Op.max)
                        c.drain("vector")
                        c.op("vector", "tensor_tensor", nmx[0:1, side:side + 1],
                             nmx[0:1, side:side + 1], ntmp[0:1, 0:1], Op.max)
                        c.drain("vector")
                        c.sem_inc("vector", "a_nr", 1)
                c.wait("scalar", "a_nr", 16 * (h + 1))
                c.op("scalar", "activation", ntmp[0:1, 2:3], nmx[0:1, 0:1], AF.Ln)
                c.drain("scalar")
                c.op("scalar", "activation", ntmp[0:1, 3:4], nmx[0:1, 1:2], AF.Ln)
                c.drain("scalar")
                c.sem_inc("scalar", "a_lnd", 1)
                c.wait("vector", "a_lnd", h + 1)
                c.op("vector", "tensor_add", ntmp[0:1, 0:1], ntmp[0:1, 2:3], ntmp[0:1, 3:4])
                c.drain("vector")
                c.sem_inc("vector", "a_sumd", 1)
                c.wait("scalar", "a_sumd", h + 1)
                if h >= 1:
                    c.wait("scalar", "a_pb1", h)
                c.op("scalar", "activation", ntmp[0:1, 1:2], ntmp[0:1, 0:1], AF.Exp,
                     scale=0.5)
                c.drain("scalar")
                c.sem_inc("scalar", "a_cb", 1)
                c.wait("tensor", "a_cb", h + 1)
                c.op("tensor", "matmul", nbp[:, 0:1], sclr[:], ntmp[0:1, 1:2],
                     start=True, stop=True, inc=("a_pb1", 1))
                c.wait("vector", "a_pb1", h + 1)
                if h >= 2:
                    c.wait("vector", "a_xv", 128 * (h - 1))
                c.op("vector", "tensor_copy", negb[hb][:], nbp[:, 0:1])
                c.drain("vector")
                c.sem_inc("vector", "a_nb", 1)
            c.wait("tensor", "a_ld", 48 * (h + 1))
            for gg in range(4):
                eg = it * 16 + gg
                r = eg % 3
                if eg >= 3:
                    c.wait("tensor", "a_xv", eg - 2)
                for kk in range(2):
                    kc = 2 * gg + kk
                    c.op("tensor", "matmul", R[r][:, kk * 512:(kk + 1) * 512],
                         ksb[hb][:, kc * 128:(kc + 1) * 128],
                         qsb[hb][:, qc * 512:(qc + 1) * 512],
                         start=True, stop=True,
                         inc=("a_qk", 1) if kk == 1 else None)
                c.wait("scalar", "a_qk", eg + 1)
                if gg == 0:
                    c.wait("scalar", "a_nb", h + 1)
                    if it >= 2:
                        c.wait("scalar", "a_av", it - 1)
                c.op("scalar", "activation", ssb[pair][:, gg * 1024:(gg + 1) * 1024],
                     R[r][:], AF.Exp, bias=negb[hb][:, 0:1], scale=0.125,
                     inc=("a_xv", 1))

        if it >= 1:
            hq = it - 1
            hB, qcB = divmod(hq, NTC)
            pb = hq % 2
            hbB = hB % 2
            c.wait("tensor", "a_xv", hq * 16 + 16)
            if hq >= 1:
                c.wait("tensor", "a_fin", hq)
            for kc in range(32):
                c.op("tensor", "matmul", avp[0:65, 0:512],
                     vaug[hbB][:, kc * 65:kc * 65 + 65],
                     ssb[pb][:, kc * 512:(kc + 1) * 512],
                     start=(kc == 0), stop=(kc == 31),
                     inc=("a_av", 1) if kc == 31 else None)
            c.wait("vector", "a_av", hq + 1)
            c.op("vector", "reciprocal", t_r[:], avp[64:65, 0:512])
            c.drain("vector")
            c.sem_inc("vector", "a_vr", 1)
            c.wait("tensor", "a_vr", hq + 1)
            c.wait("tensor", "a_rb", hq)
            c.op("tensor", "matmul", nbp[0:64, 0:512], onesr[:, 0:64], t_r[:],
                 start=True, stop=True, inc=("a_pb2", 1))
            c.wait("vector", "a_pb2", hq + 1)
            c.op("vector", "tensor_copy", t_rb[:], nbp[0:64, 0:512])
            c.drain("vector")
            c.sem_inc("vector", "a_rb", 1)
            if hq >= 2:
                c.wait("vector", "a_st", 16 * (hq - 1))
            c.op("vector", "tensor_tensor", astg[hq % 2][:], avp[0:64, 0:512],
                 t_rb[:], Op.mult)
            c.drain("vector")
            c.sem_inc("vector", "a_fin", 1)
            c.wait("sync", "a_fin", hq + 1)
            c.dma("sync", attnf[64 * hB:64 * (hB + 1), qcB * 512:(qcB + 1) * 512],
                  astg[hq % 2][:], sem="a_st")

        if it < NIT:
            h, qc = divmod(it, NTC)
            pair = it % 2
            hb = h % 2
            for gg in range(4, 16):
                eg = it * 16 + gg
                r = eg % 3
                if eg >= 3:
                    c.wait("tensor", "a_xv", eg - 2)
                for kk in range(2):
                    kc = 2 * gg + kk
                    c.op("tensor", "matmul", R[r][:, kk * 512:(kk + 1) * 512],
                         ksb[hb][:, kc * 128:(kc + 1) * 128],
                         qsb[hb][:, qc * 512:(qc + 1) * 512],
                         start=True, stop=True,
                         inc=("a_qk", 1) if kk == 1 else None)
                c.wait("scalar", "a_qk", eg + 1)
                if gg == 0:
                    c.wait("scalar", "a_nb", h + 1)
                    if it >= 2:
                        c.wait("scalar", "a_av", it - 1)
                c.op("scalar", "activation", ssb[pair][:, gg * 1024:(gg + 1) * 1024],
                     R[r][:], AF.Exp, bias=negb[hb][:, 0:1], scale=0.125,
                     inc=("a_xv", 1))

    c.wait("sync", "a_st", 16 * NIT)



def emit_ln_tail(c, P, qi, rsb, tiles, out_dst, gslice=None, bslice=None,
                 rsb_ready=None, bfstg=None, bf_dst=None, eps=1e-5, nfo=8):
    (sq1, mrow, sq2, vrow, lnrow, rstdrow, xstg, ps_s0, ps_s1, ps_b0, ps_b1) = tiles
    ones128 = c.tiles["c_ones128"]
    onesr = c.tiles["c_onesr"]
    for fo in range(nfo):
        if rsb_ready is not None:
            c.wait("scalar", rsb_ready[0], rsb_ready[1](fo))
        if qi * nfo + fo >= 1:
            c.wait("scalar", f"{P}_psq", qi * nfo + fo)
        c.op("scalar", "activation", sq1[:], rsb[:, fo * 512:(fo + 1) * 512], AF.Square)
        c.drain("scalar")
        c.sem_inc("scalar", f"{P}_sq", 1)
        if rsb_ready is not None:
            c.wait("tensor", rsb_ready[0], rsb_ready[1](fo))
        if qi > 0 or fo > 0:
            c.wait("tensor", f"{P}_row", qi)
        c.op("tensor", "matmul", ps_s0[0:1, :], ones128[:], rsb[:, fo * 512:(fo + 1) * 512],
             start=(fo == 0), stop=(fo == nfo - 1),
             inc=(f"{P}_ps", 1) if fo == nfo - 1 else None)
        c.wait("tensor", f"{P}_sq", qi * nfo + fo + 1)
        c.op("tensor", "matmul", ps_s1[0:1, :], ones128[:], sq1[:],
             start=(fo == 0), stop=(fo == nfo - 1),
             inc=(f"{P}_psq", 1))
    c.wait("scalar", f"{P}_ps", qi + 1)
    c.op("scalar", "mul", mrow[0:1, :], ps_s0[0:1, :], 1.0 / D)
    c.drain("scalar")
    c.sem_inc("scalar", f"{P}_m", 1)
    c.wait("vector", f"{P}_m", qi + 1)
    c.wait("vector", f"{P}_psq", nfo * (qi + 1))
    c.op("vector", "tensor_mul", sq2[0:1, :], mrow[0:1, :], mrow[0:1, :])
    c.drain("vector")
    c.op("vector", "scalar_tensor_tensor", vrow[0:1, :], ps_s1[0:1, :], 1.0 / D,
         sq2[0:1, :], Op.mult, Op.subtract)
    c.drain("vector")
    c.sem_inc("vector", f"{P}_v", 1)
    c.wait("scalar", f"{P}_v", qi + 1)
    c.op("scalar", "activation", lnrow[0:1, :], vrow[0:1, :], AF.Ln,
         bias=c.tiles["c_eps"][0:1, 0:1])
    c.drain("scalar")
    c.op("scalar", "activation", rstdrow[0:1, :], lnrow[0:1, :], AF.Exp, scale=-0.5)
    c.drain("scalar")
    c.sem_inc("scalar", f"{P}_row", 1)
    c.wait("tensor", f"{P}_row", qi + 1)
    c.op("tensor", "matmul", ps_b0[:, 0:512], onesr[:], mrow[0:1, :],
         start=True, stop=True)
    c.op("tensor", "matmul", ps_b1[:, 0:512], onesr[:], rstdrow[0:1, :],
         start=True, stop=True, inc=(f"{P}_bc", 1))
    for fo in range(nfo):
        c.wait("vector", f"{P}_bc", qi + 1)
        if out_dst is not None and qi * nfo + fo >= 2:
            c.wait("vector", f"{P}_st", 16 * (qi * nfo + fo - 1))
        st = xstg[(qi * nfo + fo) % 2]
        if bf_dst is not None:
            final = bf_dst(fo)
        elif bfstg is not None:
            final = bfstg[(qi * nfo + fo) % 2]
        else:
            final = st
        c.op("vector", "tensor_tensor", st[:], rsb[:, fo * 512:(fo + 1) * 512],
             ps_b0[:, 0:512], Op.subtract)
        c.drain("vector")
        if gslice is not None:
            c.op("vector", "tensor_tensor", st[:], st[:], ps_b1[:, 0:512], Op.mult)
            c.drain("vector")
            c.op("vector", "scalar_tensor_tensor", final[:], st[:], gslice(fo),
                 bslice(fo).broadcast_to([128, 512]), Op.mult, Op.add)
            c.drain("vector")
        else:
            c.op("vector", "tensor_tensor", final[:], st[:], ps_b1[:, 0:512], Op.mult)
            c.drain("vector")
        c.sem_inc("vector", f"{P}_va", 1)
        if out_dst is not None:
            c.wait("sync", f"{P}_va", qi * nfo + fo + 1)
            c.dma("sync", out_dst(fo), final[:], sem=f"{P}_st")
    c.wait("tensor", f"{P}_va", (qi + 1) * nfo)


def ln_tiles(c, P):
    return (
        c.sbuf(f"{P}_sq1", [128, 512], F32),
        c.sbuf(f"{P}_mrow", [1, 512], F32),
        c.sbuf(f"{P}_sq2", [1, 512], F32),
        c.sbuf(f"{P}_vrow", [1, 512], F32),
        c.sbuf(f"{P}_lnrow", [1, 512], F32),
        c.sbuf(f"{P}_rstd", [1, 512], F32),
        [c.sbuf(f"{P}_xstg{i}", [128, 512], F32) for i in range(2)],
        c.psum(f"{P}_pss0", [128, 512], F32),
        c.psum(f"{P}_pss1", [128, 512], F32),
        c.psum(f"{P}_psb0", [128, 512], F32),
        c.psum(f"{P}_psb1", [128, 512], F32),
    )



def ph_proj(c):
    x = c.dram("x_in", [T, D], F32, kind="ExternalInput")
    wp = c.dram("L0_attn_w_proj", [D, D], F32, kind="ExternalInput")
    bp_h = c.dram("b_proj_t", [128, 8], F32, kind="ExternalInput")
    attnf = c.dram("attnf", [D, T], BF16)
    x1 = c.dram("x1", [D, T], F32)

    idn = c.tiles["c_idnf"]
    wsb = c.sbuf("p3_w", [128, 8 * D], BF16)
    bpt = c.sbuf("p3_b", [128, 8], F32)
    xtok = [c.sbuf(f"p3_xtok{i}", [128, 4 * D], F32) for i in range(2)]
    att = [c.sbuf(f"p3_att{i}", [128, 8 * 512], BF16) for i in range(2)]
    xf32 = [c.sbuf(f"p3_xf{i}", [128, 8 * 512], F32) for i in range(2)]
    rsb = c.sbuf("p3_rsb", [128, 8 * 512], F32)
    lt = ln_tiles(c, "p3")
    ps_b0, ps_b1 = lt[-2], lt[-1]
    ps_p = [c.psum(f"p3_psp{i}", [128, 512], F32) for i in range(2)]

    for kc in range(8):
        c.dma("gpsimd", wsb[:, kc * D:(kc + 1) * D], wp[kc * 128:(kc + 1) * 128, :],
              sem="p3_w")
    c.dma("sync", bpt[:], bp_h, sem="p3_b")
    c.wait("tensor", "p3_w", 16 * 8)
    c.wait("vector", "p3_b", 16)
    c.wait("tensor", "v_init", 1)

    jtr = 0
    for qc in range(NTC):
        if qc >= 2:
            c.wait("sync", "p3_tr", 32 * (qc - 1))
            c.wait("sync", "p3_pm", 8 * (qc - 1))
        xs = x[qc * TC:(qc + 1) * TC, :].rearrange("(i p) f -> p i f", p=128)
        c.dma("sync", xtok[qc % 2][:].rearrange("p (i f) -> p i f", i=4), xs, sem="p3_x")
        asrc = attnf[:, qc * TC:(qc + 1) * TC].rearrange("(kc p) t -> p kc t", p=128)
        c.dma("sync", att[qc % 2][:].rearrange("p (kc t) -> p kc t", kc=8), asrc,
              sem="p3_a")

        c.wait("tensor", "p3_x", 16 * (qc + 1))
        if qc >= 1:
            c.wait("tensor", "p3_va", qc * 8)
        for i in range(4):
            for fc in range(8):
                tt = (ps_b0 if jtr % 2 == 0 else ps_b1)
                if jtr >= 2:
                    c.wait("tensor", "p3_vx", jtr - 1)
                c.op("tensor", "transpose", tt[:, 0:128],
                     xtok[qc % 2][:, i * D + fc * 128:i * D + (fc + 1) * 128],
                     idn[:], inc=("p3_tr", 1))
                c.wait("vector", "p3_tr", jtr + 1)
                c.op("vector", "tensor_copy",
                     xf32[qc % 2][:, fc * 512 + i * 128:fc * 512 + (i + 1) * 128],
                     tt[:, 0:128])
                c.drain("vector")
                c.sem_inc("vector", "p3_vx", 1)
                jtr += 1

        c.wait("tensor", "p3_a", 16 * (qc + 1))
        for fo in range(8):
            g = qc * 8 + fo
            if g >= 2:
                c.wait("tensor", "p3_vr", g - 1)
            for kc in range(8):
                c.op("tensor", "matmul", ps_p[fo % 2][:],
                     wsb[:, kc * D + fo * 128:kc * D + (fo + 1) * 128],
                     att[qc % 2][:, kc * 512:(kc + 1) * 512],
                     start=(kc == 0), stop=(kc == 7),
                     inc=("p3_pm", 1) if kc == 7 else None)
            c.wait("vector", "p3_pm", g + 1)
            c.op("vector", "scalar_tensor_tensor", rsb[:, fo * 512:(fo + 1) * 512],
                 ps_p[fo % 2][:], bpt[:, fo:fo + 1],
                 xf32[qc % 2][:, fo * 512:(fo + 1) * 512], Op.add, Op.add)
            c.drain("vector")
            c.sem_inc("vector", "p3_vr", 1)

        emit_ln_tail(
            c, "p3", qc, rsb, lt,
            out_dst=lambda fo, _qc=qc: x1[:, _qc * TC:(_qc + 1) * TC]
                .rearrange("(fc p) t -> fc p t", p=128)[fo],
            rsb_ready=("p3_vr", lambda fo, _qc=qc: _qc * 8 + fo + 1),
        )

    c.wait("sync", "p3_st", 16 * 64)



def ph_ffnA(c):
    x1 = c.dram("x1", [D, T], F32)
    w1h = c.dram("w1T", [D, FH], F32, kind="ExternalInput")
    w2h = c.dram("w2T", [D, FH], F32, kind="ExternalInput")
    s1h = c.dram("w1s_t", [128, 32], F32, kind="ExternalInput")
    s2h = c.dram("w2s_t", [128, 32], F32, kind="ExternalInput")
    hA = c.dram("hA", [FH, T], BF16)

    w1 = c.sbuf("f4_w1", [128, 8 * FH], BF16)
    w2 = c.sbuf("f4_w2", [128, 8 * FH], BF16)
    s1 = c.sbuf("f4_s1", [128, 32], F32)
    s2 = c.sbuf("f4_s2", [128, 32], F32)
    x1b = [c.sbuf(f"f4_x{i}", [128, 8 * 512], BF16) for i in range(2)]
    tA = c.sbuf("f4_tA", [128, 512], F32)
    tB = c.sbuf("f4_tB", [128, 512], F32)
    hstg = [c.sbuf(f"f4_h{i}", [128, 512], BF16) for i in range(2)]
    psA = [c.psum(f"f4_psA{i}", [128, 512], F32) for i in range(2)]
    psB = [c.psum(f"f4_psB{i}", [128, 512], F32) for i in range(2)]

    for kc in range(8):
        c.dma("gpsimd", w1[:, kc * FH:(kc + 1) * FH], w1h[kc * 128:(kc + 1) * 128, :], sem="f4_w")
        c.dma("gpsimd", w2[:, kc * FH:(kc + 1) * FH], w2h[kc * 128:(kc + 1) * 128, :], sem="f4_w")
    c.dma("sync", s1[:], s1h, sem="f4_s")
    c.dma("sync", s2[:], s2h, sem="f4_s")
    c.wait("tensor", "f4_w", 16 * 16)
    c.wait("scalar", "f4_s", 32)

    g = 0
    for qc in range(NTC):
        if qc >= 2:
            c.wait("gpsimd", "f4_pm", 32 * (qc - 1))
        xs = x1[:, qc * TC:(qc + 1) * TC].rearrange("(kc p) t -> p kc t", p=128)
        c.dma("gpsimd", x1b[qc % 2][:].rearrange("p (kc t) -> p kc t", kc=8), xs, sem="f4_x")
        c.wait("tensor", "f4_x", 16 * (qc + 1))
        for m in range(32):
            if g >= 2:
                c.wait("tensor", "f4_v", g - 1)
            for kc in range(8):
                c.op("tensor", "matmul", psA[g % 2][:],
                     w1[:, kc * FH + m * 128:kc * FH + (m + 1) * 128],
                     x1b[qc % 2][:, kc * 512:(kc + 1) * 512],
                     start=(kc == 0), stop=(kc == 7))
                c.op("tensor", "matmul", psB[g % 2][:],
                     w2[:, kc * FH + m * 128:kc * FH + (m + 1) * 128],
                     x1b[qc % 2][:, kc * 512:(kc + 1) * 512],
                     start=(kc == 0), stop=(kc == 7),
                     inc=("f4_pm", 1) if kc == 7 else None)
            c.wait("scalar", "f4_pm", g + 1)
            c.wait("scalar", "f4_v", g)
            c.op("scalar", "activation", tA[:], psA[g % 2][:], AF.Silu,
                 scale=s1[:, m:m + 1])
            c.drain("scalar")
            c.op("scalar", "mul", tB[:], psB[g % 2][:], s2[:, m:m + 1])
            c.drain("scalar")
            c.sem_inc("scalar", "f4_a", 1)
            c.wait("vector", "f4_a", g + 1)
            if g >= 2:
                c.wait("vector", "f4_st", 16 * (g - 1))
            c.op("vector", "tensor_tensor", hstg[g % 2][:], tA[:], tB[:], Op.mult)
            c.drain("vector")
            c.sem_inc("vector", "f4_v", 1)
            c.wait("sync", "f4_v", g + 1)
            c.dma("sync", hA[m * 128:(m + 1) * 128, qc * TC:(qc + 1) * TC],
                  hstg[g % 2][:], sem="f4_st")
            g += 1
    c.wait("sync", "f4_st", 16 * 256)



def ph_hmid(c, P, win_name, s_name, hin_name, hout_name):
    wb_h = c.dram(win_name, [32, 32, 128, 128], F32, kind="ExternalInput")
    sh = c.dram(s_name, [128, 32], F32, kind="ExternalInput")
    hIn = c.dram(hin_name, [FH, T], BF16)
    hOut = c.dram(hout_name, [FH, T], BF16)

    hAt = c.sbuf(f"{P}_hin", [128, 32 * 2048], BF16)
    wblk = [c.sbuf(f"{P}_w{i}", [128, 32 * 128], BF16) for i in range(2)]
    ssb = c.sbuf(f"{P}_s", [128, 32], F32)
    hstg = [c.sbuf(f"{P}_h{i}", [128, 512], BF16) for i in range(2)]
    ps = [c.psum(f"{P}_ps{i}", [128, 512], F32) for i in range(2)]

    c.dma("sync", ssb[:], sh, sem=f"{P}_sld")
    c.wait("scalar", f"{P}_sld", 16)

    g = 0
    for half in range(2):
        if half >= 1:
            c.wait("sync", f"{P}_pm", 128 * half)
        hs = hIn[:, half * 2048:(half + 1) * 2048].rearrange("(kc p) t -> p kc t", p=128)
        c.dma("sync", hAt[:].rearrange("p (kc t) -> p kc t", kc=32), hs, sem=f"{P}_hx")
        c.wait("gpsimd", f"{P}_hx", 16 * (half + 1))
        for m in range(32):
            gm = half * 32 + m
            if gm >= 2:
                c.wait("gpsimd", f"{P}_pm", 4 * (gm - 1))
            c.dma("gpsimd", wblk[gm % 2][:].rearrange("p (kc mo) -> p kc mo", kc=32),
                  wb_h[m].rearrange("kc p mo -> p kc mo"), sem=f"{P}_w")
            c.wait("tensor", f"{P}_w", 16 * (gm + 1))
            for n in range(4):
                if g >= 2:
                    c.wait("tensor", f"{P}_a", g - 1)
                for kc in range(32):
                    c.op("tensor", "matmul", ps[g % 2][:],
                         wblk[gm % 2][:, kc * 128:(kc + 1) * 128],
                         hAt[:, kc * 2048 + n * 512:kc * 2048 + (n + 1) * 512],
                         start=(kc == 0), stop=(kc == 31),
                         inc=(f"{P}_pm", 1) if kc == 31 else None)
                c.wait("scalar", f"{P}_pm", g + 1)
                if g >= 2:
                    c.wait("scalar", f"{P}_st", 16 * (g - 1))
                c.op("scalar", "activation", hstg[g % 2][:], ps[g % 2][:], AF.Gelu,
                     scale=ssb[:, m:m + 1])
                c.drain("scalar")
                c.sem_inc("scalar", f"{P}_a", 1)
                c.wait("sync", f"{P}_a", g + 1)
                c.dma("sync", hOut[m * 128:(m + 1) * 128,
                                   half * 2048 + n * 512:half * 2048 + (n + 1) * 512],
                      hstg[g % 2][:], sem=f"{P}_st")
                g += 1
    c.wait("sync", f"{P}_st", 16 * 256)


def ph_h1(c):
    ph_hmid(c, "p5", "h1b", "h1s_t", "hA", "hB")


def ph_h2(c):
    ph_hmid(c, "p6", "h2b", "h2s_t", "hB", "hC")



def ph_w3ln(c):
    w3h = c.dram("w3T", [FH, D], F32, kind="ExternalInput")
    s3h = c.dram("w3s_t", [128, 8], F32, kind="ExternalInput")
    hC = c.dram("hC", [FH, T], BF16)
    x1 = c.dram("x1", [D, T], F32)
    x2 = c.dram("x2", [D, T], F32)

    w3 = c.sbuf("p7_w3", [128, 32 * D], BF16)
    s3 = c.sbuf("p7_s3", [128, 8], F32)
    hCt = [c.sbuf(f"p7_hc{i}", [128, 32 * 512], BF16) for i in range(2)]
    x1f = [c.sbuf("p7_x10", [128, 8 * 512], F32)] * 2
    rsb = c.sbuf("p7_rsb", [128, 8 * 512], F32)
    lt = ln_tiles(c, "p7")
    ps_p = [c.psum(f"p7_psp{i}", [128, 512], F32) for i in range(2)]

    for kc in range(32):
        c.dma("gpsimd", w3[:, kc * D:(kc + 1) * D], w3h[kc * 128:(kc + 1) * 128, :],
              sem="p7_w")
    c.dma("sync", s3[:], s3h, sem="p7_s")
    c.wait("tensor", "p7_w", 16 * 32)
    c.wait("vector", "p7_s", 16)
    c.wait("tensor", "v_init", 1)

    for qc in range(NTC):
        if qc >= 2:
            c.wait("sync", "p7_pm", 8 * (qc - 1))
        if qc >= 1:
            c.wait("sync", "p7_vr", 8 * qc)
        hs = hC[:, qc * TC:(qc + 1) * TC].rearrange("(kc p) t -> p kc t", p=128)
        c.dma("sync", hCt[qc % 2][:].rearrange("p (kc t) -> p kc t", kc=32), hs,
              sem="p7_hx")
        xs = x1[:, qc * TC:(qc + 1) * TC].rearrange("(kc p) t -> p kc t", p=128)
        c.dma("sync", x1f[qc % 2][:].rearrange("p (kc t) -> p kc t", kc=8), xs,
              sem="p7_x1")
        c.wait("tensor", "p7_hx", 16 * (qc + 1))
        c.wait("vector", "p7_x1", 16 * (qc + 1))
        for fo in range(8):
            g = qc * 8 + fo
            if g >= 2:
                c.wait("tensor", "p7_vr", g - 1)
            for kc in range(32):
                c.op("tensor", "matmul", ps_p[fo % 2][:],
                     w3[:, kc * D + fo * 128:kc * D + (fo + 1) * 128],
                     hCt[qc % 2][:, kc * 512:(kc + 1) * 512],
                     start=(kc == 0), stop=(kc == 31),
                     inc=("p7_pm", 1) if kc == 31 else None)
            c.wait("vector", "p7_pm", g + 1)
            c.op("vector", "scalar_tensor_tensor", rsb[:, fo * 512:(fo + 1) * 512],
                 ps_p[fo % 2][:], s3[:, fo:fo + 1],
                 x1f[qc % 2][:, fo * 512:(fo + 1) * 512], Op.mult, Op.add)
            c.drain("vector")
            c.sem_inc("vector", "p7_vr", 1)
        emit_ln_tail(
            c, "p7", qc, rsb, lt,
            out_dst=lambda fo, _qc=qc: x2[:, _qc * TC:(_qc + 1) * TC]
                .rearrange("(fc p) t -> fc p t", p=128)[fo],
            rsb_ready=("p7_vr", lambda fo, _qc=qc: _qc * 8 + fo + 1),
        )
    c.wait("sync", "p7_st", 16 * 64)



def ph_lnwto(c):
    x2 = c.dram("x2", [D, T], F32)
    wtoh = c.dram("W_toT", [D, 2 * FR], F32, kind="ExternalInput")
    g1h = c.dram("ln1g_t", [128, 8], F32, kind="ExternalInput")
    b1h = c.dram("ln1b_t", [128, 8], F32, kind="ExternalInput")
    frh = c.dram("freq2", [128, 2], F32, kind="ExternalInput")
    ubuf = c.dram("ubuf", [128, 4 * T], F32)

    wto = c.sbuf("p8_wto", [128, 8 * 512], BF16)
    g1 = c.sbuf("p8_g1", [128, 8], F32)
    b1 = c.sbuf("p8_b1", [128, 8], F32)
    fr2 = c.sbuf("p8_fr", [128, 2], F32)
    om = c.sbuf("p8_om", [128, 2], F32)
    c01 = c.sbuf("p8_c01", [128, 2], F32)
    s01 = c.sbuf("p8_s01", [128, 2], F32)
    ns01 = c.sbuf("p8_ns01", [128, 2], F32)
    halfpi = c.sbuf("p8_hpi", [128, 1], F32)
    x2f = [c.sbuf(f"p8_x2{i}", [128, 8 * 512], F32) for i in range(2)]
    lnx = c.sbuf("p8_lnx", [128, 8 * T], BF16)
    tra = c.sbuf("p8_tra", [128, 512], F32)
    trb = c.sbuf("p8_trb", [128, 512], F32)
    ustg = [c.sbuf(f"p8_u{i}", [128, 4 * 512], F32) for i in range(2)]
    lt = ln_tiles(c, "p8")
    ps_w = [c.psum(f"p8_psw{i}", [128, 512], F32) for i in range(4)]

    for kc in range(8):
        c.dma("gpsimd", wto[:, kc * 512:(kc + 1) * 512],
              wtoh[kc * 128:(kc + 1) * 128, :], sem="p8_w")
    c.dma("sync", g1[:], g1h, sem="p8_g")
    c.dma("sync", b1[:], b1h, sem="p8_g")
    c.dma("sync", fr2[:], frh, sem="p8_g")
    c.wait("scalar", "p8_g", 48)
    c.wait("scalar", "v_init", 1)
    c.op("scalar", "activation", om[:], fr2[:], AF.Tanh)
    c.drain("scalar")
    c.op("scalar", "mul", om[:], om[:], 0.1)
    c.drain("scalar")
    c.op("vector", "memset", halfpi[:], 1.5707963267948966)
    c.drain("vector")
    c.sem_inc("vector", "p8_hp", 1)
    c.wait("scalar", "p8_hp", 1)
    c.op("scalar", "activation", c01[:], om[:], AF.Sin, bias=halfpi[:, 0:1])
    c.drain("scalar")
    c.op("scalar", "activation", s01[:], om[:], AF.Sin)
    c.drain("scalar")
    c.op("scalar", "mul", c01[:], c01[:], 0.1)
    c.drain("scalar")
    c.op("scalar", "mul", s01[:], s01[:], 0.1)
    c.drain("scalar")
    c.op("scalar", "mul", ns01[:], s01[:], -1.0)
    c.drain("scalar")
    c.sem_inc("scalar", "p8_cs", 1)
    c.wait("vector", "p8_cs", 1)
    c.wait("vector", "p8_g", 48)
    c.wait("tensor", "p8_w", 16 * 8)
    c.wait("tensor", "v_init", 1)

    for qc in range(NTC):
        xs = x2[:, qc * TC:(qc + 1) * TC].rearrange("(kc p) t -> p kc t", p=128)
        c.dma("sync", x2f[qc % 2][:].rearrange("p (kc t) -> p kc t", kc=8), xs,
              sem="p8_x")
        c.wait("vector", "p8_x", 16 * (qc + 1))
        c.wait("scalar", "p8_x", 16 * (qc + 1))
        c.wait("tensor", "p8_x", 16 * (qc + 1))
        emit_ln_tail(
            c, "p8l", qc, x2f[qc % 2], lt, out_dst=None,
            gslice=lambda fo: g1[:, fo:fo + 1],
            bslice=lambda fo: b1[:, fo:fo + 1],
            bf_dst=lambda fo, _qc=qc: lnx[:, fo * T + _qc * 512:fo * T + (_qc + 1) * 512],
        )
        c.drain("vector")
        c.sem_inc("vector", "p8_lnq", 1)
        if qc < NTC - 1:
            c.wait("sync", "p8l_va", (qc + 1) * 8)

    for qc in range(NTC):
        c.wait("tensor", "p8_lnq", NTC)
        for m in range(4):
            if qc >= 1:
                c.wait("tensor", "p8_rot", 4 * qc)
            for kc in range(8):
                c.op("tensor", "matmul", ps_w[m][:],
                     wto[:, kc * 512 + m * 128:kc * 512 + (m + 1) * 128],
                     lnx[:, kc * T + qc * 512:kc * T + (qc + 1) * 512],
                     start=(kc == 0), stop=(kc == 7),
                     inc=("p8_pm", 1) if kc == 7 else None)
        for h in range(2):
            c.wait("scalar", "p8_pm", 4 * (qc + 1))
            c.wait("scalar", "p8_rot", 4 * qc + 2 * h)
            if qc >= 2:
                c.wait("scalar", "p8_ust", 16 * (qc - 1))
            c.op("scalar", "mul", tra[:], ps_w[h][:], c01[:, h:h + 1])
            c.drain("scalar")
            c.op("scalar", "mul", trb[:], ps_w[h][:], s01[:, h:h + 1])
            c.drain("scalar")
            c.sem_inc("scalar", "p8_ra", 1)
            c.wait("vector", "p8_ra", 2 * qc + h + 1)
            us = ustg[qc % 2]
            c.op("vector", "scalar_tensor_tensor", us[:, h:4 * 512:4],
                 ps_w[2 + h][:], ns01[:, h:h + 1], tra[:], Op.mult, Op.add)
            c.drain("vector")
            c.op("vector", "scalar_tensor_tensor", us[:, 2 + h:4 * 512:4],
                 ps_w[2 + h][:], c01[:, h:h + 1], trb[:], Op.mult, Op.add)
            c.drain("vector")
            c.sem_inc("vector", "p8_rot", 2)
        c.wait("sync", "p8_rot", 4 * (qc + 1))
        c.dma("sync", ubuf[:, qc * 2048:(qc + 1) * 2048], ustg[qc % 2][:],
              sem="p8_ust")
    c.wait("sync", "p8_ust", 16 * NTC)



def ph_scan(c):
    ubuf = c.dram("ubuf", [128, 4 * T], F32)
    ldh = c.dram("ldec2", [128, 2], F32, kind="ExternalInput")
    frh = c.dram("freq2", [128, 2], F32, kind="ExternalInput")
    spec = c.dram("spec", [4 * 128, T], BF16)

    ld2 = c.sbuf("p9_ld", [128, 2], F32)
    fr2 = c.sbuf("p9_fr", [128, 2], F32)
    dec = c.sbuf("p9_dec", [128, 2], F32)
    om = c.sbuf("p9_om", [128, 2], F32)
    cs = c.sbuf("p9_cs", [128, 2], F32)
    sn = c.sbuf("p9_sn", [128, 2], F32)
    cd2 = c.sbuf("p9_cd2", [128, 2], F32)
    sd2 = c.sbuf("p9_sd2", [128, 2], F32)
    CD4 = c.sbuf("p9_cd4", [128, 4], F32)
    SD4 = c.sbuf("p9_sd4", [128, 4], F32)
    halfpi = c.sbuf("p9_hpi", [128, 1], F32)
    ut = [c.sbuf(f"p9_u{i}", [128, 4 * 512], F32) for i in range(2)]
    hist = c.sbuf("p9_hist", [128, 4 * (T + 1)], F32)
    tmpa = c.sbuf("p9_ta", [128, 4], F32)
    tmpb = c.sbuf("p9_tb", [128, 4], F32)
    sstg = [c.sbuf(f"p9_ss{i}", [128, 4 * 512], BF16) for i in range(2)]

    c.dma("sync", ld2[:], ldh, sem="p9_g")
    c.dma("sync", fr2[:], frh, sem="p9_g")
    c.wait("scalar", "p9_g", 32)
    c.wait("scalar", "v_init", 1)
    c.op("scalar", "activation", dec[:], ld2[:], AF.Sigmoid)
    c.drain("scalar")
    c.op("scalar", "activation", om[:], fr2[:], AF.Tanh)
    c.drain("scalar")
    c.op("scalar", "mul", om[:], om[:], 0.1)
    c.drain("scalar")
    c.sem_inc("scalar", "p9_a1", 1)
    c.wait("vector", "p9_a1", 1)
    c.op("vector", "memset", halfpi[:], 1.5707963267948966)
    c.drain("vector")
    c.sem_inc("vector", "p9_v1", 1)
    c.wait("scalar", "p9_v1", 1)
    c.op("scalar", "activation", cs[:], om[:], AF.Sin, bias=halfpi[:, 0:1])
    c.drain("scalar")
    c.op("scalar", "activation", sn[:], om[:], AF.Sin)
    c.drain("scalar")
    c.sem_inc("scalar", "p9_a2", 1)
    c.wait("vector", "p9_a2", 1)
    c.op("vector", "tensor_mul", cd2[:], cs[:], dec[:])
    c.drain("vector")
    c.op("vector", "tensor_mul", sd2[:], sn[:], dec[:])
    c.drain("vector")
    c.op("vector", "tensor_copy", CD4[:, 0:2], cd2[:])
    c.op("vector", "tensor_copy", CD4[:, 2:4], cd2[:])
    c.op("vector", "tensor_copy", SD4[:, 2:4], sd2[:])
    c.drain("vector")
    c.op("vector", "tensor_scalar_mul", SD4[:, 0:2], sd2[:], -1.0)
    c.drain("vector")
    c.op("vector", "memset", hist[:, 0:4], 0.0)
    c.drain("vector")
    c.sem_inc("vector", "p9_k", 1)
    c.wait("gpsimd", "p9_k", 1)

    for qc in range(NTC):
        if qc >= 2:
            c.wait("sync", "p9_gc", qc - 1)
        c.dma("sync", ut[qc % 2][:], ubuf[:, qc * 2048:(qc + 1) * 2048], sem="p9_u")
        c.wait("gpsimd", "p9_u", 16 * (qc + 1))
        for tl in range(512):
            t = qc * 512 + tl
            prev = hist[:, 4 * t:4 * t + 4]
            prev_sw = prev.rearrange("p (b a) -> p b a", b=2)[:, ::-1, :]
            cur = hist[:, 4 * (t + 1):4 * (t + 1) + 4]
            c.op("gpsimd", "tensor_tensor",
                 tmpb[:].rearrange("p (b a) -> p b a", b=2), prev_sw,
                 SD4[:].rearrange("p (b a) -> p b a", b=2), Op.mult)
            c.op("gpsimd", "tensor_tensor", tmpa[:], prev, CD4[:], Op.mult)
            c.op("gpsimd", "tensor_add", tmpa[:], tmpa[:], tmpb[:])
            c.op("gpsimd", "tensor_add", tmpa[:], tmpa[:], ut[qc % 2][:, 4 * tl:4 * tl + 4])
            c.op("gpsimd", "tensor_scalar", cur, tmpa[:], 10.0, -10.0, Op.min, Op.max)
        c.drain("gpsimd")
        c.sem_inc("gpsimd", "p9_gc", 1)
        c.wait("vector", "p9_gc", qc + 1)
        if qc >= 2:
            c.wait("vector", "p9_st", 16 * (qc - 1))
        for j in range(4):
            hs = hist[:, 4 * (qc * 512 + 1) + j:4 * (qc * 512 + 512 + 1):4]
            c.op("vector", "tensor_copy", sstg[qc % 2][:, j * 512:(j + 1) * 512], hs)
        c.drain("vector")
        c.sem_inc("vector", "p9_cv", 1)
        c.wait("sync", "p9_cv", qc + 1)
        c.dma("sync",
              spec[:, qc * 512:(qc + 1) * 512].rearrange("(m p) t -> p m t", p=128),
              sstg[qc % 2][:].rearrange("p (m t) -> p m t", m=4), sem="p9_st")
    c.wait("sync", "p9_st", 16 * NTC)



def ph_wfrom_ffn1(c):
    spec = c.dram("spec", [4 * 128, T], BF16)
    x2 = c.dram("x2", [D, T], F32)
    wfh = c.dram("W_fromT", [2 * FR, D], F32, kind="ExternalInput")
    g3h = c.dram("ln3g_t", [128, 8], F32, kind="ExternalInput")
    b3h = c.dram("ln3b_t", [128, 8], F32, kind="ExternalInput")
    f1h = c.dram("ffn1T", [D, FH], F32, kind="ExternalInput")
    bf1h = c.dram("bffn1_t", [128, 32], F32, kind="ExternalInput")
    r3d = c.dram("r3", [D, T], F32)
    gh = c.dram("gh", [FH, T], BF16)

    wfrom = c.sbuf("pa_wf", [128, 4 * D], BF16)
    f1 = c.sbuf("pa_f1", [128, 8 * FH], BF16)
    g3 = c.sbuf("pa_g3", [128, 8], F32)
    b3 = c.sbuf("pa_b3", [128, 8], F32)
    bf1 = c.sbuf("pa_bf1", [128, 32], F32)
    spc = [c.sbuf(f"pa_sp{i}", [128, 4 * 512], BF16) for i in range(2)]
    x2f = [c.sbuf("pa_x20", [128, 8 * 512], F32)] * 2
    rsb = c.sbuf("pa_rsb", [128, 8 * 512], F32)
    x3b = c.sbuf("pa_x3b", [128, 8 * 512], BF16)
    gstg = [c.sbuf(f"pa_g{i}", [128, 512], BF16) for i in range(2)]
    lt = ln_tiles(c, "pa")
    ps_p = [c.psum(f"pa_psp{i}", [128, 512], F32) for i in range(2)]
    ps_f = [c.psum(f"pa_psf{i}", [128, 512], F32) for i in range(2)]

    for kc in range(4):
        c.dma("gpsimd", wfrom[:, kc * D:(kc + 1) * D], wfh[kc * 128:(kc + 1) * 128, :],
              sem="pa_w")
    for kc in range(8):
        c.dma("gpsimd", f1[:, kc * FH:(kc + 1) * FH], f1h[kc * 128:(kc + 1) * 128, :],
              sem="pa_w")
    c.dma("sync", g3[:], g3h, sem="pa_g")
    c.dma("sync", b3[:], b3h, sem="pa_g")
    c.dma("sync", bf1[:], bf1h, sem="pa_g")
    c.wait("tensor", "pa_w", 16 * 12)
    c.wait("vector", "pa_g", 48)
    c.wait("scalar", "pa_g", 48)
    c.wait("tensor", "v_init", 1)

    for qc in range(NTC):
        if qc >= 2:
            c.wait("sync", "pa_pm", 8 * (qc - 1))
        if qc >= 1:
            c.wait("sync", "pa_vr", 8 * qc)
        ss = spec[:, qc * TC:(qc + 1) * TC].rearrange("(kc p) t -> p kc t", p=128)
        c.dma("sync", spc[qc % 2][:].rearrange("p (kc t) -> p kc t", kc=4), ss,
              sem="pa_sx")
        xs = x2[:, qc * TC:(qc + 1) * TC].rearrange("(kc p) t -> p kc t", p=128)
        c.dma("sync", x2f[qc % 2][:].rearrange("p (kc t) -> p kc t", kc=8), xs,
              sem="pa_xx")
        c.wait("tensor", "pa_sx", 16 * (qc + 1))
        c.wait("vector", "pa_xx", 16 * (qc + 1))
        for fo in range(8):
            g = qc * 8 + fo
            if g >= 2:
                c.wait("tensor", "pa_vr", g - 1)
            for kc in range(4):
                c.op("tensor", "matmul", ps_p[fo % 2][:],
                     wfrom[:, kc * D + fo * 128:kc * D + (fo + 1) * 128],
                     spc[qc % 2][:, kc * 512:(kc + 1) * 512],
                     start=(kc == 0), stop=(kc == 3),
                     inc=("pa_pm", 1) if kc == 3 else None)
            c.wait("vector", "pa_pm", g + 1)
            if g >= 2:
                c.wait("vector", "pa_r3st", 16 * (g - 1))
            c.op("vector", "scalar_tensor_tensor", rsb[:, fo * 512:(fo + 1) * 512],
                 ps_p[fo % 2][:], 0.1,
                 x2f[qc % 2][:, fo * 512:(fo + 1) * 512], Op.mult, Op.add)
            c.drain("vector")
            c.sem_inc("vector", "pa_vr", 1)
            c.wait("sync", "pa_vr", g + 1)
            c.dma("sync", r3d[:, qc * TC:(qc + 1) * TC]
                  .rearrange("(fc p) t -> fc p t", p=128)[fo],
                  rsb[:, fo * 512:(fo + 1) * 512], sem="pa_r3st")
        emit_ln_tail(
            c, "pa", qc, rsb, lt, out_dst=None,
            gslice=lambda fo: g3[:, fo:fo + 1],
            bslice=lambda fo: b3[:, fo:fo + 1],
            bf_dst=lambda fo: x3b[:, fo * 512:(fo + 1) * 512],
            rsb_ready=("pa_vr", lambda fo, _qc=qc: _qc * 8 + fo + 1),
        )
        for m in range(32):
            gg = qc * 32 + m
            if gg >= 2:
                c.wait("tensor", "pa_a", gg - 1)
            c.wait("tensor", "pa_va", (qc + 1) * 8)
            for kc in range(8):
                c.op("tensor", "matmul", ps_f[m % 2][:],
                     f1[:, kc * FH + m * 128:kc * FH + (m + 1) * 128],
                     x3b[:, kc * 512:(kc + 1) * 512],
                     start=(kc == 0), stop=(kc == 7),
                     inc=("pa_fm", 1) if kc == 7 else None)
            c.wait("scalar", "pa_fm", gg + 1)
            if gg >= 2:
                c.wait("scalar", "pa_gst", 16 * (gg - 1))
            c.op("scalar", "activation", gstg[gg % 2][:], ps_f[m % 2][:], AF.Gelu,
                 bias=bf1[:, m:m + 1], scale=1.0)
            c.drain("scalar")
            c.sem_inc("scalar", "pa_a", 1)
            c.wait("sync", "pa_a", gg + 1)
            c.dma("sync", gh[m * 128:(m + 1) * 128, qc * TC:(qc + 1) * TC],
                  gstg[gg % 2][:], sem="pa_gst")
        c.wait("vector", "pa_fm", (qc + 1) * 32)
    c.wait("sync", "pa_gst", 16 * 256)
    c.wait("sync", "pa_r3st", 16 * 64)



def ph_ffn2ln(c):
    gh = c.dram("gh", [FH, T], BF16)
    x2 = c.dram("x2", [D, T], F32)
    r3d = c.dram("r3", [D, T], F32)
    f2h = c.dram("ffn2T", [FH, D], F32, kind="ExternalInput")
    bf2h = c.dram("bffn2_t", [128, 8], F32, kind="ExternalInput")
    x4 = c.dram("x4", [D, T], F32)

    f2 = c.sbuf("pb_f2", [128, 32 * D], BF16)
    bf2 = c.sbuf("pb_bf2", [128, 8], F32)
    b2s = c.sbuf("pb_b2s", [128, 8], F32)
    ght = [c.sbuf(f"pb_gh{i}", [128, 32 * 512], BF16) for i in range(2)]
    x2f = [c.sbuf("pb_x20", [128, 8 * 512], F32)] * 2
    r3f = [c.sbuf("pb_r30", [128, 8 * 512], F32)] * 2
    rsb = c.sbuf("pb_rsb", [128, 8 * 512], F32)
    tps = c.sbuf("pb_tps", [128, 512], F32)
    lt = ln_tiles(c, "pb")
    ps_p = [c.psum(f"pb_psp{i}", [128, 512], F32) for i in range(2)]

    for kc in range(32):
        c.dma("gpsimd", f2[:, kc * D:(kc + 1) * D], f2h[kc * 128:(kc + 1) * 128, :],
              sem="pb_w")
    c.dma("sync", bf2[:], bf2h, sem="pb_g")
    c.wait("scalar", "pb_g", 16)
    c.wait("scalar", "v_init", 1)
    c.op("scalar", "mul", b2s[:], bf2[:], 0.1)
    c.drain("scalar")
    c.sem_inc("scalar", "pb_b", 1)
    c.wait("tensor", "pb_w", 16 * 32)
    c.wait("vector", "pb_b", 1)
    c.wait("tensor", "v_init", 1)

    for qc in range(NTC):
        if qc >= 2:
            c.wait("sync", "pb_pm", 8 * (qc - 1))
        if qc >= 1:
            c.wait("sync", "pb_vr", 8 * qc)
        gs = gh[:, qc * TC:(qc + 1) * TC].rearrange("(kc p) t -> p kc t", p=128)
        c.dma("sync", ght[qc % 2][:].rearrange("p (kc t) -> p kc t", kc=32), gs,
              sem="pb_gx")
        xs = x2[:, qc * TC:(qc + 1) * TC].rearrange("(kc p) t -> p kc t", p=128)
        c.dma("sync", x2f[qc % 2][:].rearrange("p (kc t) -> p kc t", kc=8), xs,
              sem="pb_xx")
        rs = r3d[:, qc * TC:(qc + 1) * TC].rearrange("(kc p) t -> p kc t", p=128)
        c.dma("sync", r3f[qc % 2][:].rearrange("p (kc t) -> p kc t", kc=8), rs,
              sem="pb_rx")
        c.wait("tensor", "pb_gx", 16 * (qc + 1))
        c.wait("vector", "pb_xx", 16 * (qc + 1))
        c.wait("vector", "pb_rx", 16 * (qc + 1))
        for fo in range(8):
            g = qc * 8 + fo
            if g >= 2:
                c.wait("tensor", "pb_vr", g - 1)
            for kc in range(32):
                c.op("tensor", "matmul", ps_p[fo % 2][:],
                     f2[:, kc * D + fo * 128:kc * D + (fo + 1) * 128],
                     ght[qc % 2][:, kc * 512:(kc + 1) * 512],
                     start=(kc == 0), stop=(kc == 31),
                     inc=("pb_pm", 1) if kc == 31 else None)
            c.wait("vector", "pb_pm", g + 1)
            c.op("vector", "scalar_tensor_tensor", tps[:],
                 ps_p[fo % 2][:], 0.1,
                 r3f[qc % 2][:, fo * 512:(fo + 1) * 512], Op.mult, Op.add)
            c.drain("vector")
            c.op("vector", "scalar_tensor_tensor", rsb[:, fo * 512:(fo + 1) * 512],
                 tps[:], b2s[:, fo:fo + 1],
                 x2f[qc % 2][:, fo * 512:(fo + 1) * 512], Op.add, Op.add)
            c.drain("vector")
            c.sem_inc("vector", "pb_vr", 1)
        emit_ln_tail(
            c, "pb", qc, rsb, lt,
            out_dst=lambda fo, _qc=qc: x4[:, _qc * TC:(_qc + 1) * TC]
                .rearrange("(fc p) t -> fc p t", p=128)[fo],
            rsb_ready=("pb_vr", lambda fo, _qc=qc: _qc * 8 + fo + 1),
        )
    c.wait("sync", "pb_st", 16 * 64)



def ph_l1ffn(c, upto="full"):
    x4 = c.dram("x4", [D, T], F32)
    kh = c.dram("L1_ffn_kernel", [D, D], F32, kind="ExternalInput")
    ah = c.dram("L1_ffn_a", [D, 16], F32, kind="ExternalInput")
    bh = c.dram("L1_ffn_b", [16, D], F32, kind="ExternalInput")
    bih = c.dram("bias_t", [128, 8], F32, kind="ExternalInput")
    x5 = c.dram("x5", [D, T], F32)

    kf = c.sbuf("pc_kf", [128, 8 * D], F32)
    wq = c.sbuf("pc_wq", [128, 8 * D], BF16)
    asb = c.sbuf("pc_a", [128, 8 * 16], BF16)
    bsb = c.sbuf("pc_b", [16, D], BF16)
    bi = c.sbuf("pc_bi", [128, 8], F32)
    amax = c.sbuf("pc_amax", [1, 1], F32)
    srow = c.sbuf("pc_srow", [1, 1], F32)
    rrow = c.sbuf("pc_rrow", [1, 1], F32)
    sc128 = c.sbuf("pc_sc", [128, 1], F32)
    rc128 = c.sbuf("pc_rc", [128, 1], F32)
    qt = c.sbuf("pc_qt", [128, D], F32)
    x4b = [c.sbuf(f"pc_x4b{i}", [128, 8 * 512], BF16) for i in range(2)]
    x4f = [c.sbuf(f"pc_x4f{i}", [128, 8 * 512], F32) for i in range(2)]
    t1 = c.sbuf("pc_t1", [16, 512], BF16)
    rsb = c.sbuf("pc_rsb", [128, 8 * 512], F32)
    lt = ln_tiles(c, "pc")
    ps_p = [c.psum(f"pc_psp{i}", [128, 512], F32) for i in range(2)]
    ps_l = c.psum("pc_psl", [128, 512], F32)
    onesr = None

    for kc in range(8):
        c.dma("sync", kf[:, kc * D:(kc + 1) * D], kh[kc * 128:(kc + 1) * 128, :],
              sem="pc_k")
    c.dma("gpsimd", asb[:].rearrange("p (kc r) -> p kc r", kc=8),
          ah.rearrange("(kc p) r -> p kc r", p=128), sem="pc_k")
    c.dma("gpsimd", bsb[:], bh, sem="pc_k")
    c.dma("sync", bi[:], bih, sem="pc_k")
    c.wait("gpsimd", "pc_k", 16 * 11)
    c.op("gpsimd", "tensor_reduce", amax[:], kf[:], AX.XYZWC, Op.abs_max)
    c.drain("gpsimd")
    c.sem_inc("gpsimd", "pc_am", 1)
    c.wait("scalar", "pc_am", 1)
    c.op("scalar", "mul", srow[:], amax[:], 1.0 / 7.0)
    c.drain("scalar")
    c.sem_inc("scalar", "pc_s", 1)
    c.wait("vector", "pc_s", 1)
    c.op("vector", "reciprocal", rrow[:], srow[:])
    c.drain("vector")
    c.sem_inc("vector", "pc_r", 1)
    c.wait("tensor", "v_init", 1)
    c.wait("tensor", "pc_r", 1)
    onesr = c.tiles["c_onesr"]
    c.op("tensor", "matmul", ps_l[:, 0:1], onesr[:], srow[:], start=True, stop=True)
    c.op("tensor", "matmul", ps_l[:, 1:2], onesr[:], rrow[:], start=True, stop=True,
         inc=("pc_pb", 1))
    c.wait("vector", "pc_pb", 1)
    c.op("vector", "tensor_copy", sc128[:], ps_l[:, 0:1])
    c.op("vector", "tensor_copy", rc128[:], ps_l[:, 1:2])
    c.drain("vector")
    for kc in range(8):
        c.op("vector", "tensor_scalar", qt[:], kf[:, kc * D:(kc + 1) * D],
             rc128[:, 0:1], None, Op.mult)
        c.drain("vector")
        c.op("vector", "tensor_scalar", qt[:], qt[:], 12582912.0, None, Op.add)
        c.drain("vector")
        c.op("vector", "tensor_scalar", qt[:], qt[:], 12582912.0, None, Op.subtract)
        c.drain("vector")
        c.op("vector", "tensor_scalar", qt[:], qt[:], 7.0, -8.0, Op.min, Op.max)
        c.drain("vector")
        c.op("vector", "tensor_scalar", wq[:, kc * D:(kc + 1) * D], qt[:],
             sc128[:, 0:1], None, Op.mult)
        c.drain("vector")
    c.sem_inc("vector", "pc_q", 1)
    c.wait("tensor", "pc_q", 1)
    if upto == "quant":
        wqd = c.dram("wq_dbg", [D, D], BF16, kind="ExternalOutput")
        dbg = c.dram("dbg", [128, 8], F32, kind="ExternalOutput")
        c.wait("sync", "pc_q", 1)
        for kc in range(8):
            c.dma("sync", wqd[kc * 128:(kc + 1) * 128, :], wq[:, kc * D:(kc + 1) * D],
                  sem="pc_dbg")
        dbt = c.sbuf("pc_dbt", [128, 8], F32)
        c.op("vector", "memset", dbt[:], 0.0)
        c.drain("vector")
        c.op("vector", "tensor_copy", dbt[0:1, 0:1], amax[:])
        c.op("vector", "tensor_copy", dbt[0:1, 1:2], srow[:])
        c.op("vector", "tensor_copy", dbt[0:1, 2:3], rrow[:])
        c.op("vector", "tensor_copy", dbt[:, 3:4], sc128[:])
        c.op("vector", "tensor_copy", dbt[:, 4:5], rc128[:])
        c.drain("vector")
        c.sem_inc("vector", "pc_dv", 1)
        c.wait("sync", "pc_dv", 1)
        c.dma("sync", dbg, dbt[:], sem="pc_dbg")
        c.wait("sync", "pc_dbg", 16 * 9)
        return

    for qc in range(NTC):
        if qc >= 2:
            c.wait("gpsimd", "pc_pm", 8 * (qc - 1))
            c.wait("sync", "pc_pm", 8 * (qc - 1))
        xs = x4[:, qc * TC:(qc + 1) * TC].rearrange("(kc p) t -> p kc t", p=128)
        c.dma("gpsimd", x4b[qc % 2][:].rearrange("p (kc t) -> p kc t", kc=8), xs,
              sem="pc_xb")
        c.dma("sync", x4f[qc % 2][:].rearrange("p (kc t) -> p kc t", kc=8), xs,
              sem="pc_xf")
        c.wait("tensor", "pc_xb", 16 * (qc + 1))
        c.wait("vector", "pc_xf", 16 * (qc + 1))
        if qc >= 1:
            c.wait("tensor", "pc_l1v", qc)
        for kc in range(8):
            c.op("tensor", "matmul", ps_l[0:16, 0:512],
                 asb[:, kc * 16:(kc + 1) * 16],
                 x4b[qc % 2][:, kc * 512:(kc + 1) * 512],
                 start=(kc == 0), stop=(kc == 7),
                 inc=("pc_l1", 1) if kc == 7 else None)
        c.wait("vector", "pc_l1", qc + 1)
        if qc >= 1:
            c.wait("vector", "pc_pm", 8 * qc)
        c.op("vector", "tensor_copy", t1[:], ps_l[0:16, 0:512])
        c.drain("vector")
        c.sem_inc("vector", "pc_l1v", 1)
        c.wait("tensor", "pc_l1v", qc + 1)
        if upto == "lora1":
            continue
        for fo in range(8):
            g = qc * 8 + fo
            if g >= 2:
                c.wait("tensor", "pc_vr", g - 1)
            for kc in range(8):
                c.op("tensor", "matmul", ps_p[fo % 2][:],
                     wq[:, kc * D + fo * 128:kc * D + (fo + 1) * 128],
                     x4b[qc % 2][:, kc * 512:(kc + 1) * 512],
                     start=(kc == 0), stop=False)
            c.op("tensor", "matmul", ps_p[fo % 2][:],
                 bsb[:, fo * 128:(fo + 1) * 128], t1[:],
                 start=False, stop=True, inc=("pc_pm", 1))
            c.wait("vector", "pc_pm", g + 1)
            c.op("vector", "scalar_tensor_tensor", rsb[:, fo * 512:(fo + 1) * 512],
                 ps_p[fo % 2][:], bi[:, fo:fo + 1],
                 x4f[qc % 2][:, fo * 512:(fo + 1) * 512], Op.add, Op.add)
            c.drain("vector")
            c.sem_inc("vector", "pc_vr", 1)
        emit_ln_tail(
            c, "pc", qc, rsb, lt,
            out_dst=lambda fo, _qc=qc: x5[:, _qc * TC:(_qc + 1) * TC]
                .rearrange("(fc p) t -> fc p t", p=128)[fo],
            rsb_ready=("pc_vr", lambda fo, _qc=qc: _qc * 8 + fo + 1),
        )
    if upto == "full":
        c.wait("sync", "pc_st", 16 * 64)



def ph_outT(c):
    x5 = c.dram("x5", [D, T], F32)
    out = c.dram("y_out", [T, D], F32, kind="ExternalOutput")

    idn = c.tiles["c_idnf"]
    x5f = [c.sbuf(f"pd_x5{i}", [128, 8 * 512], F32) for i in range(2)]
    ostg = [c.sbuf(f"pd_o{i}", [128, D], F32) for i in range(2)]
    ps_t = [c.psum(f"pd_ps{i}", [128, 128], F32) for i in range(2)]

    c.wait("tensor", "v_init", 1)
    j = 0
    gi = 0
    for qc in range(NTC):
        if qc >= 2:
            c.wait("sync", "pd_tr", 32 * (qc - 1))
        xs = x5[:, qc * TC:(qc + 1) * TC].rearrange("(kc p) t -> p kc t", p=128)
        c.dma("sync", x5f[qc % 2][:].rearrange("p (kc t) -> p kc t", kc=8), xs,
              sem="pd_x")
        c.wait("tensor", "pd_x", 16 * (qc + 1))
        for i in range(4):
            for fc in range(8):
                if j >= 2:
                    c.wait("tensor", "pd_v", j - 1)
                c.op("tensor", "transpose", ps_t[j % 2][:],
                     x5f[qc % 2][:, fc * 512 + i * 128:fc * 512 + (i + 1) * 128],
                     idn[:], inc=("pd_tr", 1))
                c.wait("vector", "pd_tr", j + 1)
                if gi >= 2 and fc == 0:
                    c.wait("vector", "pd_st", 16 * (gi - 1))
                c.op("vector", "tensor_copy",
                     ostg[gi % 2][:, fc * 128:(fc + 1) * 128], ps_t[j % 2][:])
                c.drain("vector")
                c.sem_inc("vector", "pd_v", 1)
                j += 1
            c.wait("sync", "pd_v", j)
            c.dma("sync", out[qc * TC + i * 128:qc * TC + (i + 1) * 128, :],
                  ostg[gi % 2][:], sem="pd_st")
            gi += 1
    c.wait("sync", "pd_st", 16 * 32)



def build_program(phase_fns, kinds=None):
    nc = bass.Bass("TRN2", target_bir_lowering=False)
    es = contextlib.ExitStack()
    c = Ctx(nc, es, kinds)
    setup_common(c)
    for fn in phase_fns:
        fn(c)
    block = es.enter_context(nc.Block())
    finalize(c, block)
    es.close()
    return nc


def build_fused(nphases=None, out_kinds=None):
    nc = bass.Bass("TRN2", target_bir_lowering=False)
    es = contextlib.ExitStack()
    kinds = dict(out_kinds or {"y_out": "ExternalOutput"})
    c = Ctx(nc, es, kinds, fused=True)
    setup_common(c)
    for name, fn, ins, outs in PHASES[:nphases]:
        fn(c)
        c.end_phase()
    block = es.enter_context(nc.Block())
    finalize(c, block)
    es.close()
    return nc


def host_layouts(inputs):
    f32 = np.float32
    g = {k: np.asarray(v, f32) for k, v in inputs.items() if k != "L0_attn_x"}
    out = {}
    out["L0_attn_w_qkv"] = np.ascontiguousarray(g["L0_attn_w_qkv"])
    out["b_qkv_t"] = np.ascontiguousarray(g["L0_attn_b_qkv"].reshape(24, 128).T)
    out["b_qkv_r"] = np.ascontiguousarray(g["L0_attn_b_qkv"].reshape(1, 3 * D))
    out["L0_attn_w_proj"] = np.ascontiguousarray(g["L0_attn_w_proj"])
    out["b_proj_t"] = np.ascontiguousarray(g["L0_attn_b_proj"].reshape(8, 128).T)
    out["w1T"] = np.ascontiguousarray(g["L0_ffn_w1_w"].T)
    out["w2T"] = np.ascontiguousarray(g["L0_ffn_w2_w"].T)
    out["w1s_t"] = np.ascontiguousarray(g["L0_ffn_w1_s"].reshape(32, 128).T)
    out["w2s_t"] = np.ascontiguousarray(g["L0_ffn_w2_s"].reshape(32, 128).T)
    out["h1b"] = np.ascontiguousarray(
        g["L0_ffn_h1_w"].T.reshape(32, 128, 32, 128).transpose(2, 0, 1, 3))
    out["h2b"] = np.ascontiguousarray(
        g["L0_ffn_h2_w"].T.reshape(32, 128, 32, 128).transpose(2, 0, 1, 3))
    out["h1s_t"] = np.ascontiguousarray(g["L0_ffn_h1_s"].reshape(32, 128).T)
    out["h2s_t"] = np.ascontiguousarray(g["L0_ffn_h2_s"].reshape(32, 128).T)
    out["w3T"] = np.ascontiguousarray(g["L0_ffn_w3_w"].T)
    out["w3s_t"] = np.ascontiguousarray(g["L0_ffn_w3_s"].reshape(8, 128).T)
    out["W_toT"] = np.ascontiguousarray(g["L1_attn_W_to"].T)
    out["W_fromT"] = np.ascontiguousarray(g["L1_attn_W_from"].T)
    out["freq2"] = np.ascontiguousarray(g["L1_attn_frequencies"].reshape(2, 128).T)
    out["ldec2"] = np.ascontiguousarray(g["L1_attn_log_decay"].reshape(2, 128).T)
    out["ln1g_t"] = np.ascontiguousarray(g["L1_attn_ln1_g"].reshape(8, 128).T)
    out["ln1b_t"] = np.ascontiguousarray(g["L1_attn_ln1_b"].reshape(8, 128).T)
    out["ln3g_t"] = np.ascontiguousarray(g["L1_attn_ln3_g"].reshape(8, 128).T)
    out["ln3b_t"] = np.ascontiguousarray(g["L1_attn_ln3_b"].reshape(8, 128).T)
    out["ffn1T"] = np.ascontiguousarray(g["L1_attn_W_ffn1"].T)
    out["bffn1_t"] = np.ascontiguousarray(g["L1_attn_b_ffn1"].reshape(32, 128).T)
    out["ffn2T"] = np.ascontiguousarray(g["L1_attn_W_ffn2"].T)
    out["bffn2_t"] = np.ascontiguousarray(g["L1_attn_b_ffn2"].reshape(8, 128).T)
    out["L1_ffn_kernel"] = np.ascontiguousarray(g["L1_ffn_kernel"])
    out["L1_ffn_a"] = np.ascontiguousarray(g["L1_ffn_a"])
    out["L1_ffn_b"] = np.ascontiguousarray(g["L1_ffn_b"])
    out["bias_t"] = np.ascontiguousarray(g["L1_ffn_bias"].reshape(8, 128).T)
    return out


PHASES = [
    ("qkv", ph_qkv, [], ["Qf", "Kf", "Vt"]),
    ("attn", ph_attn, ["Qf", "Kf", "Vt"], ["attnf"]),
    ("proj", ph_proj, ["attnf"], ["x1"]),
    ("ffnA", ph_ffnA, ["x1"], ["hA"]),
    ("h1", ph_h1, ["hA"], ["hB"]),
    ("h2", ph_h2, ["hB"], ["hC"]),
    ("w3ln", ph_w3ln, ["hC", "x1"], ["x2"]),
    ("lnwto", ph_lnwto, ["x2"], ["ubuf"]),
    ("scan", ph_scan, ["ubuf"], ["spec"]),
    ("wfrom", ph_wfrom_ffn1, ["spec", "x2"], ["r3", "gh"]),
    ("ffn2", ph_ffn2ln, ["gh", "x2", "r3"], ["x4"]),
    ("l1ffn", ph_l1ffn, ["x4"], ["x5"]),
    ("outT", ph_outT, ["x5"], ["y_out"]),
]

PHASE_EXT = {
    "qkv": ["x_in", "L0_attn_w_qkv", "b_qkv_t", "b_qkv_r"],
    "attn": [],
    "proj": ["x_in", "L0_attn_w_proj", "b_proj_t"],
    "ffnA": ["w1T", "w2T", "w1s_t", "w2s_t"],
    "h1": ["h1b", "h1s_t"],
    "h2": ["h2b", "h2s_t"],
    "w3ln": ["w3T", "w3s_t"],
    "lnwto": ["W_toT", "ln1g_t", "ln1b_t", "freq2"],
    "scan": ["ldec2", "freq2"],
    "wfrom": ["W_fromT", "ln3g_t", "ln3b_t", "ffn1T", "bffn1_t"],
    "ffn2": ["ffn2T", "bffn2_t"],
    "l1ffn": ["L1_ffn_kernel", "L1_ffn_a", "L1_ffn_b", "bias_t"],
    "outT": [],
}


FUSED = True


def kernel(**inputs):
    xfull = np.ascontiguousarray(np.asarray(inputs["L0_attn_x"], np.float32))
    hl = host_layouts(inputs)
    xs = [np.ascontiguousarray(xfull[b]) for b in range(NCORES)]

    if FUSED:
        nc = build_fused()
        ext = sorted({k for v in PHASE_EXT.values() for k in v})
        in_maps = []
        for b in range(NCORES):
            m = {k: hl[k] for k in ext if k != "x_in"}
            m["x_in"] = xs[b]
            in_maps.append(m)
        res = run_bass_kernel_spmd(nc, in_maps, core_ids=list(range(NCORES)))
        out = np.stack([res.results[b]["y_out"] for b in range(NCORES)], axis=0)
        return out.astype(np.float32)

    state = [dict() for _ in range(NCORES)]
    for b in range(NCORES):
        state[b]["x_in"] = xs[b]

    for name, fn, ins, outs in PHASES:
        kinds = {k: "ExternalInput" for k in ins}
        kinds.update({k: "ExternalOutput" for k in outs})
        nc = build_program([fn], kinds)
        in_maps = []
        for b in range(NCORES):
            m = {}
            for k in PHASE_EXT[name]:
                m[k] = state[b][k] if k in state[b] else hl[k]
            for k in ins:
                m[k] = state[b][k]
            in_maps.append(m)
        res = run_bass_kernel_spmd(nc, in_maps, core_ids=list(range(NCORES)))
        for b in range(NCORES):
            for k in outs:
                state[b][k] = res.results[b][k]

    out = np.stack([state[b]["y_out"] for b in range(NCORES)], axis=0)
    return out.astype(np.float32)
